# version 05
# baseline (speedup 1.0000x reference)
import contextlib
import numpy as np
import ml_dtypes
import concourse.bass as bass
import concourse.mybir as mybir
from concourse.bass_utils import run_bass_kernel_spmd
from concourse.ap import AP

F32 = mybir.dt.float32
BF16 = mybir.dt.bfloat16
AF = mybir.ActivationFunctionType
ALU = mybir.AluOpType
AX = mybir.AxisListType

T = 4096
D = 1024
NT = 32
NCH = 8
ENGINES = ("sync", "scalar", "vector", "tensor", "gpsimd")


def bfc(x):
    return np.ascontiguousarray(x).astype(ml_dtypes.bfloat16)


class Bld:

    def __init__(self, nc):
        self.nc = nc
        self.stack = contextlib.ExitStack()
        self.bar = self.sem("bar")
        self.dma_sem = {e: self.sem("dsem_" + e) for e in ENGINES}
        self.dma_cnt = {e: 0 for e in ENGINES}
        self.phases = []
        self.open_bufs = {}
        self._new()
        self.dum = {e: self.sbuf("_dum_" + e, [1, 2], F32)
                    for e in ("vector", "scalar", "gpsimd")}

    def sem(self, name):
        return self.stack.enter_context(self.nc.semaphore(name))

    _namectr = [0]

    def sbuf(self, name, shape, dtype):
        assert name not in self.open_bufs, name
        Bld._namectr[0] += 1
        cm = self.nc.sbuf_tensor(f"{name}_N{Bld._namectr[0]}", list(shape), dtype)
        t = cm.__enter__()
        self.open_bufs[name] = cm
        return t

    def free(self, name):
        cm = self.open_bufs.pop(name)
        cm.__exit__(None, None, None)

    def psum(self, name, shape, dtype):
        cm = self.nc.psum_tensor(name, list(shape), dtype)
        t = cm.__enter__()
        self.open_bufs[name] = cm
        return t

    def _new(self):
        self.cur = {e: [] for e in ENGINES}
        self.cur["_dmawm"] = {}
        self.phases.append(self.cur)

    def bar_phase(self):
        self._new()

    def on(self, eng, fn):
        self.cur[eng].append(fn)

    def dma(self, eng, out, in_, **kw):
        semh = self.dma_sem[eng]
        self.dma_cnt[eng] += 16
        tgt = self.dma_cnt[eng]

        def f(e):
            e.dma_start(out=out, in_=in_, **kw).then_inc(semh, 16)
        self.on(eng, f)
        self.cur["_dmawm"][eng] = tgt

    def dmat(self, eng, out, in_):
        semh = self.dma_sem[eng]
        self.dma_cnt[eng] += 16
        tgt = self.dma_cnt[eng]

        def f(e):
            e.dma_start_transpose(out, in_).then_inc(semh, 16)
        self.on(eng, f)
        self.cur["_dmawm"][eng] = tgt

    def finalize(self):
        dum = self.dum
        block = self.stack.enter_context(self.nc.Block())
        bar = self.bar
        dma_sem = self.dma_sem
        phases = self.phases

        def mk(eng_name):
            def body(eng):
                for pi, ph in enumerate(phases):
                    last = None
                    for fn in ph[eng_name]:
                        r = fn(eng)
                        if r is not None:
                            last = r
                    wm = ph["_dmawm"].get(eng_name)
                    if wm is not None:
                        eng.wait_ge(dma_sem[eng_name], wm)
                        last = None
                    if ph[eng_name] and eng_name == "vector":
                        eng.memset(dum[eng_name][:], 0.0).then_inc(bar, 1)
                    elif ph[eng_name] and eng_name == "gpsimd":
                        eng.memset(dum[eng_name][:], 0.0).then_inc(bar, 1)
                    elif ph[eng_name] and eng_name == "scalar":
                        eng.copy(dum[eng_name][:], dum[eng_name][:]).then_inc(bar, 1)
                    elif eng_name == "tensor" and last is not None:
                        last.then_inc(bar, 1)
                    else:
                        eng.nop().then_inc(bar, 1)
                    eng.wait_ge(bar, len(ENGINES) * (pi + 1))
            return body

        for eng_name in ENGINES:
            getattr(block, eng_name)(mk(eng_name))
        return len(phases)



def mk_lnslab(b, tag):
    return {
        "sums": b.sbuf(tag + "_sums", [128, 8], F32),
        "ss": b.sbuf(tag + "_ss", [128, 8], F32),
        "mu": b.sbuf(tag + "_mu", [128, 8], F32),
        "m2": b.sbuf(tag + "_m2", [128, 8], F32),
        "var": b.sbuf(tag + "_var", [128, 8], F32),
        "sd": b.sbuf(tag + "_sd", [128, 8], F32),
        "rstd": b.sbuf(tag + "_rstd", [128, 8], F32),
        "bias": b.sbuf(tag + "_bias", [128, 8], F32),
        "sq": b.sbuf(tag + "_sq", [128, D], F32),
        "eps": b.sbuf(tag + "_eps", [128, 1], F32),
    }


def emit_ln_group(b, s, xg, xng, nj=8, eps=1e-5):
    for j in range(nj):
        b.on("vector", (lambda jj: lambda e: e.reduce_sum(
            s["sums"][:, jj:jj + 1], xg[:, jj, :], axis=AX.X))(j))
        b.on("vector", (lambda jj: lambda e: e.tensor_tensor(
            s["sq"][:], xg[:, jj, :], xg[:, jj, :], op=ALU.mult))(j))
        b.on("vector", (lambda jj: lambda e: e.reduce_sum(
            s["ss"][:, jj:jj + 1], s["sq"][:], axis=AX.X))(j))
    b.on("vector", lambda e: e.tensor_scalar(s["mu"][:, :nj], s["sums"][:, :nj],
                                             1.0 / D, 0.0, op0=ALU.mult,
                                             op1=ALU.add))
    b.bar_phase()
    b.on("vector", lambda e: e.tensor_tensor(s["m2"][:, :nj], s["mu"][:, :nj],
                                             s["mu"][:, :nj], op=ALU.mult))
    b.bar_phase()
    b.on("vector", lambda e: e.scalar_tensor_tensor(
        s["var"][:, :nj], s["ss"][:, :nj], 1.0 / D, s["m2"][:, :nj],
        op0=ALU.mult, op1=ALU.subtract))
    b.bar_phase()
    b.on("scalar", lambda e: e.activation(s["sd"][:, :nj], s["var"][:, :nj],
                                          AF.Sqrt, bias=s["eps"][:]))
    b.bar_phase()
    b.on("vector", lambda e: e.reciprocal(s["rstd"][:, :nj], s["sd"][:, :nj]))
    b.bar_phase()
    b.on("vector", lambda e: e.scalar_tensor_tensor(
        s["bias"][:, :nj], s["mu"][:, :nj], -1.0, s["rstd"][:, :nj],
        op0=ALU.mult, op1=ALU.mult))
    b.bar_phase()
    for j in range(nj):
        b.on("vector", (lambda jj: lambda e: e.tensor_scalar(
            xng[:, jj, :], xg[:, jj, :], s["rstd"][:, jj:jj + 1],
            s["bias"][:, jj:jj + 1], op0=ALU.mult, op1=ALU.add))(j))
    b.bar_phase()


def emit_tok2feat(b, ps, identb, xng, feat_slab, n0, nj=8):
    for j in range(nj):
        pbf = ps[:, j * 512:(j + 1) * 512].bitcast(BF16)

        def mkt(jj, pb):
            def f(e):
                r = None
                for c in range(NCH):
                    r = e.transpose(pb[:, c * 128:(c + 1) * 128],
                                    xng[:, jj, c * 128:(c + 1) * 128], identb[:])
                return r
            return f
        b.on("tensor", mkt(j, pbf))
    b.bar_phase()
    for j in range(nj):
        pbf = ps[:, j * 512:(j + 1) * 512].bitcast(BF16)
        n = n0 + j

        def mke(nn, pb):
            def f(e):
                e.activation(
                    feat_slab[:, :, nn * 128:(nn + 1) * 128],
                    pb.rearrange("p (c t) -> p c t", c=NCH), AF.Copy)
            return f
        b.on("scalar", mke(n, pbf))
    b.bar_phase()


def emit_wload(b, dst_bf, src_hbm_view, stage, nchunks=NCH, scale_pp=None):
    wcols = stage.ap().shape[1]
    for c in range(nchunks):
        b.dma("sync", stage[:], src_hbm_view[:, c, :])
        b.bar_phase()
        if scale_pp is None:
            b.on("vector", (lambda cc: lambda e: e.tensor_copy(
                dst_bf[:, cc, :], stage[:]))(c))
        else:
            b.on("vector", (lambda cc: lambda e: e.tensor_scalar(
                dst_bf[:, cc, :], stage[:], scale_pp[:, cc:cc + 1], None,
                op0=ALU.mult))(c))
        b.bar_phase()



def emit_l1attn(b, aps, xin_hbm, xout_hbm, ps, identb, identf):
    dstg = b.sbuf("l1a_dstg", [128, NCH, 512], BF16)
    bdel = b.sbuf("l1a_bdel", [128, 8], F32)
    a1 = b.sbuf("l1a_a1", [128, 8], F32)
    b1 = b.sbuf("l1a_b1", [128, 8], F32)
    dv = b.sbuf("l1a_dv", [128, 1], F32)
    uc_p = b.sbuf("l1a_ucp", [128, 8, 8], F32)
    cs_p = b.sbuf("l1a_csp", [128, 8, 8], F32)
    uc = b.sbuf("l1a_uc", [128, 8], F32)
    csum = b.sbuf("l1a_csum", [128, 8], F32)
    b1uc = b.sbuf("l1a_b1uc", [128, 8], F32)
    carry = b.sbuf("l1a_carry", [128, 8], F32)
    ones512 = b.sbuf("l1a_ones", [128, 512], F32)
    s = mk_lnslab(b, "l1a_ln")
    xn = b.sbuf("l1a_xn", [128, NCH, T], BF16)

    b.dma("sync", bdel[:], aps["bdel_pp"])
    b.dma("sync", a1[:], aps["A1_pp"])
    b.dma("sync", b1[:], aps["B1_pp"])
    b.dma("sync", dv[:], aps["Dv"])
    b.on("vector", lambda e: e.memset(ones512[:], 1.0))
    b.on("vector", lambda e: e.memset(carry[:], 0.0))
    b.on("vector", lambda e: e.memset(s["eps"][:], 1e-5))
    b.bar_phase()

    xg = b.sbuf("l1a_xg", [128, 8, D], F32)
    xng = b.sbuf("l1a_xng", [128, 8, D], BF16)
    for g in range(4):
        b.dma("sync", xg[:], xin_hbm[:, g * 8:(g + 1) * 8, :])
        b.bar_phase()
        emit_ln_group(b, s, xg, xng)
        emit_tok2feat(b, ps, identb, xng, xn, g * 8)
    if aps.get("_debug"):
        b.dma("gpsimd", aps["xn_dbg"], xn[:])
        b.bar_phase()
    b.free("l1a_xng")
    b.free("l1a_xg")

    wstage = b.sbuf("l1a_wstage", [128, 1024], F32)
    wa = b.sbuf("l1a_wa", [128, NCH, 2048], BF16)
    wr = b.sbuf("l1a_wr", [128, NCH, 1024], BF16)
    csc = b.sbuf("l1a_csc", [128, 512], F32)
    xnf = b.sbuf("l1a_xnf", [128, 512], F32)
    rsl = b.sbuf("l1a_rsl", [128, 8, 512], F32)
    xio = b.sbuf("l1a_xio", [128, 4, D], F32)
    wp_hbm = aps["WpT"].rearrange("(c p) o -> p c o", p=128)
    for c in range(NCH):
        for (dst, lo) in ((wa[:, c, 0:1024], 0), (wa[:, c, 1024:2048], 2048),
                          (wr[:, c, :], 3072)):
            b.dma("sync", wstage[:], wp_hbm[:, c, lo:lo + 1024])
            b.bar_phase()
            b.on("vector", (lambda d: lambda e: e.tensor_copy(d, wstage[:]))(dst))
            b.bar_phase()

    def proj_mms(wsl, ntn):
        def f(e):
            r = None
            for mc in range(8):
                for dic in range(NCH):
                    r = e.matmul(ps[:, mc * 512:(mc + 1) * 512],
                                 wsl(mc, dic),
                                 xn[:, dic, ntn * 512:(ntn + 1) * 512],
                                 start=(dic == 0), stop=(dic == NCH - 1))
            return r
        return f

    for nt in range(8):
        b.on("tensor", proj_mms(
            lambda mc, dic: wa[:, dic, mc * 128:(mc + 1) * 128], nt))
        b.bar_phase()
        for mc in range(8):
            b.on("scalar", (lambda m: lambda e: e.activation(
                dstg[:, m, :], ps[:, m * 512:(m + 1) * 512], AF.Copy))(mc))
        b.bar_phase()
        b.dma("gpsimd", aps["din_hbm"][:, :, nt * 512:(nt + 1) * 512], dstg[:])
        b.bar_phase()
        b.on("tensor", proj_mms(
            lambda mc, dic: wa[:, dic, 1024 + mc * 128:1024 + (mc + 1) * 128], nt))
        b.bar_phase()
        for mc in range(8):
            b.on("vector", (lambda m: lambda e: e.tensor_copy(
                csc[:], ps[:, m * 512:(m + 1) * 512]))(mc))
            b.on("vector", (lambda m, ntn: lambda e: e.tensor_copy(
                xnf[:], xn[:, m, ntn * 512:(ntn + 1) * 512]))(mc, nt))
            b.bar_phase()
            b.on("vector", (lambda m, ntn: lambda e: e.tensor_tensor(
                s["sq"][:, 0:512], csc[:], xnf[:], op=ALU.mult))(mc, nt))
            b.on("vector", (lambda m, ntn: lambda e: e.reduce_sum(
                uc_p[:, m, ntn:ntn + 1], s["sq"][:, 0:512], axis=AX.X))(mc, nt))
            b.on("vector", (lambda m, ntn: lambda e: e.reduce_sum(
                cs_p[:, m, ntn:ntn + 1], csc[:], axis=AX.X))(mc, nt))
            b.bar_phase()
        b.on("tensor", proj_mms(
            lambda mc, dic: wr[:, dic, mc * 128:(mc + 1) * 128], nt))
        b.bar_phase()
        for mc in range(8):
            b.on("vector", (lambda m: lambda e: e.tensor_scalar(
                rsl[:, m, :], ps[:, m * 512:(m + 1) * 512], dv[:, 0:1], None,
                op0=ALU.mult))(mc))
        b.bar_phase()
        b.dma("sync", xio[:], xin_hbm[:, nt * 4:(nt + 1) * 4, :])

        def tr_all(e):
            r = None
            for st in range(4):
                for oc in range(8):
                    r = e.transpose(
                        ps[:, st * 1024 + oc * 128: st * 1024 + (oc + 1) * 128],
                        rsl[:, oc, st * 128:(st + 1) * 128], identf[:])
            return r
        b.on("tensor", tr_all)
        b.bar_phase()
        for st in range(4):
            b.on("vector", (lambda ss: lambda e: e.tensor_tensor(
                xio[:, ss, :], xio[:, ss, :], ps[:, ss * 1024:(ss + 1) * 1024],
                op=ALU.add))(st))
        b.bar_phase()
        b.dma("gpsimd", xout_hbm[:, nt * 4:(nt + 1) * 4, :], xio[:])
        b.bar_phase()

    for c in range(NCH):
        b.on("vector", (lambda cc: lambda e: e.reduce_sum(
            uc[:, cc:cc + 1], uc_p[:, cc, :], axis=AX.X))(c))
        b.on("vector", (lambda cc: lambda e: e.reduce_sum(
            csum[:, cc:cc + 1], cs_p[:, cc, :], axis=AX.X))(c))
    b.bar_phase()
    b.on("vector", lambda e: e.tensor_tensor(b1uc[:], b1[:], uc[:], op=ALU.mult))
    b.bar_phase()
    if aps.get("_debug"):
        b.dma("gpsimd", aps["ucsum_dbg"][:, 0:8], uc[:])
        b.dma("gpsimd", aps["ucsum_dbg"][:, 8:16], csum[:])
        b.bar_phase()

    for nm in ("l1a_xio", "l1a_rsl", "l1a_xnf", "l1a_csc", "l1a_wr", "l1a_wa",
               "l1a_wstage", "l1a_xn"):
        b.free(nm)

    wstage2 = b.sbuf("l1a_wstage2", [128, 1024], F32)
    wd = b.sbuf("l1a_wd", [128, NCH, 1024], BF16)
    spA = b.sbuf("l1a_spA", [128, 8, 512], F32)
    spB = b.sbuf("l1a_spB", [128, 8, 512], F32)
    dlt = b.sbuf("l1a_dlt", [128, 8, 512], F32)
    ysl = b.sbuf("l1a_ysl", [128, 8, 512], F32)
    xio2 = b.sbuf("l1a_xio2", [128, 4, D], F32)
    wd_hbm = aps["WdT"].rearrange("(c p) o -> p c o", p=128)
    for c in range(NCH):
        b.dma("sync", wstage2[:], wd_hbm[:, c, :])
        b.bar_phase()
        b.on("vector", (lambda cc: lambda e: e.tensor_copy(
            wd[:, cc, :], wstage2[:]))(c))
        b.bar_phase()

    for nt in range(8):
        b.dma("sync", dstg[:], aps["din_hbm"][:, :, nt * 512:(nt + 1) * 512])
        b.bar_phase()

        def mmd(e):
            r = None
            for oc in range(8):
                for dic in range(NCH):
                    r = e.matmul(ps[:, oc * 512:(oc + 1) * 512],
                                 wd[:, dic, oc * 128:(oc + 1) * 128],
                                 dstg[:, dic, :],
                                 start=(dic == 0), stop=(dic == NCH - 1))
            return r
        b.on("tensor", mmd)
        b.bar_phase()
        for oc in range(8):
            b.on("scalar", (lambda o: lambda e: e.activation(
                spA[:, o, :], ps[:, o * 512:(o + 1) * 512], AF.Abs,
                bias=bdel[:, o:o + 1]))(oc))
        b.bar_phase()
        b.on("scalar", lambda e: e.activation(
            spB[:].rearrange("p c t -> p (c t)"),
            spA[:].rearrange("p c t -> p (c t)"), AF.Exp, scale=-1.0))
        b.bar_phase()
        b.on("vector", lambda e: e.tensor_scalar(
            spB[:].rearrange("p c t -> p (c t)"),
            spB[:].rearrange("p c t -> p (c t)"), 1.0, 1.0,
            op0=ALU.mult, op1=ALU.add))
        b.bar_phase()
        b.on("scalar", lambda e: e.activation(
            spA[:].rearrange("p c t -> p (c t)"),
            spB[:].rearrange("p c t -> p (c t)"), AF.Ln))
        b.bar_phase()
        for oc in range(8):
            b.on("scalar", (lambda o: lambda e: e.activation(
                spB[:, o, :], ps[:, o * 512:(o + 1) * 512], AF.Relu,
                bias=bdel[:, o:o + 1]))(oc))
        b.bar_phase()
        b.on("vector", lambda e: e.tensor_tensor(
            dlt[:].rearrange("p c t -> p (c t)"),
            spA[:].rearrange("p c t -> p (c t)"),
            spB[:].rearrange("p c t -> p (c t)"), op=ALU.add))
        b.bar_phase()
        for oc in range(8):
            b.on("scalar", (lambda o: lambda e: e.activation(
                spA[:, o, :], dlt[:, o, :], AF.Exp, scale=a1[:, o:o + 1]))(oc))
        b.bar_phase()
        for oc in range(8):
            b.on("vector", (lambda o: lambda e: e.tensor_scalar(
                ysl[:, o, :], spA[:, o, :], b1uc[:, o:o + 1], None,
                op0=ALU.mult))(oc))
            b.on("vector", (lambda o: lambda e: e.tensor_scalar(
                spB[:, o, :], dlt[:, o, :], b1[:, o:o + 1], None,
                op0=ALU.mult))(oc))
        b.bar_phase()
        for oc in range(8):
            b.on("vector", (lambda o: lambda e: e.tensor_tensor_scan(
                spA[:, o, :], ones512[:], spB[:, o, :], carry[:, o:o + 1],
                op0=ALU.mult, op1=ALU.add))(oc))
        b.bar_phase()
        for oc in range(8):
            b.on("vector", (lambda o: lambda e: e.tensor_copy(
                carry[:, o:o + 1], spA[:, o, 511:512]))(oc))
        b.bar_phase()
        for oc in range(8):
            b.on("vector", (lambda o: lambda e: e.scalar_tensor_tensor(
                ysl[:, o, :], spA[:, o, :], csum[:, o:o + 1], ysl[:, o, :],
                op0=ALU.mult, op1=ALU.add))(oc))
        b.bar_phase()
        b.dma("sync", xio2[:], xout_hbm[:, nt * 4:(nt + 1) * 4, :])

        def tr_all2(e):
            r = None
            for st in range(4):
                for oc in range(8):
                    r = e.transpose(
                        ps[:, st * 1024 + oc * 128: st * 1024 + (oc + 1) * 128],
                        ysl[:, oc, st * 128:(st + 1) * 128], identf[:])
            return r
        b.on("tensor", tr_all2)
        b.bar_phase()
        for st in range(4):
            b.on("vector", (lambda ss: lambda e: e.tensor_tensor(
                xio2[:, ss, :], xio2[:, ss, :], ps[:, ss * 1024:(ss + 1) * 1024],
                op=ALU.add))(st))
        b.bar_phase()
        b.dma("gpsimd", xout_hbm[:, nt * 4:(nt + 1) * 4, :], xio2[:])
        b.bar_phase()

    for nm in ("l1a_xio2", "l1a_ysl", "l1a_dlt", "l1a_spB", "l1a_spA", "l1a_wd",
               "l1a_wstage2"):
        b.free(nm)
    for nm in ("l1a_ln_eps", "l1a_ln_sq", "l1a_ln_bias", "l1a_ln_rstd",
               "l1a_ln_sd", "l1a_ln_var", "l1a_ln_m2", "l1a_ln_mu", "l1a_ln_ss",
               "l1a_ln_sums", "l1a_ones", "l1a_carry", "l1a_b1uc", "l1a_csum",
               "l1a_uc", "l1a_csp", "l1a_ucp", "l1a_dv", "l1a_b1", "l1a_a1",
               "l1a_bdel", "l1a_dstg"):
        b.free(nm)



def emit_l1ffn(b, aps, xin_hbm, xout_hbm, ps, identb, identf, stop_after=None):
    logT = b.sbuf("l1f_logT", [8, T], F32)
    cwT = b.sbuf("l1f_cwT", [8, T], F32)
    cwTb = b.sbuf("l1f_cwTb", [8, T], BF16)
    beb = b.sbuf("l1f_beb", [8, D], BF16)
    bef = b.sbuf("l1f_bef", [8, D], F32)
    wg_hi = b.sbuf("l1f_wghi", [128, NCH, 8], BF16)
    wg_lo = b.sbuf("l1f_wglo", [128, NCH, 8], BF16)
    wg_f = b.sbuf("l1f_wgf", [128, NCH, 8], F32)
    wg_t = b.sbuf("l1f_wgt", [128, NCH, 8], F32)
    onesb = b.sbuf("l1f_onesb", [1, 128], BF16)
    s = mk_lnslab(b, "l1f_ln")
    m1 = b.sbuf("l1f_m1", [128, 8], F32)
    m2 = b.sbuf("l1f_m2", [128, 8], F32)
    sg = b.sbuf("l1f_sg", [128, 8], F32)
    w1 = b.sbuf("l1f_w1", [128, 8], F32)
    w2 = b.sbuf("l1f_w2", [128, 8], F32)
    msk1 = b.sbuf("l1f_msk1", [128, 8, 8], F32)
    msk2 = b.sbuf("l1f_msk2", [128, 8, 8], F32)
    lgt = b.sbuf("l1f_lgt", [128, 8, 8], F32)
    cwg = b.sbuf("l1f_cwg", [128, 8, 8], F32)
    xn = b.sbuf("l1f_xn", [128, NCH, T], BF16)

    b.dma("sync", wg_f[:], aps["WgT"].rearrange("(c p) e -> p c e", p=128))
    b.dma("sync", bef[:], aps["be"])
    b.on("vector", lambda e: e.memset(onesb[:], 1.0))
    b.on("vector", lambda e: e.memset(s["eps"][:], 1e-5))
    b.bar_phase()
    b.on("vector", lambda e: e.tensor_copy(
        wg_hi.ap().rearrange("p c e -> p (c e)"),
        wg_f.ap().rearrange("p c e -> p (c e)")))
    b.on("vector", lambda e: e.tensor_copy(beb[:], bef[:]))
    b.bar_phase()
    b.on("vector", lambda e: e.tensor_copy(
        wg_t.ap().rearrange("p c e -> p (c e)"),
        wg_hi.ap().rearrange("p c e -> p (c e)")))
    b.bar_phase()
    b.on("vector", lambda e: e.tensor_tensor(
        wg_t.ap().rearrange("p c e -> p (c e)"),
        wg_f.ap().rearrange("p c e -> p (c e)"),
        wg_t.ap().rearrange("p c e -> p (c e)"), op=ALU.subtract))
    b.bar_phase()
    b.on("vector", lambda e: e.tensor_copy(
        wg_lo.ap().rearrange("p c e -> p (c e)"),
        wg_t.ap().rearrange("p c e -> p (c e)")))
    b.bar_phase()
    if stop_after == "prep":
        return

    xg = b.sbuf("l1f_xg", [128, 8, D], F32)
    xng = b.sbuf("l1f_xng", [128, 8, D], BF16)
    xlg = b.sbuf("l1f_xlg", [128, 8, D], BF16)
    xlo = b.sbuf("l1f_xlo", [128, NCH, 1024], BF16)
    for g in range(4):
        b.dma("sync", xg[:], xin_hbm[:, g * 8:(g + 1) * 8, :])
        b.bar_phase()
        emit_ln_group(b, s, xg, xng)
        for j in range(8):
            b.on("vector", (lambda jj: lambda e: e.tensor_scalar(
                s["sq"][:], xg[:, jj, :], s["rstd"][:, jj:jj + 1],
                s["bias"][:, jj:jj + 1], op0=ALU.mult, op1=ALU.add))(jj if False else j))
            b.on("vector", (lambda jj: lambda e: e.tensor_tensor(
                xlg[:, jj, :], s["sq"][:], xng[:, jj, :], op=ALU.subtract))(j))
        b.bar_phase()
        emit_tok2feat(b, ps, identb, xng, xn, g * 8)
        for j in range(8):
            pbf = ps[:, j * 512:(j + 1) * 512].bitcast(BF16)
            b.on("tensor", (lambda jj, pb: lambda e: [
                e.transpose(pb[:, c * 128:(c + 1) * 128],
                            xlg[:, jj, c * 128:(c + 1) * 128], identb[:])
                for c in range(NCH)][-1])(j, pbf))
        b.bar_phase()
        for j in range(8):
            pbf = ps[:, j * 512:(j + 1) * 512].bitcast(BF16)
            b.on("scalar", (lambda jj, pb: lambda e: e.activation(
                xlo[:, :, jj * 128:(jj + 1) * 128],
                pb.rearrange("p (c t) -> p c t", c=NCH), AF.Copy))(j, pbf))
        b.bar_phase()
        def gmm(gg):
            def f(e):
                r = None
                for half in range(2):
                    pb = ps[0:8, half * 512:(half + 1) * 512]
                    for dic in range(NCH):
                        xs_hi = xn[:, dic, gg * 1024 + half * 512:
                                   gg * 1024 + (half + 1) * 512]
                        xs_lo = xlo[:, dic, half * 512:(half + 1) * 512]
                        e.matmul(pb, wg_hi[:, dic, :], xs_hi,
                                 start=(dic == 0), stop=False)
                        e.matmul(pb, wg_lo[:, dic, :], xs_hi,
                                 start=False, stop=False)
                        r = e.matmul(pb, wg_hi[:, dic, :], xs_lo,
                                     start=False, stop=(dic == NCH - 1))
                return r
            return f
        if stop_after != "nogmm":
            b.on("tensor", gmm(g))
            b.bar_phase()
            if stop_after != "noevac":
                b.on("scalar", (lambda gg: lambda e: e.activation(
                    logT[:, gg * 1024:gg * 1024 + 512], ps[0:8, 0:512],
                    AF.Copy))(g))
                b.on("scalar", (lambda gg: lambda e: e.activation(
                    logT[:, gg * 1024 + 512:(gg + 1) * 1024],
                    ps[0:8, 512:1024], AF.Copy))(g))
                b.bar_phase()
    b.free("l1f_xlo")
    b.free("l1f_xlg")
    b.free("l1f_xng")
    b.free("l1f_xg")
    if stop_after in ("p1", "nogmm", "noevac"):
        return

    for g in range(4):
        def ltr(gg):
            def f(e):
                r = None
                for j in range(8):
                    n = gg * 8 + j
                    r = e.transpose(ps[:, j * 512:j * 512 + 8],
                                    logT[:, n * 128:(n + 1) * 128],
                                    identf[0:8, 0:8])
                return r
            return f
        b.on("tensor", ltr(g))
        b.bar_phase()
        for j in range(8):
            b.on("scalar", (lambda jj: lambda e: e.activation(
                lgt[:, jj, :], ps[:, jj * 512:jj * 512 + 8], AF.Copy))(j))
        b.bar_phase()
        for j in range(8):
            b.on("vector", (lambda jj: lambda e: e.reduce_max(
                m1[:, jj:jj + 1], lgt[:, jj, :], axis=AX.X))(j))
        b.bar_phase()
        for j in range(8):
            b.on("vector", (lambda jj: lambda e: e.tensor_single_scalar(
                msk1[:, jj, :], lgt[:, jj, :], m1[:, jj:jj + 1],
                op=ALU.is_equal))(j))
        b.bar_phase()
        for j in range(8):
            b.on("vector", (lambda jj: lambda e: e.scalar_tensor_tensor(
                cwg[:, jj, :], msk1[:, jj, :], -1e30, lgt[:, jj, :],
                op0=ALU.mult, op1=ALU.add))(j))
        b.bar_phase()
        for j in range(8):
            b.on("vector", (lambda jj: lambda e: e.reduce_max(
                m2[:, jj:jj + 1], cwg[:, jj, :], axis=AX.X))(j))
        b.bar_phase()
        for j in range(8):
            b.on("vector", (lambda jj: lambda e: e.tensor_single_scalar(
                msk2[:, jj, :], cwg[:, jj, :], m2[:, jj:jj + 1],
                op=ALU.is_equal))(j))
        b.bar_phase()
        b.on("vector", lambda e: e.tensor_tensor(sg[:], m2[:], m1[:],
                                                 op=ALU.subtract))
        b.bar_phase()
        b.on("scalar", lambda e: e.activation(sg[:], sg[:], AF.Exp))
        b.bar_phase()
        b.on("vector", lambda e: e.tensor_scalar(w1[:], sg[:], 1.0, 1.0,
                                                 op0=ALU.mult, op1=ALU.add))
        b.bar_phase()
        b.on("vector", lambda e: e.reciprocal(w1[:], w1[:]))
        b.bar_phase()
        b.on("vector", lambda e: e.tensor_tensor(w2[:], sg[:], w1[:],
                                                 op=ALU.mult))
        b.bar_phase()
        for j in range(8):
            b.on("vector", (lambda jj: lambda e: e.tensor_scalar(
                cwg[:, jj, :], msk1[:, jj, :], w1[:, jj:jj + 1], 0.0,
                op0=ALU.mult, op1=ALU.add))(j))
        b.bar_phase()
        for j in range(8):
            b.on("vector", (lambda jj: lambda e: e.scalar_tensor_tensor(
                cwg[:, jj, :], msk2[:, jj, :], w2[:, jj:jj + 1], cwg[:, jj, :],
                op0=ALU.mult, op1=ALU.add))(j))
        b.bar_phase()
        def ctr(gg):
            def f(e):
                r = None
                for j in range(8):
                    r = e.transpose(ps[0:8, j * 512:j * 512 + 128],
                                    cwg[:, j, :], identf[:])
                return r
            return f
        b.on("tensor", ctr(g))
        b.bar_phase()
        for j in range(8):
            n = g * 8 + j
            b.on("scalar", (lambda nn, jj: lambda e: e.activation(
                cwT[:, nn * 128:(nn + 1) * 128],
                ps[0:8, jj * 512:jj * 512 + 128], AF.Copy))(n, j))
        b.bar_phase()
    b.on("vector", lambda e: e.tensor_copy(cwTb[:], cwT[:]))
    b.bar_phase()
    if stop_after == "top2":
        return

    westg = b.sbuf("l1f_westg", [128, 4, D], F32)
    webst = b.sbuf("l1f_webst", [128, 4, D], BF16)
    we_hbm = aps["WeT"].rearrange("e (c p) o -> p e c o", p=128)
    for e_ in range(8):
        for hh in range(2):
            b.dma("sync", westg[:], we_hbm[:, e_, hh * 4:(hh + 1) * 4, :])
            b.bar_phase()
            b.on("vector", (lambda: lambda e: e.tensor_copy(
                webst.ap().rearrange("p c o -> p (c o)"),
                westg.ap().rearrange("p c o -> p (c o)")))())
            b.bar_phase()
            b.dma("gpsimd", aps["webf_hbm"][:, e_, hh * 4:(hh + 1) * 4, :],
                  webst[:])
            b.bar_phase()
    b.free("l1f_webst")
    b.free("l1f_westg")
    if stop_after == "wecast":
        return

    wes = b.sbuf("l1f_wes", [128, NCH, D], BF16)
    xe = b.sbuf("l1f_xe", [128, NCH, 512], BF16)
    rows = b.sbuf("l1f_rows", [1, 8 * 512], BF16)
    cwb8 = b.sbuf("l1f_cwb8", [128, 8, 512], BF16)
    ysl = b.sbuf("l1f_ysl", [128, 8, 512], F32)
    xio = b.sbuf("l1f_xio", [128, 4, D], F32)
    for nt in range(8):
        for e_ in range(8):
            b.dma("gpsimd", rows[0:1, e_ * 512:(e_ + 1) * 512],
                  cwTb[e_:e_ + 1, nt * 512:(nt + 1) * 512])
        b.bar_phase()
        def bmm(e):
            r = None
            for e_ in range(8):
                r = e.matmul(ps[:, e_ * 512:(e_ + 1) * 512], onesb[:],
                             rows[0:1, e_ * 512:(e_ + 1) * 512],
                             start=True, stop=True)
            return r
        b.on("tensor", bmm)
        b.bar_phase()
        for e_ in range(8):
            b.on("scalar", (lambda ee: lambda e: e.activation(
                cwb8[:, ee, :], ps[:, ee * 512:(ee + 1) * 512], AF.Copy))(e_))
        b.bar_phase()
        for e_ in range(8):
            b.dma("sync", wes[:], aps["webf_hbm"][:, e_, :, :])
            for c in range(NCH):
                b.on("vector", (lambda ee, cc, ntn: lambda e: e.tensor_tensor(
                    xe[:, cc, :], xn[:, cc, ntn * 512:(ntn + 1) * 512],
                    cwb8[:, ee, :], op=ALU.mult))(e_, c, nt))
            b.bar_phase()
            def emm(ee):
                def f(e):
                    r = None
                    for oc in range(8):
                        for dic in range(NCH):
                            r = e.matmul(ps[:, oc * 512:(oc + 1) * 512],
                                         wes[:, dic, oc * 128:(oc + 1) * 128],
                                         xe[:, dic, :],
                                         start=(ee == 0 and dic == 0),
                                         stop=False)
                    return r
                return f
            b.on("tensor", emm(e_))
            b.bar_phase()
        def biasmm(ntn):
            def f(e):
                r = None
                for oc in range(8):
                    r = e.matmul(ps[:, oc * 512:(oc + 1) * 512],
                                 beb[:, oc * 128:(oc + 1) * 128],
                                 cwTb[:, ntn * 512:(ntn + 1) * 512],
                                 start=False, stop=True)
                return r
            return f
        b.on("tensor", biasmm(nt))
        b.bar_phase()
        for oc in range(8):
            b.on("scalar", (lambda o: lambda e: e.activation(
                ysl[:, o, :], ps[:, o * 512:(o + 1) * 512], AF.Copy))(oc))
        b.bar_phase()
        b.dma("sync", xio[:], xin_hbm[:, nt * 4:(nt + 1) * 4, :])

        def trf(e):
            r = None
            for st in range(4):
                for oc in range(8):
                    r = e.transpose(
                        ps[:, st * 1024 + oc * 128: st * 1024 + (oc + 1) * 128],
                        ysl[:, oc, st * 128:(st + 1) * 128], identf[:])
            return r
        b.on("tensor", trf)
        b.bar_phase()
        for st in range(4):
            b.on("vector", (lambda ss: lambda e: e.tensor_tensor(
                xio[:, ss, :], xio[:, ss, :], ps[:, ss * 1024:(ss + 1) * 1024],
                op=ALU.add))(st))
        b.bar_phase()
        b.dma("gpsimd", xout_hbm[:, nt * 4:(nt + 1) * 4, :], xio[:])
        b.bar_phase()

    for nm in ("l1f_xio", "l1f_ysl", "l1f_cwb8", "l1f_rows", "l1f_xe",
               "l1f_wes", "l1f_xn", "l1f_cwg", "l1f_lgt", "l1f_msk2",
               "l1f_msk1", "l1f_w2", "l1f_w1", "l1f_sg", "l1f_m2", "l1f_m1"):
        b.free(nm)
    for nm in ("l1f_ln_eps", "l1f_ln_sq", "l1f_ln_bias", "l1f_ln_rstd",
               "l1f_ln_sd", "l1f_ln_var", "l1f_ln_m2", "l1f_ln_mu",
               "l1f_ln_ss", "l1f_ln_sums", "l1f_onesb", "l1f_wgt", "l1f_wgf",
               "l1f_wglo", "l1f_wghi", "l1f_bef", "l1f_beb", "l1f_cwTb",
               "l1f_cwT", "l1f_logT"):
        b.free(nm)


def l1ffn_host_inputs(inputs):
    Wg = np.asarray(inputs["L1_ffn_Wg"], np.float32)
    We = np.asarray(inputs["L1_ffn_We"], np.float32)
    be = np.asarray(inputs["L1_ffn_be"], np.float32)
    return {
        "WgT": np.ascontiguousarray(Wg.T),
        "WeT": np.ascontiguousarray(We.transpose(0, 2, 1)),
        "be": np.ascontiguousarray(be),
    }


def build_l1ffn_prog(stop_after=None):
    nc = bass.Bass("TRN2", target_bir_lowering=False)
    xd = nc.dram_tensor("x", [T, D], F32, kind="ExternalInput").ap()
    aps = {
        "WgT": nc.dram_tensor("WgT", [D, 8], F32, kind="ExternalInput").ap(),
        "WeT": nc.dram_tensor("WeT", [8, D, D], F32, kind="ExternalInput").ap(),
        "be": nc.dram_tensor("be", [8, D], F32, kind="ExternalInput").ap(),
        "webf_hbm": nc.dram_tensor("webf_hbm", [128, 8, NCH, D], BF16,
                                   kind="Internal").ap(),
    }
    identbd = nc.dram_tensor("identb", [128, 128], BF16, kind="ExternalInput").ap()
    identfd = nc.dram_tensor("identf", [128, 128], F32, kind="ExternalInput").ap()
    xod = nc.dram_tensor("xo", [T, D], F32, kind="ExternalOutput").ap()

    b = Bld(nc)
    identb = b.sbuf("identbS", [128, 128], BF16)
    identf = b.sbuf("identfS", [128, 128], F32)
    ps = b.psum("PS", [128, 4096], F32)
    b.dma("sync", identb[:], identbd)
    b.dma("sync", identf[:], identfd)
    b.bar_phase()
    xin_hbm = xd.rearrange("(n p) d -> p n d", p=128)
    xout_hbm = xod.rearrange("(n p) d -> p n d", p=128)
    emit_l1ffn(b, aps, xin_hbm, xout_hbm, ps, identb, identf, stop_after)
    b.finalize()
    return nc



def emit_l0attn(b, aps, xin_hbm, xout_hbm, ps, identb, identf):
    s = mk_lnslab(b, "l0a_ln")
    bq = b.sbuf("l0a_bq", [128, 24], F32)
    bk8 = b.sbuf("l0a_bk8", [128, 8], F32)
    wrx = b.sbuf("l0a_wrxf", [128, 512], F32)
    wrxb = b.sbuf("l0a_wrxb", [128, 512], BF16)
    brx = b.sbuf("l0a_brx", [128, 512], F32)
    rpc = b.sbuf("l0a_rpc", [128, 2, 8, 32], F32)
    den_p = b.sbuf("l0a_denp", [128, 2, 16], F32)
    recip = b.sbuf("l0a_recip", [128, 8], F32)
    xn = b.sbuf("l0a_xn", [128, NCH, T], BF16)
    b.dma("sync", bq[:], aps["bqkv_pp"])
    b.dma("sync", wrx[:], aps["wrx2"])
    b.dma("sync", brx[:], aps["brx_pp"])
    b.on("vector", lambda e: e.memset(s["eps"][:], 1e-5))
    b.bar_phase()
    b.on("vector", lambda e: e.tensor_scalar(bk8[:], bq[:, 8:16], 0.125, 0.0,
                                             op0=ALU.mult, op1=ALU.add))
    b.on("vector", lambda e: e.tensor_copy(wrxb[:], wrx[:]))
    b.bar_phase()

    xg = b.sbuf("l0a_xg", [128, 8, D], F32)
    xng = b.sbuf("l0a_xng", [128, 8, D], BF16)
    for g in range(4):
        b.dma("sync", xg[:], xin_hbm[:, g * 8:(g + 1) * 8, :])
        b.bar_phase()
        emit_ln_group(b, s, xg, xng)
        emit_tok2feat(b, ps, identb, xng, xn, g * 8)
    b.free("l0a_xng")
    b.free("l0a_xg")

    wstg = b.sbuf("l0a_wstg", [128, 8, 128], F32)
    wbf = b.sbuf("l0a_wbf", [128, 8, 128], BF16)
    qsl = b.sbuf("l0a_qsl", [128, 4, T], BF16)
    wq_hbm = aps["WqkvT"].rearrange("(c p) o -> p c o", p=128)
    for hg in range(2):
        for kind in range(3):
            for pcl in range(4):
                mc = kind * 8 + hg * 4 + pcl
                b.dma("sync", wstg[:], wq_hbm[:, :, mc * 128:(mc + 1) * 128])
                b.bar_phase()
                b.on("vector", lambda e: e.tensor_copy(
                    wbf.ap().rearrange("p c o -> p (c o)"),
                    wstg.ap().rearrange("p c o -> p (c o)")))
                b.bar_phase()

                def qkvmm(e):
                    r = None
                    for nt in range(8):
                        for dic in range(NCH):
                            r = e.matmul(ps[:, nt * 512:(nt + 1) * 512],
                                         wbf[:, dic, :],
                                         xn[:, dic, nt * 512:(nt + 1) * 512],
                                         start=(dic == 0),
                                         stop=(dic == NCH - 1))
                    return r
                b.on("tensor", qkvmm)
                b.bar_phase()
                for nt in range(8):
                    if kind == 1:
                        b.on("scalar", (lambda ntn, m=mc, p_=pcl: lambda e:
                             e.activation(qsl[:, p_, ntn * 512:(ntn + 1) * 512],
                                          ps[:, ntn * 512:(ntn + 1) * 512],
                                          AF.Identity,
                                          bias=bk8[:, m - 8:m - 7],
                                          scale=0.125))(nt))
                    else:
                        b.on("scalar", (lambda ntn, m=mc, p_=pcl: lambda e:
                             e.activation(qsl[:, p_, ntn * 512:(ntn + 1) * 512],
                                          ps[:, ntn * 512:(ntn + 1) * 512],
                                          AF.Identity,
                                          bias=bq[:, m:m + 1]))(nt))
                b.bar_phase()
            b.dma("gpsimd", aps["qkv_hbm"][hg, kind], qsl[:])
            b.bar_phase()
    b.free("l0a_qsl")
    b.free("l0a_wbf")
    b.free("l0a_wstg")
    b.free("l0a_xn")

    for hg in range(2):
        qs = b.sbuf(f"l0a_qs{hg}", [128, 4, T], BF16)
        ks = b.sbuf(f"l0a_ks{hg}", [128, 4, T], BF16)
        vt = b.sbuf(f"l0a_vt{hg}", [128, 32, 512], BF16)
        vl = b.sbuf(f"l0a_vl{hg}", [128, 4, T], BF16)
        b.dma("sync", qs[:], aps["qkv_hbm"][hg, 0])
        b.dma("sync", ks[:], aps["qkv_hbm"][hg, 1])
        b.dma("sync", vl[:], aps["qkv_hbm"][hg, 2])
        b.bar_phase()
        for pcl in range(4):
            def vtr(p_):
                def f(e):
                    r = None
                    for jsub in range(32):
                        bank = jsub // 4
                        r = e.transpose(
                            ps[:, bank * 512:(bank + 1) * 512].bitcast(BF16)
                            [:, (jsub % 4) * 128:(jsub % 4) * 128 + 128],
                            vl[:, p_, jsub * 128:(jsub + 1) * 128], identb[:])
                    return r
                return f
            b.on("tensor", vtr(pcl))
            b.bar_phase()
            for bank in range(8):
                b.on("scalar", (lambda p_, bk: lambda e: e.activation(
                    vt[:, bk * 4:(bk + 1) * 4, p_ * 128:(p_ + 1) * 128],
                    ps[:, bk * 512:(bk + 1) * 512].bitcast(BF16)[:, 0:512]
                    .rearrange("p (a t) -> p a t", a=4),
                    AF.Copy))(pcl, bank))
            b.bar_phase()
        b.free(f"l0a_vl{hg}")
        rp_sb = b.sbuf(f"l0a_rpsb{hg}", [128, 8, 512], F32)

        for qt in range(32):
            def rpmm(qtn):
                def f(e):
                    rv = None
                    for pcl in range(4):
                        for hh in range(2):
                            rv = e.matmul(
                                ps[:, (pcl * 2 + hh) * 512:
                                   (pcl * 2 + hh + 1) * 512],
                                qs[hh * 64:(hh + 1) * 64, pcl,
                                   qtn * 128:(qtn + 1) * 128],
                                wrxb[hh * 64:(hh + 1) * 64, :],
                                start=True, stop=True,
                                tile_position=(hh * 64, 0))
                    return rv
                return f
            b.on("tensor", rpmm(qt))
            b.bar_phase()
            for hl in range(8):
                b.on("vector", (lambda h: lambda e: e.tensor_tensor(
                    rp_sb[:, h, :], ps[:, h * 512:(h + 1) * 512], brx[:],
                    op=ALU.add))(hl))
            for hl in range(8):
                b.on("vector", (lambda h, qtn: lambda e: e.tensor_copy(
                    rpc[:, 0, h, qtn:qtn + 1], rp_sb[:, h, 0:1]))(hl, qt))
                b.on("vector", (lambda h, qtn: lambda e: e.tensor_copy(
                    rpc[:, 1, h, qtn:qtn + 1], rp_sb[:, h, 511:512]))(hl, qt))
            b.bar_phase()
            b.dma("gpsimd",
                  aps["rp_hbm"][hg * 8:(hg + 1) * 8, qt].rearrange(
                      "h p u -> p h u"),
                  rp_sb[:])
            b.bar_phase()
        b.free(f"l0a_rpsb{hg}")

        Pp = b.sbuf(f"l0a_Pp{hg}", [128, 2, T], BF16)
        PT4 = b.sbuf(f"l0a_PT4{hg}", [128, 2, 32, 512], BF16)
        Eb2 = b.sbuf(f"l0a_Eb2{hg}", [128, 2, 384], F32)
        osl4 = b.sbuf(f"l0a_osl4{hg}", [128, 4, 512], F32)
        xio = b.sbuf(f"l0a_xio{hg}", [128, 2, D], F32)
        den_g = b.sbuf(f"l0a_deng{hg}", [128, 32], F32)

        for qg in range(8):
            for pc in range(4):
                for qtl in range(4):
                    qt = qg * 4 + qtl
                    q0 = qt * 128
                    jb0 = max(0, q0 - 128)
                    jb1 = min(T, q0 + 256)
                    b.on("vector", lambda e: e.memset(
                        den_p.ap().rearrange("p h k -> p (h k)")[:, 0:32], 0.0))
                    for hh in range(2):
                        h_abs = hg * 8 + pc * 2 + hh
                        off = ((h_abs * 32 + qt) * 128 * 512) + (jb0 - q0 + 256)
                        b.dma("sync", Eb2[:, hh, 0:jb1 - jb0],
                              AP(aps["rp_hbm_h"], off, [[511, 128],
                                                        [1, jb1 - jb0]]))
                    b.bar_phase()

                    def smm(jcn, qtn=qt, pcn=pc):
                        st = (jcn % 2) * 2

                        def f(e):
                            rv = None
                            for hh in range(2):
                                rv = e.matmul(
                                    ps[:, (st + hh) * 512:(st + hh + 1) * 512],
                                    qs[hh * 64:(hh + 1) * 64, pcn,
                                       qtn * 128:(qtn + 1) * 128],
                                    ks[hh * 64:(hh + 1) * 64, pcn,
                                       jcn * 512:(jcn + 1) * 512],
                                    start=True, stop=True,
                                    tile_position=(hh * 64, 0))
                            return rv
                        return f

                    def banddve(jcn):
                        c0 = jcn * 512
                        bl0, bl1 = max(c0, jb0), min(c0 + 512, jb1)
                        st = (jcn % 2) * 2
                        for hh in range(2):
                            b.on("vector", (lambda h, a0=bl0, a1=bl1, cc=c0,
                                            jb=jb0, s_=st: lambda e:
                                 e.tensor_tensor(
                                     ps[:, (s_ + h) * 512 + (a0 - cc):
                                        (s_ + h) * 512 + (a1 - cc)],
                                     ps[:, (s_ + h) * 512 + (a0 - cc):
                                        (s_ + h) * 512 + (a1 - cc)],
                                     Eb2[:, h, (a0 - jb):(a1 - jb)],
                                     op=ALU.add))(hh))

                    def expph(jcn, qtn=qt):
                        c0, c1 = jcn * 512, (jcn + 1) * 512
                        bl0, bl1 = max(c0, jb0), min(c1, jb1)
                        st = (jcn % 2) * 2
                        for hh in range(2):
                            hl = pc * 2 + hh
                            pieces = []
                            if c0 < jb0:
                                pieces.append((c0, min(c1, jb0), 0))
                            if bl0 < bl1:
                                pieces.append((bl0, bl1, None))
                            if c1 > jb1:
                                pieces.append((max(c0, jb1), c1, 1))
                            for k_, (a0, a1, bi) in enumerate(pieces):
                                slot = jcn if len(pieces) == 1 else                                     8 + (jcn % 2) * 4 + k_
                                kw = {}
                                if bi is not None:
                                    kw["bias"] = rpc[:, bi, hl, qtn:qtn + 1]
                                b.on("scalar", (lambda h, x0=a0, x1=a1, cc=c0,
                                                sl=slot, s_=st,
                                                kww=dict(kw): lambda e:
                                     e.activation(
                                         Pp[:, h, x0:x1],
                                         ps[:, (s_ + h) * 512 + (x0 - cc):
                                            (s_ + h) * 512 + (x1 - cc)],
                                         AF.Exp,
                                         accum_out=den_p[:, h, sl:sl + 1],
                                         **kww))(hh))

                    pend = None
                    for jc in range(8):
                        c0 = jc * 512
                        isband = max(c0, jb0) < min(c0 + 512, jb1)
                        if isband:
                            if pend is not None:
                                expph(pend)
                                b.bar_phase()
                            b.on("tensor", smm(jc))
                            b.bar_phase()
                            banddve(jc)
                            b.bar_phase()
                            expph(jc)
                            b.bar_phase()
                            pend = None
                        else:
                            b.on("tensor", smm(jc))
                            if pend is not None:
                                expph(pend)
                            b.bar_phase()
                            pend = jc
                    if pend is not None:
                        expph(pend)
                        b.bar_phase()
                    for hh in range(2):
                        idx = pc * 8 + qtl * 2 + hh
                        b.on("vector", (lambda h, ix=idx: lambda e:
                             e.reduce_sum(den_g[:, ix:ix + 1],
                                          den_p[:, h, :], axis=AX.X))(hh))
                    b.dmat("sync", PT4[:, 0, :, qtl * 128:(qtl + 1) * 128],
                           Pp[:, 0, :])
                    b.bar_phase()
                    b.dmat("sync", PT4[:, 1, :, qtl * 128:(qtl + 1) * 128],
                           Pp[:, 1, :])
                    b.bar_phase()

                def avmm(pcn=pc):
                    def f(e):
                        rv = None
                        for jsub in range(32):
                            st = (jsub == 0)
                            sp = (jsub == 31)
                            rv = e.matmul(ps[0:64, 0:512],
                                          vt[:, jsub, pcn * 128:pcn * 128 + 64],
                                          PT4[:, 0, jsub, :],
                                          start=st, stop=sp,
                                          tile_position=(0, 0))
                            rv = e.matmul(ps[64:128, 0:512],
                                          vt[:, jsub,
                                             pcn * 128 + 64:pcn * 128 + 128],
                                          PT4[:, 1, jsub, :],
                                          start=st, stop=sp,
                                          tile_position=(0, 64))
                        return rv
                    return f
                b.on("tensor", avmm())
                b.bar_phase()
                b.on("scalar", (lambda p_: lambda e: e.activation(
                    osl4[:, p_, :], ps[:, 0:512], AF.Identity))(pc))
                b.bar_phase()

            src_x = xin_hbm if hg == 0 else xout_hbm
            b.on("vector", lambda e: e.reciprocal(den_g[:], den_g[:]))
            b.bar_phase()
            for half in range(2):
                b.dma("sync", xio[:],
                      src_x[:, qg * 4 + half * 2:qg * 4 + half * 2 + 2, :])

                def otr(hf=half):
                    def f(e):
                        rv = None
                        for ql in range(2):
                            for p_ in range(4):
                                rv = e.transpose(
                                    ps[:, ql * 1024 + p_ * 128:
                                       ql * 1024 + (p_ + 1) * 128],
                                    osl4[:, p_, (hf * 2 + ql) * 128:
                                         (hf * 2 + ql + 1) * 128], identf[:])
                        return rv
                    return f
                b.on("tensor", otr())
                b.bar_phase()
                for ql in range(2):
                    qtl = half * 2 + ql
                    for p_ in range(4):
                        for hh in range(2):
                            idx = p_ * 8 + qtl * 2 + hh
                            dcol = hg * 512 + p_ * 128 + hh * 64
                            b.on("vector", (lambda ix=idx, dc=dcol, q_=ql,
                                            pp_=p_, hx=hh: lambda e:
                                 e.scalar_tensor_tensor(
                                     xio[:, q_, dc:dc + 64],
                                     ps[:, q_ * 1024 + pp_ * 128 + hx * 64:
                                        q_ * 1024 + pp_ * 128 + hx * 64 + 64],
                                     den_g[:, ix:ix + 1],
                                     xio[:, q_, dc:dc + 64],
                                     op0=ALU.mult, op1=ALU.add))())
                b.bar_phase()
                b.dma("gpsimd",
                      xout_hbm[:, qg * 4 + half * 2:qg * 4 + half * 2 + 2, :],
                      xio[:])
                b.bar_phase()

        for nm in (f"l0a_deng{hg}", f"l0a_xio{hg}", f"l0a_osl4{hg}",
                   f"l0a_Eb2{hg}", f"l0a_PT4{hg}", f"l0a_Pp{hg}",
                   f"l0a_vt{hg}", f"l0a_ks{hg}", f"l0a_qs{hg}"):
            b.free(nm)

    for nm in ("l0a_recip", "l0a_denp", "l0a_rpc", "l0a_brx", "l0a_wrxb",
               "l0a_wrxf", "l0a_bk8", "l0a_bq", "l0a_ln_eps", "l0a_ln_sq",
               "l0a_ln_bias", "l0a_ln_rstd", "l0a_ln_sd", "l0a_ln_var",
               "l0a_ln_m2", "l0a_ln_mu", "l0a_ln_ss", "l0a_ln_sums"):
        b.free(nm)


def l0attn_host_inputs(inputs):
    Wqkv = np.asarray(inputs["L0_attn_Wqkv"], np.float32)
    bqkv = np.asarray(inputs["L0_attn_bqkv"], np.float32)
    Wr = np.asarray(inputs["L0_attn_Wr"], np.float32)
    br = np.asarray(inputs["L0_attn_br"], np.float32)
    perm = np.array([h * 192 + kind * 64 + c
                     for kind in range(3) for h in range(16) for c in range(64)])
    WqkvT = np.ascontiguousarray(Wqkv[perm, :].T)
    bperm = bqkv[perm]
    idx = np.clip(np.arange(512) - 128, 0, 256)
    Wr_ext = Wr[idx, :].T
    return {
        "WqkvT": WqkvT,
        "bqkv_pp": np.ascontiguousarray(bperm.reshape(24, 128).T),
        "wrx2": np.ascontiguousarray(np.vstack([Wr_ext, Wr_ext])),
        "brx_pp": np.ascontiguousarray(np.tile(br[idx][None, :], (128, 1))),
    }


def build_l0attn_prog():
    nc = bass.Bass("TRN2", target_bir_lowering=False)
    xd = nc.dram_tensor("x", [T, D], F32, kind="ExternalInput").ap()
    rp_h = nc.dram_tensor("rp_hbm", [16, 32, 128, 512], F32, kind="Internal")
    aps = {
        "WqkvT": nc.dram_tensor("WqkvT", [D, 3072], F32, kind="ExternalInput").ap(),
        "bqkv_pp": nc.dram_tensor("bqkv_pp", [128, 24], F32, kind="ExternalInput").ap(),
        "wrx2": nc.dram_tensor("wrx2", [128, 512], F32, kind="ExternalInput").ap(),
        "brx_pp": nc.dram_tensor("brx_pp", [128, 512], F32, kind="ExternalInput").ap(),
        "rp_hbm": rp_h.ap(),
        "rp_hbm_h": rp_h,
        "qkv_hbm": nc.dram_tensor("qkv_hbm", [2, 3, 128, 4, T], BF16,
                                  kind="Internal").ap(),
    }
    identbd = nc.dram_tensor("identb", [128, 128], BF16, kind="ExternalInput").ap()
    identfd = nc.dram_tensor("identf", [128, 128], F32, kind="ExternalInput").ap()
    xod = nc.dram_tensor("xo", [T, D], F32, kind="ExternalOutput").ap()

    b = Bld(nc)
    identb = b.sbuf("identbS", [128, 128], BF16)
    identf = b.sbuf("identfS", [128, 128], F32)
    ps = b.psum("PS", [128, 4096], F32)
    b.dma("sync", identb[:], identbd)
    b.dma("sync", identf[:], identfd)
    b.bar_phase()
    xin_hbm = xd.rearrange("(n p) d -> p n d", p=128)
    xout_hbm = xod.rearrange("(n p) d -> p n d", p=128)
    emit_l0attn(b, aps, xin_hbm, xout_hbm, ps, identb, identf)
    b.finalize()
    return nc



def _shift_load(b, eng, dst, z_flat, n):
    t0 = n * 128
    b.dma(eng, dst[:, 0:512], z_flat[t0:t0 + 128, 0:512])
    if n == 0:
        b.on("vector", lambda e: e.memset(dst[0:1, 512:1024], 0.0))
        b.dma(eng, dst[1:128, 512:1024], z_flat[0:127, 512:1024])
    else:
        b.dma(eng, dst[:, 512:1024], z_flat[t0 - 1:t0 + 127, 512:1024])


def emit_l0ffn_attnhalf(b, aps, li, ps, identb, identf, z_in, z2, dbg=None):
    qs_pp = b.sbuf("lfa_qspp", [128, 1], F32)
    ks_pp = b.sbuf("lfa_kspp", [128, 1], F32)
    perm = b.sbuf("lfa_perm", [128, 128], BF16)
    ind2 = b.sbuf("lfa_ind2", [128, 2], BF16)
    bc2 = b.sbuf("lfa_bc2", [2, 128], BF16)
    mski = b.sbuf("lfa_mski", [128, 384], F32)
    msk0a = b.sbuf("lfa_msk0a", [128, 384], F32)
    msk0b = b.sbuf("lfa_msk0b", [128, 384], F32)
    wq = b.sbuf("lfa_wq", [128, NCH, 3072], BF16)
    wo = b.sbuf("lfa_wo", [128, NCH, 1024], BF16)
    kc = b.sbuf("lfa_kc", [128, NCH, 512], BF16)
    vc = b.sbuf("lfa_vc", [128, 4, 1024], BF16)
    tabs = b.sbuf("lfa_tabs", [128, 4, 256], F32)
    den = b.sbuf("lfa_den", [128, 16], F32)
    nmx = b.sbuf("lfa_nmx", [128, 16], F32)
    rec = b.sbuf("lfa_rec", [128, 16], F32)
    wstg = b.sbuf("lfa_wstg", [128, 3072], F32)

    b.dma("sync", qs_pp[:], aps["qs_pp"][li])
    b.dma("sync", ks_pp[:], aps["ks_pp"][li])
    b.dma("sync", perm[:], aps["perm"])
    b.dma("sync", ind2[:], aps["ind2"])
    b.dma("sync", bc2[:], aps["bc2"])
    b.dma("sync", mski[:], aps["mask_i"])
    b.dma("sync", msk0a[:], aps["mask_0a"])
    b.dma("sync", msk0b[:], aps["mask_0b"])
    b.on("vector", lambda e: e.memset(kc.ap().rearrange("p c v -> p (c v)"), 0.0))
    b.on("vector", lambda e: e.memset(vc.ap().rearrange("p s v -> p (s v)"), 0.0))
    b.bar_phase()
    for dic in range(NCH):
        b.dma("sync", wstg[:], aps["Wqkv"][li].rearrange(
            "(c p) o -> p c o", p=128)[:, dic, :])
        b.bar_phase()
        b.on("vector", (lambda d: lambda e: e.tensor_copy(
            wq[:, d, :], wstg[:]))(dic))
        b.bar_phase()
        b.dma("sync", wstg[:, 0:1024], aps["Wo"][li].rearrange(
            "(c p) o -> p c o", p=128)[:, dic, :])
        b.bar_phase()
        b.on("vector", (lambda d: lambda e: e.tensor_copy(
            wo[:, d, :], wstg[:, 0:1024]))(dic))
        b.bar_phase()
    b.free("lfa_wstg")

    ush = b.sbuf("lfa_ush", [128, 2, D], F32)
    qf = b.sbuf("lfa_qf", [128, NCH, 256], F32)
    kf = b.sbuf("lfa_kf", [128, NCH, 256], F32)
    rq = b.sbuf("lfa_rq", [2, 2048], F32)
    rqb = b.sbuf("lfa_rqb", [2, 2048], BF16)
    rql = b.sbuf("lfa_rql", [2, 2048], BF16)
    rqt = b.sbuf("lfa_rqt", [2, 2048], F32)
    t1 = b.sbuf("lfa_t1", [128, NCH, 256], F32)
    tb = b.sbuf("lfa_tb", [128, NCH, 256], BF16)
    qpb = b.sbuf("lfa_qpb", [128, NCH, 256], BF16)
    sq2 = tb
    vev = qpb
    uf = tb
    P16 = b.sbuf("lfa_P16", [128, 16, 384], BF16)
    PT16 = b.sbuf("lfa_PT16", [128, 48, 128], BF16)
    ao = b.sbuf("lfa_ao", [128, NCH, 256], BF16)
    wsl = b.sbuf("lfa_wsl", [128, NCH, 256], F32)
    ub = wsl.ap()[:, 0:4, :].bitcast(BF16).rearrange(
        "p a t -> p (a t)").rearrange("p (a d) -> p a d", a=2)
    zres = b.sbuf("lfa_zres", [128, 2, D], F32)

    z_flat = z_in
    zo_flat = z2
    ush2 = b.sbuf("lfa_ush2", [128, 2, D], F32)

    def issue_win_loads(wn, ubuf):
        _shift_load(b, "sync", ubuf[:, 0, :], z_flat, wn * 2)
        _shift_load(b, "sync", ubuf[:, 1, :], z_flat, wn * 2 + 1)

    issue_win_loads(0, ush)
    b.bar_phase()
    for w in range(16):
        w0 = w * 256
        ubuf = ush if w % 2 == 0 else ush2
        nbuf = ush2 if w % 2 == 0 else ush
        b.dma("gpsimd", tabs[:],
              aps["rotab"][:, :, w0:w0 + 256].rearrange("k p t -> p k t"))
        b.on("scalar", (lambda ub_=ubuf: lambda e: e.activation(
            ub.rearrange("p a d -> p (a d)"),
            ub_.ap().rearrange("p a d -> p (a d)"), AF.Copy))())
        b.bar_phase()
        for j in range(2):
            pbf = ps[:, j * 512:(j + 1) * 512].bitcast(BF16)
            b.on("tensor", (lambda jj, pb: lambda e: [
                e.transpose(pb[:, c * 128:(c + 1) * 128],
                            ub[:, jj, c * 128:(c + 1) * 128], identb[:])
                for c in range(NCH)][-1])(j, pbf))
        b.bar_phase()
        for j in range(2):
            pbf = ps[:, j * 512:(j + 1) * 512].bitcast(BF16)
            b.on("scalar", (lambda jj, pb: lambda e: e.activation(
                uf[:, :, jj * 128:(jj + 1) * 128],
                pb.rearrange("p (c t) -> p c t", c=NCH), AF.Copy))(j, pbf))
        b.bar_phase()
        def qkmm(r_):
            def f(e):
                rv = None
                base = (r_ % 2) * 2048
                for m8 in range(8):
                    mc = r_ * 8 + m8
                    for dic in range(NCH):
                        rv = e.matmul(ps[:, base + m8 * 256:base + m8 * 256 + 256],
                                      wq[:, dic, mc * 128:(mc + 1) * 128],
                                      uf[:, dic, :],
                                      start=(dic == 0),
                                      stop=(dic == NCH - 1))
                return rv
            return f

        def qkevac(r_):
            base = (r_ % 2) * 2048
            if r_ < 2:
                dst = qf if r_ == 0 else kf
                for m8 in range(8):
                    b.on("scalar", (lambda m, d=dst, ba=base: lambda e:
                         e.activation(d[:, m, :],
                                      ps[:, ba + m * 256:ba + m * 256 + 256],
                                      AF.Copy))(m8))
            else:
                b.on("vector", lambda e: e.tensor_copy(
                    vc.ap().rearrange("p s v -> p (s v)")[:, 0:2048],
                    vc.ap().rearrange("p s v -> p (s v)")[:, 2048:4096]))
                b.on("vector", lambda e: e.tensor_copy(
                    kc[:, :, 0:256], kc[:, :, 256:512]))
                for m8 in range(8):
                    b.on("scalar", (lambda m, ba=base: lambda e: e.activation(
                        vev[:, m, :], ps[:, ba + m * 256:ba + m * 256 + 256],
                        AF.Copy))(m8))

        b.on("tensor", qkmm(0))
        if w < 15:
            issue_win_loads(w + 1, nbuf)
        b.bar_phase()
        b.on("tensor", qkmm(1))
        qkevac(0)
        b.bar_phase()
        b.on("tensor", qkmm(2))
        qkevac(1)
        b.bar_phase()
        qkevac(2)
        b.bar_phase()

        def vtr(e):
            rv = None
            for m8 in range(8):
                for jl in range(2):
                    rv = e.transpose(
                        ps[:, 2048 + jl * 512:2048 + (jl + 1) * 512]
                        .bitcast(BF16)[:, m8 * 128:(m8 + 1) * 128],
                        vev[:, m8, jl * 128:(jl + 1) * 128], identb[:])
            return rv
        b.on("tensor", vtr)
        b.bar_phase()
        for jl in range(2):
            b.on("scalar", (lambda j: lambda e: e.activation(
                vc[:, 2 + j, :],
                ps[:, 2048 + j * 512:2048 + (j + 1) * 512].bitcast(BF16),
                AF.Copy))(jl))
        b.bar_phase()
        b.on("scalar", lambda e: e.activation(
            sq2.ap().rearrange("p c t -> p (c t)"),
            qf.ap().rearrange("p c t -> p (c t)"), AF.Square))
        b.bar_phase()

        def ssq_q(e):
            rv = None
            for pc in range(8):
                rv = e.matmul(ps[0:2, pc * 256:(pc + 1) * 256],
                              ind2[:], sq2[:, pc, :], start=True, stop=True)
            return rv
        b.on("tensor", ssq_q)
        b.bar_phase()
        b.on("scalar", lambda e: e.activation(rqt[0:2, 0:2048],
                                              ps[0:2, 0:2048], AF.Ln))
        b.on("scalar", lambda e: e.activation(
            sq2.ap().rearrange("p c t -> p (c t)"),
            kf.ap().rearrange("p c t -> p (c t)"), AF.Square))
        b.bar_phase()

        def ssq_k(e):
            rv = None
            for pc in range(8):
                rv = e.matmul(ps[0:2, 2048 + pc * 256:2048 + (pc + 1) * 256],
                              ind2[:], sq2[:, pc, :], start=True, stop=True)
            return rv
        b.on("tensor", ssq_k)
        b.on("scalar", lambda e: e.activation(rqt[0:2, 0:2048],
                                              rqt[0:2, 0:2048], AF.Exp,
                                              scale=-0.5))
        b.bar_phase()
        b.on("scalar", lambda e: e.activation(rq[0:2, 0:2048],
                                              ps[0:2, 2048:4096], AF.Ln))
        b.bar_phase()
        b.on("scalar", lambda e: e.activation(rq[0:2, 0:2048],
                                              rq[0:2, 0:2048], AF.Exp,
                                              scale=-0.5))
        b.bar_phase()
        b.on("vector", lambda e: e.tensor_copy(rqb[0:2, 0:2048], rqt[0:2, 0:2048]))
        b.bar_phase()
        b.on("vector", lambda e: e.tensor_tensor(rqt[0:2, 0:2048],
                                                 rqt[0:2, 0:2048],
                                                 rqb[0:2, 0:2048],
                                                 op=ALU.subtract))
        b.bar_phase()
        b.on("vector", lambda e: e.tensor_copy(rql[0:2, 0:2048], rqt[0:2, 0:2048]))
        b.bar_phase()

        for side in range(2):
            tabA = tabs[:, 0 if side == 0 else 2, :]
            tabB = tabs[:, 1 if side == 0 else 3, :]
            spp = qs_pp if side == 0 else ks_pp
            srcf = qf if side == 0 else kf
            if side == 1:
                b.on("vector", lambda e: e.tensor_copy(rqb[0:2, 0:2048],
                                                       rq[0:2, 0:2048]))
                b.bar_phase()
                b.on("vector", lambda e: e.tensor_tensor(rq[0:2, 0:2048],
                                                         rq[0:2, 0:2048],
                                                         rqb[0:2, 0:2048],
                                                         op=ALU.subtract))
                b.bar_phase()
                b.on("vector", lambda e: e.tensor_copy(rql[0:2, 0:2048],
                                                       rq[0:2, 0:2048]))
                b.bar_phase()

            def fbcmm(e):
                rv = None
                for pc in range(8):
                    rv = e.matmul(ps[:, pc * 256:(pc + 1) * 256], bc2[:],
                                  rqb[0:2, pc * 256:(pc + 1) * 256],
                                  start=True, stop=False)
                    rv = e.matmul(ps[:, pc * 256:(pc + 1) * 256], bc2[:],
                                  rql[0:2, pc * 256:(pc + 1) * 256],
                                  start=False, stop=True)
                return rv
            b.on("tensor", fbcmm)
            b.bar_phase()
            if dbg is not None and w == 8 and side == 0:
                b.on("scalar", lambda e: e.activation(
                    dbg["_fbcs"][:], ps[:, 0:256], AF.Copy))
                b.bar_phase()
                b.dma("gpsimd", dbg["fbc"], dbg["_fbcs"][:])
                b.dma("gpsimd", dbg["qspp_d"], qs_pp[:])
                b.bar_phase()
            for pc in range(8):
                b.on("vector", (lambda p_, sf=srcf, sp=spp: lambda e:
                     e.scalar_tensor_tensor(
                         t1[:, p_, :], sf[:, p_, :], sp[:, 0:1],
                         ps[:, p_ * 256:(p_ + 1) * 256],
                         op0=ALU.mult, op1=ALU.mult))(pc))
            b.bar_phase()
            for pc in range(8):
                b.on("vector", (lambda p_, tB=tabB: lambda e: e.tensor_tensor(
                    tb[:, p_, :], t1[:, p_, :], tB, op=ALU.mult))(pc))
            b.bar_phase()
            for pc in range(8):
                b.on("vector", (lambda p_, tA=tabA: lambda e: e.tensor_tensor(
                    t1[:, p_, :], t1[:, p_, :], tA, op=ALU.mult))(pc))
            b.bar_phase()

            if dbg is not None and w == 8 and side == 0:
                b.dma("gpsimd", dbg["tb"], tb[:])
                b.dma("gpsimd", dbg["t1"], t1[:])
                b.dma("gpsimd", dbg["qf"], qf[:])
                b.dma("gpsimd", dbg["rq2"], rqb[:])
                b.bar_phase()

            def permm(e):
                rv = None
                for pc in range(8):
                    rv = e.matmul(ps[:, pc * 256:(pc + 1) * 256], perm[:],
                                  tb[:, pc, :], start=True, stop=True)
                return rv
            b.on("tensor", permm)
            b.bar_phase()
            for pc in range(8):
                if side == 0:
                    dstap = qpb[:, pc, :]
                else:
                    dstap = kc[:, pc, 256:512]
                b.on("vector", (lambda p_, da=dstap: lambda e: e.tensor_tensor(
                    da, t1[:, p_, :], ps[:, p_ * 256:(p_ + 1) * 256],
                    op=ALU.add))(pc))
            b.bar_phase()

        for qtl in range(2):
            msk = mski if w > 0 else (msk0a if qtl == 0 else msk0b)
            jlo = qtl * 128
            for rnd in range(2):
                def smm(r_, qq=qtl, jl=jlo):
                    def f(e):
                        rv = None
                        for p4 in range(4):
                            pc = r_ * 4 + p4
                            for hh in range(2):
                                rv = e.matmul(
                                    ps[:, (p4 * 2 + hh) * 512:
                                       (p4 * 2 + hh) * 512 + 384],
                                    qpb[hh * 64:(hh + 1) * 64, pc,
                                        qq * 128:(qq + 1) * 128],
                                    kc[hh * 64:(hh + 1) * 64, pc,
                                       jl:jl + 384],
                                    start=True, stop=True,
                                    tile_position=(hh * 64, 0))
                        return rv
                    return f
                b.on("tensor", smm(rnd))
                b.bar_phase()
                for p4 in range(4):
                    for hh in range(2):
                        hl = (rnd * 4 + p4) * 2 + hh
                        bk = (p4 * 2 + hh)
                        b.on("vector", (lambda h, k: lambda e: e.reduce_max(
                            nmx[:, h:h + 1],
                            ps[:, k * 512:k * 512 + 384], axis=AX.X,
                            negate=True))(hl, bk))
                for p4 in range(4):
                    for hh in range(2):
                        hl = (rnd * 4 + p4) * 2 + hh
                        bk = (p4 * 2 + hh)
                        b.on("vector", (lambda h, k, m=msk: lambda e:
                             e.scalar_tensor_tensor(
                                 ps[:, k * 512:k * 512 + 384],
                                 ps[:, k * 512:k * 512 + 384],
                                 nmx[:, h:h + 1], m[:],
                                 op0=ALU.add, op1=ALU.add))(hl, bk))
                b.bar_phase()
                for p4 in range(4):
                    for hh in range(2):
                        hl = (rnd * 4 + p4) * 2 + hh
                        bk = (p4 * 2 + hh)
                        b.on("scalar", (lambda h, k: lambda e: e.activation(
                            P16[:, h, :], ps[:, k * 512:k * 512 + 384],
                            AF.Exp, accum_out=den[:, h:h + 1]))(hl, bk))
                b.bar_phase()
            b.on("vector", lambda e: e.reciprocal(rec[:], den[:]))
            b.bar_phase()
            for hl in range(16):
                b.on("vector", (lambda h: lambda e: e.tensor_scalar(
                    P16[:, h, :], P16[:, h, :], rec[:, h:h + 1], None,
                    op0=ALU.mult))(hl))
            b.bar_phase()

            b.dmat("sync", PT16.ap(),
                   P16.ap().rearrange("p h t -> p (h t)"))
            b.bar_phase()

            def avmm(qq=qtl):
                def f(e):
                    rv = None
                    for pc in range(8):
                        for js in range(3):
                            slot = qq + js
                            st = (js == 0)
                            sp = (js == 2)
                            h0 = pc * 2
                            rv = e.matmul(
                                ps[0:64, pc * 512:pc * 512 + 128],
                                vc[:, slot, h0 * 64:h0 * 64 + 64],
                                PT16[:, h0 * 3 + js, :],
                                start=st, stop=sp, tile_position=(0, 0))
                            rv = e.matmul(
                                ps[64:128, pc * 512:pc * 512 + 128],
                                vc[:, slot, h0 * 64 + 64:h0 * 64 + 128],
                                PT16[:, (h0 + 1) * 3 + js, :],
                                start=st, stop=sp, tile_position=(0, 64))
                    return rv
                return f
            b.on("tensor", avmm())
            b.bar_phase()
            for pc in range(8):
                b.on("scalar", (lambda p_, qq=qtl: lambda e: e.activation(
                    ao[:, p_, qq * 128:(qq + 1) * 128],
                    ps[:, p_ * 512:p_ * 512 + 128], AF.Identity))(pc))
            b.bar_phase()

        if dbg is not None and w == 8:
            b.dma("gpsimd", dbg["tabs"], tabs[:])
            b.dma("gpsimd", dbg["qpb"], qpb[:])
            b.dma("gpsimd", dbg["kc"], kc[:])
            b.dma("gpsimd", dbg["vc"], vc[:])
            b.dma("gpsimd", dbg["P16"], P16[:])
            b.dma("gpsimd", dbg["ao"], ao[:])
            b.dma("gpsimd", dbg["uf"], uf[:])
            b.bar_phase()
        def womm(e):
            rv = None
            for oc in range(8):
                for dic in range(NCH):
                    rv = e.matmul(ps[:, oc * 512:oc * 512 + 256],
                                  wo[:, dic, oc * 128:(oc + 1) * 128],
                                  ao[:, dic, :],
                                  start=(dic == 0), stop=(dic == NCH - 1))
            return rv
        b.on("tensor", womm)
        b.bar_phase()
        for oc in range(8):
            b.on("scalar", (lambda o: lambda e: e.activation(
                wsl[:, o, :], ps[:, o * 512:o * 512 + 256], AF.Copy))(oc))
        b.bar_phase()
        b.dma("sync", zres[:, 0, :], z_flat[w0:w0 + 128, :])
        b.dma("sync", zres[:, 1, :], z_flat[w0 + 128:w0 + 256, :])

        def wotr(e):
            rv = None
            for oc in range(8):
                for jl in range(2):
                    rv = e.transpose(
                        ps[:, jl * 2048 + oc * 128:jl * 2048 + (oc + 1) * 128],
                        wsl[:, oc, jl * 128:(jl + 1) * 128], identf[:])
            return rv
        b.on("tensor", wotr)
        b.bar_phase()
        for jl in range(2):
            b.on("vector", (lambda j: lambda e: e.tensor_tensor(
                zres[:, j, :], zres[:, j, :],
                ps[:, j * 2048:j * 2048 + 1024], op=ALU.add))(jl))
        b.bar_phase()
        b.dma("gpsimd", zo_flat[w0:w0 + 128, :], zres[:, 0, :])
        b.dma("gpsimd", zo_flat[w0 + 128:w0 + 256, :], zres[:, 1, :])
        b.bar_phase()

    for nm in ("lfa_ush2", "lfa_zres", "lfa_wsl", "lfa_ao", "lfa_PT16", "lfa_P16",
               "lfa_qpb", "lfa_tb", "lfa_t1", "lfa_rqt", "lfa_rql", "lfa_rqb",
               "lfa_rq", "lfa_kf", "lfa_qf",
               "lfa_ush", "lfa_rec", "lfa_nmx",
               "lfa_den", "lfa_tabs", "lfa_vc", "lfa_kc", "lfa_wo", "lfa_wq",
               "lfa_msk0b", "lfa_msk0a", "lfa_mski", "lfa_bc2", "lfa_ind2",
               "lfa_perm", "lfa_kspp", "lfa_qspp"):
        b.free(nm)


def emit_l0ffn_ffnhalf(b, aps, li, ps, identb, identf, z2, z_out):
    s = mk_lnslab(b, "lff_ln")
    gcol = b.sbuf("lff_gcol", [128, 8], F32)
    bcol = b.sbuf("lff_bcol", [128, 8], BF16)
    b1e = b.sbuf("lff_b1e", [128, 44], F32)
    b1p = b.sbuf("lff_b1p", [128, 44], F32)
    b2c = b.sbuf("lff_b2c", [128, 8], F32)
    w2bf = b.sbuf("lff_w2bf", [128, 22, 1024], BF16)

    b.dma("sync", gcol[:], aps["g_pp"][li])
    b.dma("sync", b1p[:], aps["b1_pp"][li])
    b.dma("sync", b2c[:], aps["b2_pp"][li])
    b.on("vector", lambda e: e.memset(s["eps"][:], 1e-5))
    b.bar_phase()
    bstg = b.sbuf("lff_bstg", [128, 8], F32)
    b.dma("sync", bstg[:], aps["beta_pp"][li])
    b.bar_phase()
    b.on("vector", lambda e: e.tensor_copy(bcol[:], bstg[:]))
    b.bar_phase()

    stg = b.sbuf("lff_stg", [128, 5632], F32)
    tmpb = b.sbuf("lff_tmpb", [128, 5632], BF16)
    w1g = b.sbuf("lff_w1g", [128, 5632], BF16)
    for dic in range(NCH):
        b.dma("sync", stg[:], aps["W1p"][li].rearrange(
            "(c p) o -> p c o", p=128)[:, dic, :])
        b.bar_phase()
        b.on("vector", (lambda d: lambda e: e.tensor_scalar(
            w1g[:], stg[:], gcol[:, d:d + 1], None, op0=ALU.mult))(dic))
        b.on("vector", lambda e: e.tensor_copy(tmpb[:], stg[:]))
        b.bar_phase()

        def b1emm(d):
            def f(e):
                rv = None
                for mc in range(44):
                    rv = e.matmul(ps[:, mc:mc + 1],
                                  tmpb[:, mc * 128:(mc + 1) * 128],
                                  bcol[:, d:d + 1],
                                  start=(d == 0), stop=(d == NCH - 1))
                return rv
            return f
        b.on("tensor", b1emm(dic))
        b.bar_phase()
        b.dma("gpsimd", aps["w1bf_hbm"][:, dic, :], w1g[:])
        b.bar_phase()
    b.on("vector", lambda e: e.tensor_tensor(b1e[:], ps[:, 0:44], b1p[:],
                                             op=ALU.add))
    b.bar_phase()
    b.free("lff_w1g")
    b.free("lff_tmpb")
    b.free("lff_stg")

    stg2 = b.sbuf("lff_stg2", [128, 4, 1024], F32)
    w2_hbm = aps["W2p"][li].rearrange("(c p) o -> p c o", p=128)
    for g in range(6):
        c0 = g * 4
        nc_ = min(4, 22 - c0)
        b.dma("sync", stg2[:, 0:nc_, :], w2_hbm[:, c0:c0 + nc_, :])
        b.bar_phase()
        b.on("vector", (lambda cc0=c0, nn=nc_: lambda e: e.tensor_copy(
            w2bf[:, cc0:cc0 + nn, :].rearrange("p c o -> p (c o)"),
            stg2[:, 0:nn, :].rearrange("p c o -> p (c o)")))())
        b.bar_phase()
    b.free("lff_stg2")

    zsh = b.sbuf("lff_zsh", [128, 4, D], F32)
    xng = b.sbuf("lff_xng", [128, 4, D], BF16)
    xf = b.sbuf("lff_xf", [128, NCH, 512], BF16)
    w1r = b.sbuf("lff_w1r", [128, NCH, 512], BF16)
    w1r2 = b.sbuf("lff_w1r2", [128, NCH, 512], BF16)
    sa = b.sbuf("lff_sa", [128, 4, 512], F32)
    sg = b.sbuf("lff_sg", [128, 4, 512], F32)
    hb = b.sbuf("lff_hb", [128, 22, 512], BF16)
    wsl2 = b.sbuf("lff_wsl2", [128, NCH, 512], F32)
    zres2 = b.sbuf("lff_zres2", [128, 4, D], F32)

    z2_flat = z2
    zo_flat = z_out
    for nt in range(8):
        for j in range(4):
            _shift_load(b, "sync", zsh[:, j, :], z2_flat, nt * 4 + j)
        b.bar_phase()
        emit_ln_group(b, s, zsh, xng, nj=4)
        for j in range(4):
            pbf = ps[:, j * 512:(j + 1) * 512].bitcast(BF16)
            b.on("tensor", (lambda jj, pb: lambda e: [
                e.transpose(pb[:, c * 128:(c + 1) * 128],
                            xng[:, jj, c * 128:(c + 1) * 128], identb[:])
                for c in range(NCH)][-1])(j, pbf))
        b.bar_phase()
        for j in range(4):
            pbf = ps[:, j * 512:(j + 1) * 512].bitcast(BF16)
            b.on("scalar", (lambda jj, pb: lambda e: e.activation(
                xf[:, :, jj * 128:(jj + 1) * 128],
                pb.rearrange("p (c t) -> p c t", c=NCH), AF.Copy))(j, pbf))
        b.bar_phase()
        w1v = aps["w1bf_hbm"].rearrange("p c (m x) -> p c m x", x=128)

        def w1dma(r_):
            buf = w1r if r_ % 2 == 0 else w1r2
            b.dma("sync", buf[:, :, 0:256].rearrange(
                "p c (a x) -> p c a x", a=2), w1v[:, :, r_ * 2:r_ * 2 + 2, :])
            b.dma("sync", buf[:, :, 256:512].rearrange(
                "p c (a x) -> p c a x", a=2),
                w1v[:, :, 22 + r_ * 2:22 + r_ * 2 + 2, :])

        def w1mm(r_):
            buf = w1r if r_ % 2 == 0 else w1r2
            base = (r_ % 2) * 4

            def f(e):
                rv = None
                for k in range(2):
                    for half in range(2):
                        bk = base + k * 2 + half
                        for dic in range(NCH):
                            rv = e.matmul(
                                ps[:, bk * 512:(bk + 1) * 512],
                                buf[:, dic, half * 256 + k * 128:
                                    half * 256 + (k + 1) * 128],
                                xf[:, dic, :],
                                start=(dic == 0), stop=(dic == NCH - 1))
                return rv
            return f

        def w1evac(r_):
            base = (r_ % 2) * 4
            for k in range(2):
                mc = r_ * 2 + k
                sl = (r_ % 2) * 2 + k
                b.on("scalar", (lambda s_, m, bk: lambda e: e.activation(
                    sa[:, s_, :], ps[:, bk * 512:(bk + 1) * 512],
                    AF.Identity, bias=b1e[:, m:m + 1]))(sl, mc, base + k * 2))
                b.on("scalar", (lambda s_, m, bk: lambda e: e.activation(
                    sg[:, s_, :], ps[:, bk * 512:(bk + 1) * 512],
                    AF.Gelu, bias=b1e[:, m + 22:m + 23]))(sl, mc,
                                                          base + k * 2 + 1))

        def w1hmult(r_):
            for k in range(2):
                mc = r_ * 2 + k
                sl = (r_ % 2) * 2 + k
                b.on("vector", (lambda s_, m: lambda e: e.tensor_tensor(
                    hb[:, m, :], sa[:, s_, :], sg[:, s_, :],
                    op=ALU.mult))(sl, mc))

        w1dma(0)
        b.bar_phase()
        for r in range(11):
            b.on("tensor", w1mm(r))
            if r < 10:
                w1dma(r + 1)
            if r >= 1:
                w1evac(r - 1)
            if r >= 2:
                w1hmult(r - 2)
            b.bar_phase()
        w1evac(10)
        w1hmult(9)
        b.bar_phase()
        w1hmult(10)
        b.bar_phase()

        def w2mm(e):
            rv = None
            for oc in range(8):
                for hc in range(22):
                    rv = e.matmul(ps[:, oc * 512:(oc + 1) * 512],
                                  w2bf[:, hc, oc * 128:(oc + 1) * 128],
                                  hb[:, hc, :],
                                  start=(hc == 0), stop=(hc == 21))
            return rv
        b.on("tensor", w2mm)
        b.bar_phase()
        for oc in range(8):
            b.on("scalar", (lambda o: lambda e: e.activation(
                wsl2[:, o, :], ps[:, o * 512:(o + 1) * 512], AF.Identity,
                bias=b2c[:, o:o + 1]))(oc))
        b.bar_phase()
        for j in range(4):
            t0 = (nt * 4 + j) * 128
            b.dma("sync", zres2[:, j, :], z2_flat[t0:t0 + 128, :])

        def w2tr(e):
            rv = None
            for j in range(4):
                for oc in range(8):
                    rv = e.transpose(
                        ps[:, j * 1024 + oc * 128:j * 1024 + (oc + 1) * 128],
                        wsl2[:, oc, j * 128:(j + 1) * 128], identf[:])
            return rv
        b.on("tensor", w2tr)
        b.bar_phase()
        for j in range(4):
            b.on("vector", (lambda jj: lambda e: e.tensor_tensor(
                zres2[:, jj, :], zres2[:, jj, :],
                ps[:, jj * 1024:(jj + 1) * 1024], op=ALU.add))(j))
        b.bar_phase()
        for j in range(4):
            t0 = (nt * 4 + j) * 128
            b.dma("gpsimd", zo_flat[t0:t0 + 128, :], zres2[:, j, :])
        b.bar_phase()

    for nm in ("lff_zres2", "lff_wsl2", "lff_hb", "lff_sg", "lff_sa",
               "lff_w1r2", "lff_w1r", "lff_xf", "lff_xng", "lff_zsh", "lff_bstg",
               "lff_w2bf", "lff_b2c", "lff_b1p", "lff_b1e", "lff_bcol",
               "lff_gcol", "lff_ln_eps", "lff_ln_sq", "lff_ln_bias",
               "lff_ln_rstd", "lff_ln_sd", "lff_ln_var", "lff_ln_m2",
               "lff_ln_mu", "lff_ln_ss", "lff_ln_sums"):
        b.free(nm)


def emit_ln_to_z(b, aps, xin_hbm, z_out, ps, identb):
    s = mk_lnslab(b, "lz_ln")
    xg = b.sbuf("lz_xg", [128, 8, D], F32)
    zng = b.sbuf("lz_zng", [128, 8, D], F32)
    b.on("vector", lambda e: e.memset(s["eps"][:], 1e-5))
    b.bar_phase()
    z_flat = z_out
    for g in range(4):
        b.dma("sync", xg[:], xin_hbm[:, g * 8:(g + 1) * 8, :])
        b.bar_phase()
        emit_ln_group(b, s, xg, zng)
        for j in range(8):
            n = g * 8 + j
            b.dma("gpsimd", z_flat[n * 128:(n + 1) * 128, :], zng[:, j, :])
        b.bar_phase()
    for nm in ("lz_zng", "lz_xg", "lz_ln_eps", "lz_ln_sq", "lz_ln_bias",
               "lz_ln_rstd", "lz_ln_sd", "lz_ln_var", "lz_ln_m2", "lz_ln_mu",
               "lz_ln_ss", "lz_ln_sums"):
        b.free(nm)


def emit_final_add(b, aps, xin_hbm, xout_hbm, z_in):
    xg = b.sbuf("lfz_xg", [128, 8, D], F32)
    zg = b.sbuf("lfz_zg", [128, 8, D], F32)
    z_flat = z_in
    for g in range(4):
        b.dma("sync", xg[:], xin_hbm[:, g * 8:(g + 1) * 8, :])
        b.dma("sync", zg[:],
              z_flat[g * 1024:(g + 1) * 1024, :].rearrange(
                  "(a p) d -> p a d", p=128))
        b.bar_phase()
        b.on("vector", lambda e: e.tensor_tensor(
            xg.ap().rearrange("p a d -> p (a d)"),
            xg.ap().rearrange("p a d -> p (a d)"),
            zg.ap().rearrange("p a d -> p (a d)"), op=ALU.add))
        b.bar_phase()
        b.dma("gpsimd", xout_hbm[:, g * 8:(g + 1) * 8, :], xg[:])
        b.bar_phase()
    b.free("lfz_zg")
    b.free("lfz_xg")


def _l0ffn_aps(nc):
    return {
        "Wqkv": nc.dram_tensor("f_Wqkv", [6, D, 3072], F32, kind="ExternalInput").ap(),
        "Wo": nc.dram_tensor("f_Wo", [6, D, D], F32, kind="ExternalInput").ap(),
        "W1p": nc.dram_tensor("f_W1p", [6, D, 5632], F32, kind="ExternalInput").ap(),
        "W2p": nc.dram_tensor("f_W2p", [6, 2816, D], F32, kind="ExternalInput").ap(),
        "b1_pp": nc.dram_tensor("f_b1pp", [6, 128, 44], F32, kind="ExternalInput").ap(),
        "b2_pp": nc.dram_tensor("f_b2pp", [6, 128, 8], F32, kind="ExternalInput").ap(),
        "g_pp": nc.dram_tensor("f_gpp", [6, 128, 8], F32, kind="ExternalInput").ap(),
        "beta_pp": nc.dram_tensor("f_betapp", [6, 128, 8], F32, kind="ExternalInput").ap(),
        "qs_pp": nc.dram_tensor("f_qspp", [6, 128, 1], F32, kind="ExternalInput").ap(),
        "ks_pp": nc.dram_tensor("f_kspp", [6, 128, 1], F32, kind="ExternalInput").ap(),
        "rotab": nc.dram_tensor("f_rotab", [4, 128, T], F32, kind="ExternalInput").ap(),
        "perm": nc.dram_tensor("f_perm", [128, 128], BF16, kind="ExternalInput").ap(),
        "ind2": nc.dram_tensor("f_ind2", [128, 2], BF16, kind="ExternalInput").ap(),
        "bc2": nc.dram_tensor("f_bc2", [2, 128], BF16, kind="ExternalInput").ap(),
        "mask_i": nc.dram_tensor("f_maski", [128, 384], F32, kind="ExternalInput").ap(),
        "mask_0a": nc.dram_tensor("f_mask0a", [128, 384], F32, kind="ExternalInput").ap(),
        "mask_0b": nc.dram_tensor("f_mask0b", [128, 384], F32, kind="ExternalInput").ap(),
        "w1bf_hbm": nc.dram_tensor("f_w1bf", [128, NCH, 5632], BF16, kind="Internal").ap(),
    }


def l0ffn_host_inputs(inputs):
    Wqkv = np.asarray(inputs["L0_ffn_Wqkv"], np.float32)
    Wo = np.asarray(inputs["L0_ffn_Wo"], np.float32)
    W1 = np.asarray(inputs["L0_ffn_W1"], np.float32)
    b1 = np.asarray(inputs["L0_ffn_b1"], np.float32)
    W2 = np.asarray(inputs["L0_ffn_W2"], np.float32)
    b2 = np.asarray(inputs["L0_ffn_b2"], np.float32)
    g = np.asarray(inputs["L0_ffn_ln_g"], np.float32)
    beta = np.asarray(inputs["L0_ffn_ln_b"], np.float32)
    qsc = np.asarray(inputs["L0_ffn_q_scale"], np.float32)
    ksc = np.asarray(inputs["L0_ffn_k_scale"], np.float32)

    W1p = np.zeros((6, D, 5632), np.float32)
    W1p[:, :, 0:2730] = W1[:, :, 0:2730]
    W1p[:, :, 2816:2816 + 2730] = W1[:, :, 2730:5460]
    b1p = np.zeros((6, 5632), np.float32)
    b1p[:, 0:2730] = b1[:, 0:2730]
    b1p[:, 2816:2816 + 2730] = b1[:, 2730:5460]
    W2p = np.zeros((6, 2816, D), np.float32)
    W2p[:, 0:2730, :] = W2

    DIM_HEAD, WINDOW = 64, 256
    half = np.arange(0, DIM_HEAD, 2, dtype=np.float64)
    inv_freq = 1.0 / 10000.0 ** (half / DIM_HEAD)
    t = np.arange(T, dtype=np.float64)
    freqs = t[:, None] * inv_freq[None, :]
    freqs = np.concatenate([freqs, freqs], -1)
    base_scale = (half + 0.4 * DIM_HEAD) / (1.4 * DIM_HEAD)
    power = (t - T // 2) / (WINDOW // 2)
    scale = base_scale[None, :] ** power[:, None]
    scale = np.concatenate([scale, scale], -1)
    cos, sin = np.cos(freqs), np.sin(freqs)
    Aq = (cos * scale).T
    Bq = (sin * scale).T
    Ak = (cos / scale).T
    Bk = (sin / scale).T
    sig = np.where(np.arange(64) < 32, -1.0, 1.0)
    sgm = (np.arange(64) + 32) % 64
    Bqs = sig[sgm][:, None] * Bq[sgm, :]
    Bks = sig[sgm][:, None] * Bk[sgm, :]
    def dup(a):
        return np.vstack([a, a]).astype(np.float32)
    rotab = np.stack([dup(Aq) * 8.0, dup(Bqs) * 8.0, dup(Ak), dup(Bks)])

    permM = np.zeros((128, 128), np.float32)
    for m in range(128):
        sm = (m // 64) * 64 + ((m % 64) + 32) % 64
        permM[sm, m] = 1.0
    ind2 = np.zeros((128, 2), np.float32)
    ind2[0:64, 0] = 1.0
    ind2[64:128, 1] = 1.0
    bc2 = np.zeros((2, 128), np.float32)
    bc2[0, 0:64] = 1.0
    bc2[1, 64:128] = 1.0

    NEG = np.float32(-3.38e38)
    p = np.arange(128)[:, None]
    jl = np.arange(384)[None, :]
    mi = np.where((jl >= p) & (jl <= p + 256), 0.0, NEG).astype(np.float32)
    m0a = np.where((jl >= 256) & (jl <= 256 + p), 0.0, NEG).astype(np.float32)
    m0b = np.where((jl >= 128) & (jl <= 256 + p), 0.0, NEG).astype(np.float32)

    def pp(a, w):
        return np.ascontiguousarray(a.reshape(6, w, 128).transpose(0, 2, 1))

    return {
        "f_Wqkv": Wqkv, "f_Wo": Wo, "f_W1p": W1p, "f_W2p": W2p,
        "f_b1pp": pp(b1p, 44), "f_b2pp": pp(b2, 8), "f_gpp": pp(g, 8),
        "f_betapp": pp(beta, 8),
        "f_qspp": np.ascontiguousarray(
            np.concatenate([qsc, qsc], 1)[:, :, None]),
        "f_kspp": np.ascontiguousarray(
            np.concatenate([ksc, ksc], 1)[:, :, None]),
        "f_rotab": rotab, "f_perm": bfc(permM), "f_ind2": bfc(ind2),
        "f_bc2": bfc(bc2), "f_maski": mi, "f_mask0a": m0a, "f_mask0b": m0b,
    }


def build_l0ffn_layer_prog(li, first=False, last=False, debug_z2=False, debug_w=False):
    nc = bass.Bass("TRN2", target_bir_lowering=False)
    aps = _l0ffn_aps(nc)
    if first:
        xd = nc.dram_tensor("x", [T, D], F32, kind="ExternalInput").ap()
        zin = nc.dram_tensor("zmid", [T, D], F32, kind="Internal").ap()
    else:
        xd = nc.dram_tensor("x", [T, D], F32, kind="ExternalInput").ap()             if last else None
        zin = nc.dram_tensor("zin", [T, D], F32, kind="ExternalInput").ap()
    z2 = nc.dram_tensor("z2", [T, D], F32,
                        kind="ExternalOutput" if debug_z2 else "Internal").ap()
    if last:
        zout = nc.dram_tensor("zfin", [T, D], F32, kind="Internal").ap()
        xod = nc.dram_tensor("xo", [T, D], F32, kind="ExternalOutput").ap()
    else:
        zout = nc.dram_tensor("zout", [T, D], F32, kind="ExternalOutput").ap()
        xod = None
    identbd = nc.dram_tensor("identb", [128, 128], BF16, kind="ExternalInput").ap()
    identfd = nc.dram_tensor("identf", [128, 128], F32, kind="ExternalInput").ap()

    b = Bld(nc)
    identb = b.sbuf("identbS", [128, 128], BF16)
    identf = b.sbuf("identfS", [128, 128], F32)
    ps = b.psum("PS", [128, 4096], F32)
    b.dma("sync", identb[:], identbd)
    b.dma("sync", identf[:], identfd)
    b.bar_phase()
    if first:
        xin_hbm = xd.rearrange("(n p) d -> p n d", p=128)
        emit_ln_to_z(b, aps, xin_hbm, zin, ps, identb)
    dbg = None
    if debug_w:
        dbg = {
            "qpb": nc.dram_tensor("dbg_qpb", [128, NCH, 256], BF16, kind="ExternalOutput").ap(),
            "kc": nc.dram_tensor("dbg_kc", [128, NCH, 512], BF16, kind="ExternalOutput").ap(),
            "vc": nc.dram_tensor("dbg_vc", [128, 4, 1024], BF16, kind="ExternalOutput").ap(),
            "P16": nc.dram_tensor("dbg_P16", [128, 16, 384], BF16, kind="ExternalOutput").ap(),
            "ao": nc.dram_tensor("dbg_ao", [128, NCH, 256], BF16, kind="ExternalOutput").ap(),
            "uf": nc.dram_tensor("dbg_uf", [128, NCH, 256], BF16, kind="ExternalOutput").ap(),
            "tabs": nc.dram_tensor("dbg_tabs", [128, 4, 256], F32, kind="ExternalOutput").ap(),
            "tb": nc.dram_tensor("dbg_tb", [128, NCH, 256], BF16, kind="ExternalOutput").ap(),
            "t1": nc.dram_tensor("dbg_t1", [128, NCH, 256], F32, kind="ExternalOutput").ap(),
            "qf": nc.dram_tensor("dbg_qf", [128, NCH, 256], F32, kind="ExternalOutput").ap(),
            "rq2": nc.dram_tensor("dbg_rq2", [2, 2048], BF16, kind="ExternalOutput").ap(),
            "fbc": nc.dram_tensor("dbg_fbc", [128, 256], F32, kind="ExternalOutput").ap(),
            "qspp_d": nc.dram_tensor("dbg_qspp", [128, 1], F32, kind="ExternalOutput").ap(),
        }
        dbg["_fbcs"] = b.sbuf("dbg_fbcs", [128, 256], F32)
    emit_l0ffn_attnhalf(b, aps, li, ps, identb, identf, zin, z2, dbg)
    emit_l0ffn_ffnhalf(b, aps, li, ps, identb, identf, z2, zout)
    if last:
        xin_hbm = xd.rearrange("(n p) d -> p n d", p=128)
        xout_hbm = xod.rearrange("(n p) d -> p n d", p=128)
        emit_final_add(b, aps, xin_hbm, xout_hbm, zout)
    b.finalize()
    return nc


def l1attn_host_inputs(inputs):
    Wp = np.asarray(inputs["L1_attn_W_proj"], np.float32)
    Wd = np.asarray(inputs["L1_attn_W_delta"], np.float32)
    bd = np.asarray(inputs["L1_attn_b_delta"], np.float32)
    A1 = np.asarray(inputs["L1_attn_A"], np.float32)[0, 0]
    B1 = np.asarray(inputs["L1_attn_Bp"], np.float32)[0, :, 0]
    Dv = np.asarray(inputs["L1_attn_D"], np.float32)
    return {
        "WpT": np.ascontiguousarray(Wp.T),
        "WdT": np.ascontiguousarray(Wd.T),
        "bdel_pp": np.ascontiguousarray(bd.reshape(8, 128).T),
        "A1_pp": np.ascontiguousarray(A1.reshape(8, 128).T),
        "B1_pp": np.ascontiguousarray(B1.reshape(8, 128).T),
        "Dv": np.broadcast_to(Dv.reshape(1, 1), (128, 1)).copy(),
    }


def common_host_inputs():
    return {
        "identb": bfc(np.eye(128, dtype=np.float32)),
        "identf": np.eye(128, dtype=np.float32),
    }



def build_l1attn_prog(debug=False):
    nc = bass.Bass("TRN2", target_bir_lowering=False)
    xd = nc.dram_tensor("x", [T, D], F32, kind="ExternalInput").ap()
    aps = {
        "WpT": nc.dram_tensor("WpT", [D, 4096], F32, kind="ExternalInput").ap(),
        "WdT": nc.dram_tensor("WdT", [D, D], F32, kind="ExternalInput").ap(),
        "bdel_pp": nc.dram_tensor("bdel_pp", [128, 8], F32, kind="ExternalInput").ap(),
        "A1_pp": nc.dram_tensor("A1_pp", [128, 8], F32, kind="ExternalInput").ap(),
        "B1_pp": nc.dram_tensor("B1_pp", [128, 8], F32, kind="ExternalInput").ap(),
        "Dv": nc.dram_tensor("Dv", [128, 1], F32, kind="ExternalInput").ap(),
    }
    aps["din_hbm"] = nc.dram_tensor("din_hbm", [128, NCH, T], BF16,
                                    kind="ExternalOutput" if debug else "Internal").ap()
    aps["_debug"] = debug
    identbd = nc.dram_tensor("identb", [128, 128], BF16, kind="ExternalInput").ap()
    identfd = nc.dram_tensor("identf", [128, 128], F32, kind="ExternalInput").ap()
    xod = nc.dram_tensor("xo", [T, D], F32, kind="ExternalOutput").ap()

    b = Bld(nc)
    identb = b.sbuf("identbS", [128, 128], BF16)
    identf = b.sbuf("identfS", [128, 128], F32)
    ps = b.psum("PS", [128, 4096], F32)
    b.dma("sync", identb[:], identbd)
    b.dma("sync", identf[:], identfd)
    b.bar_phase()
    if debug:
        aps["xn_dbg"] = nc.dram_tensor("xn_dbg", [128, NCH, T], BF16,
                                       kind="ExternalOutput").ap()
        aps["ucsum_dbg"] = nc.dram_tensor("ucsum_dbg", [128, 16], F32,
                                          kind="ExternalOutput").ap()
    xin_hbm = xd.rearrange("(n p) d -> p n d", p=128)
    xout_hbm = xod.rearrange("(n p) d -> p n d", p=128)
    emit_l1attn(b, aps, xin_hbm, xout_hbm, ps, identb, identf)
    b.finalize()
    return nc


def build_fused_prog():
    nc = bass.Bass("TRN2", target_bir_lowering=False)
    xd = nc.dram_tensor("x", [T, D], F32, kind="ExternalInput").ap()
    xod = nc.dram_tensor("xo", [T, D], F32, kind="ExternalOutput").ap()
    identbd = nc.dram_tensor("identb", [128, 128], BF16, kind="ExternalInput").ap()
    identfd = nc.dram_tensor("identf", [128, 128], F32, kind="ExternalInput").ap()

    x1 = nc.dram_tensor("x1_h", [T, D], F32, kind="Internal").ap()
    x2 = nc.dram_tensor("x2_h", [T, D], F32, kind="Internal").ap()
    x3 = nc.dram_tensor("x3_h", [T, D], F32, kind="Internal").ap()
    zA = nc.dram_tensor("zA_h", [T, D], F32, kind="Internal").ap()
    z2 = nc.dram_tensor("z2_h", [T, D], F32, kind="Internal").ap()

    rp_h = nc.dram_tensor("rp_hbm", [16, 32, 128, 512], F32, kind="Internal")
    aps0 = {
        "WqkvT": nc.dram_tensor("WqkvT", [D, 3072], F32, kind="ExternalInput").ap(),
        "bqkv_pp": nc.dram_tensor("bqkv_pp", [128, 24], F32, kind="ExternalInput").ap(),
        "wrx2": nc.dram_tensor("wrx2", [128, 512], F32, kind="ExternalInput").ap(),
        "brx_pp": nc.dram_tensor("brx_pp", [128, 512], F32, kind="ExternalInput").ap(),
        "rp_hbm": rp_h.ap(),
        "rp_hbm_h": rp_h,
        "qkv_hbm": nc.dram_tensor("qkv_hbm", [2, 3, 128, 4, T], BF16,
                                  kind="Internal").ap(),
    }
    apsf = _l0ffn_aps(nc)
    aps1a = {
        "WpT": nc.dram_tensor("WpT", [D, 4096], F32, kind="ExternalInput").ap(),
        "WdT": nc.dram_tensor("WdT", [D, D], F32, kind="ExternalInput").ap(),
        "bdel_pp": nc.dram_tensor("bdel_pp", [128, 8], F32, kind="ExternalInput").ap(),
        "A1_pp": nc.dram_tensor("A1_pp", [128, 8], F32, kind="ExternalInput").ap(),
        "B1_pp": nc.dram_tensor("B1_pp", [128, 8], F32, kind="ExternalInput").ap(),
        "Dv": nc.dram_tensor("Dv", [128, 1], F32, kind="ExternalInput").ap(),
        "din_hbm": nc.dram_tensor("din_hbm", [128, NCH, T], BF16,
                                  kind="Internal").ap(),
    }
    aps1f = {
        "WgT": nc.dram_tensor("WgT", [D, 8], F32, kind="ExternalInput").ap(),
        "WeT": nc.dram_tensor("WeT", [8, D, D], F32, kind="ExternalInput").ap(),
        "be": nc.dram_tensor("be", [8, D], F32, kind="ExternalInput").ap(),
        "webf_hbm": nc.dram_tensor("webf_hbm", [128, 8, NCH, D], BF16,
                                   kind="Internal").ap(),
    }

    b = Bld(nc)
    identb = b.sbuf("identbS", [128, 128], BF16)
    identf = b.sbuf("identfS", [128, 128], F32)
    ps = b.psum("PS", [128, 4096], F32)
    b.dma("sync", identb[:], identbd)
    b.dma("sync", identf[:], identfd)
    b.bar_phase()

    xin = xd.rearrange("(n p) d -> p n d", p=128)
    x1v = x1.rearrange("(n p) d -> p n d", p=128)
    x2v = x2.rearrange("(n p) d -> p n d", p=128)
    x3v = x3.rearrange("(n p) d -> p n d", p=128)
    xov = xod.rearrange("(n p) d -> p n d", p=128)

    emit_l0attn(b, aps0, xin, x1v, ps, identb, identf)
    emit_ln_to_z(b, apsf, x1v, zA, ps, identb)
    for li in range(6):
        emit_l0ffn_attnhalf(b, apsf, li, ps, identb, identf, zA, z2)
        emit_l0ffn_ffnhalf(b, apsf, li, ps, identb, identf, z2, zA)
    emit_final_add(b, apsf, x1v, x2v, zA)
    emit_l1attn(b, aps1a, x2v, x3v, ps, identb, identf)
    emit_l1ffn(b, aps1f, x3v, xov, ps, identb, identf)
    b.finalize()
    return nc


_PROGS = {}


def _prog(key, fn):
    if key not in _PROGS:
        _PROGS[key] = fn()
    return _PROGS[key]


FUSED = True


def kernel(**inputs):
    if FUSED:
        return kernel_fused(**inputs)
    return kernel_unfused(**inputs)


def kernel_fused(**inputs):
    x = np.ascontiguousarray(np.asarray(inputs["L0_attn_x"], np.float32))
    ncore = 8
    shared = {}
    shared.update(common_host_inputs())
    shared.update(l0attn_host_inputs(inputs))
    shared.update(l0ffn_host_inputs(inputs))
    shared.update(l1attn_host_inputs(inputs))
    shared.update(l1ffn_host_inputs(inputs))
    nc = _prog("fused", build_fused_prog)
    in_maps = []
    for c in range(ncore):
        m = dict(shared)
        m["x"] = x[c]
        in_maps.append(m)
    res = run_bass_kernel_spmd(nc, in_maps, core_ids=list(range(ncore)))
    return np.stack([np.asarray(res.results[c]["xo"]) for c in range(ncore)])


def kernel_unfused(**inputs):
    x = np.ascontiguousarray(np.asarray(inputs["L0_attn_x"], np.float32))
    ncore = 8
    com = common_host_inputs()
    h0a = l0attn_host_inputs(inputs)
    h0f = l0ffn_host_inputs(inputs)
    h1a = l1attn_host_inputs(inputs)
    h1f = l1ffn_host_inputs(inputs)

    def launch(nc, per_core_extra, shared):
        in_maps = []
        for c in range(ncore):
            m = dict(shared)
            m.update(com)
            m.update(per_core_extra[c])
            in_maps.append(m)
        res = run_bass_kernel_spmd(nc, in_maps, core_ids=list(range(ncore)))
        return res.results

    nc = _prog("l0a", build_l0attn_prog)
    r = launch(nc, [{"x": x[c]} for c in range(ncore)], h0a)
    x1 = [np.asarray(r[c]["xo"]) for c in range(ncore)]

    z = None
    for li in range(6):
        first, last = li == 0, li == 5
        nc = _prog(f"l0f{li}", lambda li=li, f=first, l=last:
                   build_l0ffn_layer_prog(li, first=f, last=l))
        if first:
            per = [{"x": x1[c]} for c in range(ncore)]
        elif last:
            per = [{"x": x1[c], "zin": z[c]} for c in range(ncore)]
        else:
            per = [{"zin": z[c]} for c in range(ncore)]
        r = launch(nc, per, h0f)
        if last:
            x2 = [np.asarray(r[c]["xo"]) for c in range(ncore)]
        else:
            z = [np.asarray(r[c]["zout"]) for c in range(ncore)]

    nc = _prog("l1a", build_l1attn_prog)
    r = launch(nc, [{"x": x2[c]} for c in range(ncore)], h1a)
    x3 = [np.asarray(r[c]["xo"]) for c in range(ncore)]

    nc = _prog("l1f", build_l1ffn_prog)
    r = launch(nc, [{"x": x3[c]} for c in range(ncore)], h1f)
    out = np.stack([np.asarray(r[c]["xo"]) for c in range(ncore)])
    return out


# version 06
# speedup vs baseline: 1.0021x; 1.0021x over previous
import contextlib
import numpy as np
import ml_dtypes
import concourse.bass as bass
import concourse.mybir as mybir
from concourse.bass_utils import run_bass_kernel_spmd
from concourse.ap import AP

F32 = mybir.dt.float32
BF16 = mybir.dt.bfloat16
AF = mybir.ActivationFunctionType
ALU = mybir.AluOpType
AX = mybir.AxisListType

T = 4096
D = 1024
NT = 32
NCH = 8
ENGINES = ("sync", "scalar", "vector", "tensor", "gpsimd")


def bfc(x):
    return np.ascontiguousarray(x).astype(ml_dtypes.bfloat16)


class Bld:

    def __init__(self, nc):
        self.nc = nc
        self.stack = contextlib.ExitStack()
        self.bar = self.sem("bar")
        self.dma_sem = {e: self.sem("dsem_" + e) for e in ENGINES}
        self.dma_cnt = {e: 0 for e in ENGINES}
        self.phases = []
        self.open_bufs = {}
        self._new()
        self.dum = {e: self.sbuf("_dum_" + e, [1, 2], F32)
                    for e in ("vector", "scalar", "gpsimd")}

    def sem(self, name):
        return self.stack.enter_context(self.nc.semaphore(name))

    _namectr = [0]

    def sbuf(self, name, shape, dtype):
        assert name not in self.open_bufs, name
        Bld._namectr[0] += 1
        cm = self.nc.sbuf_tensor(f"{name}_N{Bld._namectr[0]}", list(shape), dtype)
        t = cm.__enter__()
        self.open_bufs[name] = cm
        return t

    def free(self, name):
        cm = self.open_bufs.pop(name)
        cm.__exit__(None, None, None)

    def psum(self, name, shape, dtype):
        cm = self.nc.psum_tensor(name, list(shape), dtype)
        t = cm.__enter__()
        self.open_bufs[name] = cm
        return t

    def _new(self):
        self.cur = {e: [] for e in ENGINES}
        self.cur["_dmawm"] = {}
        self.phases.append(self.cur)

    def bar_phase(self):
        self._new()

    def on(self, eng, fn):
        self.cur[eng].append(fn)

    def dma(self, eng, out, in_, **kw):
        semh = self.dma_sem[eng]
        self.dma_cnt[eng] += 16
        tgt = self.dma_cnt[eng]

        def f(e):
            e.dma_start(out=out, in_=in_, **kw).then_inc(semh, 16)
        self.on(eng, f)
        self.cur["_dmawm"][eng] = tgt

    def dmat(self, eng, out, in_):
        semh = self.dma_sem[eng]
        self.dma_cnt[eng] += 16
        tgt = self.dma_cnt[eng]

        def f(e):
            e.dma_start_transpose(out, in_).then_inc(semh, 16)
        self.on(eng, f)
        self.cur["_dmawm"][eng] = tgt

    def finalize(self):
        dum = self.dum
        block = self.stack.enter_context(self.nc.Block())
        bar = self.bar
        dma_sem = self.dma_sem
        phases = self.phases

        def mk(eng_name):
            def body(eng):
                for pi, ph in enumerate(phases):
                    last = None
                    for fn in ph[eng_name]:
                        r = fn(eng)
                        if r is not None:
                            last = r
                    wm = ph["_dmawm"].get(eng_name)
                    if wm is not None:
                        eng.wait_ge(dma_sem[eng_name], wm)
                        last = None
                    if ph[eng_name] and eng_name == "vector":
                        eng.memset(dum[eng_name][:], 0.0).then_inc(bar, 1)
                    elif ph[eng_name] and eng_name == "gpsimd":
                        eng.memset(dum[eng_name][:], 0.0).then_inc(bar, 1)
                    elif ph[eng_name] and eng_name == "scalar":
                        eng.copy(dum[eng_name][:], dum[eng_name][:]).then_inc(bar, 1)
                    elif eng_name == "tensor" and last is not None:
                        last.then_inc(bar, 1)
                    else:
                        eng.nop().then_inc(bar, 1)
                    eng.wait_ge(bar, len(ENGINES) * (pi + 1))
            return body

        for eng_name in ENGINES:
            getattr(block, eng_name)(mk(eng_name))
        return len(phases)



def mk_lnslab(b, tag):
    return {
        "sums": b.sbuf(tag + "_sums", [128, 8], F32),
        "ss": b.sbuf(tag + "_ss", [128, 8], F32),
        "mu": b.sbuf(tag + "_mu", [128, 8], F32),
        "m2": b.sbuf(tag + "_m2", [128, 8], F32),
        "var": b.sbuf(tag + "_var", [128, 8], F32),
        "sd": b.sbuf(tag + "_sd", [128, 8], F32),
        "rstd": b.sbuf(tag + "_rstd", [128, 8], F32),
        "bias": b.sbuf(tag + "_bias", [128, 8], F32),
        "sq": b.sbuf(tag + "_sq", [128, D], F32),
        "eps": b.sbuf(tag + "_eps", [128, 1], F32),
    }


def emit_ln_group(b, s, xg, xng, nj=8, eps=1e-5):
    for j in range(nj):
        b.on("vector", (lambda jj: lambda e: e.reduce_sum(
            s["sums"][:, jj:jj + 1], xg[:, jj, :], axis=AX.X))(j))
        b.on("vector", (lambda jj: lambda e: e.tensor_tensor(
            s["sq"][:], xg[:, jj, :], xg[:, jj, :], op=ALU.mult))(j))
        b.on("vector", (lambda jj: lambda e: e.reduce_sum(
            s["ss"][:, jj:jj + 1], s["sq"][:], axis=AX.X))(j))
    b.on("vector", lambda e: e.tensor_scalar(s["mu"][:, :nj], s["sums"][:, :nj],
                                             1.0 / D, 0.0, op0=ALU.mult,
                                             op1=ALU.add))
    b.bar_phase()
    b.on("vector", lambda e: e.tensor_tensor(s["m2"][:, :nj], s["mu"][:, :nj],
                                             s["mu"][:, :nj], op=ALU.mult))
    b.bar_phase()
    b.on("vector", lambda e: e.scalar_tensor_tensor(
        s["var"][:, :nj], s["ss"][:, :nj], 1.0 / D, s["m2"][:, :nj],
        op0=ALU.mult, op1=ALU.subtract))
    b.bar_phase()
    b.on("scalar", lambda e: e.activation(s["sd"][:, :nj], s["var"][:, :nj],
                                          AF.Sqrt, bias=s["eps"][:]))
    b.bar_phase()
    b.on("vector", lambda e: e.reciprocal(s["rstd"][:, :nj], s["sd"][:, :nj]))
    b.bar_phase()
    b.on("vector", lambda e: e.scalar_tensor_tensor(
        s["bias"][:, :nj], s["mu"][:, :nj], -1.0, s["rstd"][:, :nj],
        op0=ALU.mult, op1=ALU.mult))
    b.bar_phase()
    for j in range(nj):
        b.on("vector", (lambda jj: lambda e: e.tensor_scalar(
            xng[:, jj, :], xg[:, jj, :], s["rstd"][:, jj:jj + 1],
            s["bias"][:, jj:jj + 1], op0=ALU.mult, op1=ALU.add))(j))
    b.bar_phase()


def emit_tok2feat(b, ps, identb, xng, feat_slab, n0, nj=8):
    for j in range(nj):
        pbf = ps[:, j * 512:(j + 1) * 512].bitcast(BF16)

        def mkt(jj, pb):
            def f(e):
                r = None
                for c in range(NCH):
                    r = e.transpose(pb[:, c * 128:(c + 1) * 128],
                                    xng[:, jj, c * 128:(c + 1) * 128], identb[:])
                return r
            return f
        b.on("tensor", mkt(j, pbf))
    b.bar_phase()
    for j in range(nj):
        pbf = ps[:, j * 512:(j + 1) * 512].bitcast(BF16)
        n = n0 + j

        def mke(nn, pb):
            def f(e):
                e.activation(
                    feat_slab[:, :, nn * 128:(nn + 1) * 128],
                    pb.rearrange("p (c t) -> p c t", c=NCH), AF.Copy)
            return f
        b.on("scalar", mke(n, pbf))
    b.bar_phase()


def emit_wload(b, dst_bf, src_hbm_view, stage, nchunks=NCH, scale_pp=None):
    wcols = stage.ap().shape[1]
    for c in range(nchunks):
        b.dma("sync", stage[:], src_hbm_view[:, c, :])
        b.bar_phase()
        if scale_pp is None:
            b.on("vector", (lambda cc: lambda e: e.tensor_copy(
                dst_bf[:, cc, :], stage[:]))(c))
        else:
            b.on("vector", (lambda cc: lambda e: e.tensor_scalar(
                dst_bf[:, cc, :], stage[:], scale_pp[:, cc:cc + 1], None,
                op0=ALU.mult))(c))
        b.bar_phase()



def emit_l1attn(b, aps, xin_hbm, xout_hbm, ps, identb, identf):
    dstg = b.sbuf("l1a_dstg", [128, NCH, 512], BF16)
    bdel = b.sbuf("l1a_bdel", [128, 8], F32)
    a1 = b.sbuf("l1a_a1", [128, 8], F32)
    b1 = b.sbuf("l1a_b1", [128, 8], F32)
    dv = b.sbuf("l1a_dv", [128, 1], F32)
    uc_p = b.sbuf("l1a_ucp", [128, 8, 8], F32)
    cs_p = b.sbuf("l1a_csp", [128, 8, 8], F32)
    uc = b.sbuf("l1a_uc", [128, 8], F32)
    csum = b.sbuf("l1a_csum", [128, 8], F32)
    b1uc = b.sbuf("l1a_b1uc", [128, 8], F32)
    carry = b.sbuf("l1a_carry", [128, 8], F32)
    ones512 = b.sbuf("l1a_ones", [128, 512], F32)
    s = mk_lnslab(b, "l1a_ln")
    xn = b.sbuf("l1a_xn", [128, NCH, T], BF16)

    b.dma("sync", bdel[:], aps["bdel_pp"])
    b.dma("sync", a1[:], aps["A1_pp"])
    b.dma("sync", b1[:], aps["B1_pp"])
    b.dma("sync", dv[:], aps["Dv"])
    b.on("vector", lambda e: e.memset(ones512[:], 1.0))
    b.on("vector", lambda e: e.memset(carry[:], 0.0))
    b.on("vector", lambda e: e.memset(s["eps"][:], 1e-5))
    b.bar_phase()

    xg = b.sbuf("l1a_xg", [128, 8, D], F32)
    xng = b.sbuf("l1a_xng", [128, 8, D], BF16)
    for g in range(4):
        b.dma("sync", xg[:], xin_hbm[:, g * 8:(g + 1) * 8, :])
        b.bar_phase()
        emit_ln_group(b, s, xg, xng)
        emit_tok2feat(b, ps, identb, xng, xn, g * 8)
    if aps.get("_debug"):
        b.dma("gpsimd", aps["xn_dbg"], xn[:])
        b.bar_phase()
    b.free("l1a_xng")
    b.free("l1a_xg")

    wstage = b.sbuf("l1a_wstage", [128, 1024], F32)
    wa = b.sbuf("l1a_wa", [128, NCH, 2048], BF16)
    wr = b.sbuf("l1a_wr", [128, NCH, 1024], BF16)
    csc = b.sbuf("l1a_csc", [128, 512], F32)
    xnf = b.sbuf("l1a_xnf", [128, 512], F32)
    rsl = b.sbuf("l1a_rsl", [128, 8, 512], F32)
    xio = b.sbuf("l1a_xio", [128, 4, D], F32)
    wp_hbm = aps["WpT"].rearrange("(c p) o -> p c o", p=128)
    for c in range(NCH):
        for (dst, lo) in ((wa[:, c, 0:1024], 0), (wa[:, c, 1024:2048], 2048),
                          (wr[:, c, :], 3072)):
            b.dma("sync", wstage[:], wp_hbm[:, c, lo:lo + 1024])
            b.bar_phase()
            b.on("vector", (lambda d: lambda e: e.tensor_copy(d, wstage[:]))(dst))
            b.bar_phase()

    def proj_mms(wsl, ntn):
        def f(e):
            r = None
            for mc in range(8):
                for dic in range(NCH):
                    r = e.matmul(ps[:, mc * 512:(mc + 1) * 512],
                                 wsl(mc, dic),
                                 xn[:, dic, ntn * 512:(ntn + 1) * 512],
                                 start=(dic == 0), stop=(dic == NCH - 1))
            return r
        return f

    for nt in range(8):
        b.on("tensor", proj_mms(
            lambda mc, dic: wa[:, dic, mc * 128:(mc + 1) * 128], nt))
        b.bar_phase()
        for mc in range(8):
            b.on("scalar", (lambda m: lambda e: e.activation(
                dstg[:, m, :], ps[:, m * 512:(m + 1) * 512], AF.Copy))(mc))
        b.bar_phase()
        b.dma("gpsimd", aps["din_hbm"][:, :, nt * 512:(nt + 1) * 512], dstg[:])
        b.bar_phase()
        b.on("tensor", proj_mms(
            lambda mc, dic: wa[:, dic, 1024 + mc * 128:1024 + (mc + 1) * 128], nt))
        b.bar_phase()
        for mc in range(8):
            b.on("vector", (lambda m: lambda e: e.tensor_copy(
                csc[:], ps[:, m * 512:(m + 1) * 512]))(mc))
            b.on("vector", (lambda m, ntn: lambda e: e.tensor_copy(
                xnf[:], xn[:, m, ntn * 512:(ntn + 1) * 512]))(mc, nt))
            b.bar_phase()
            b.on("vector", (lambda m, ntn: lambda e: e.tensor_tensor(
                s["sq"][:, 0:512], csc[:], xnf[:], op=ALU.mult))(mc, nt))
            b.on("vector", (lambda m, ntn: lambda e: e.reduce_sum(
                uc_p[:, m, ntn:ntn + 1], s["sq"][:, 0:512], axis=AX.X))(mc, nt))
            b.on("vector", (lambda m, ntn: lambda e: e.reduce_sum(
                cs_p[:, m, ntn:ntn + 1], csc[:], axis=AX.X))(mc, nt))
            b.bar_phase()
        b.on("tensor", proj_mms(
            lambda mc, dic: wr[:, dic, mc * 128:(mc + 1) * 128], nt))
        b.bar_phase()
        for mc in range(8):
            b.on("vector", (lambda m: lambda e: e.tensor_scalar(
                rsl[:, m, :], ps[:, m * 512:(m + 1) * 512], dv[:, 0:1], None,
                op0=ALU.mult))(mc))
        b.bar_phase()
        b.dma("sync", xio[:], xin_hbm[:, nt * 4:(nt + 1) * 4, :])

        def tr_all(e):
            r = None
            for st in range(4):
                for oc in range(8):
                    r = e.transpose(
                        ps[:, st * 1024 + oc * 128: st * 1024 + (oc + 1) * 128],
                        rsl[:, oc, st * 128:(st + 1) * 128], identf[:])
            return r
        b.on("tensor", tr_all)
        b.bar_phase()
        for st in range(4):
            b.on("vector", (lambda ss: lambda e: e.tensor_tensor(
                xio[:, ss, :], xio[:, ss, :], ps[:, ss * 1024:(ss + 1) * 1024],
                op=ALU.add))(st))
        b.bar_phase()
        b.dma("gpsimd", xout_hbm[:, nt * 4:(nt + 1) * 4, :], xio[:])
        b.bar_phase()

    for c in range(NCH):
        b.on("vector", (lambda cc: lambda e: e.reduce_sum(
            uc[:, cc:cc + 1], uc_p[:, cc, :], axis=AX.X))(c))
        b.on("vector", (lambda cc: lambda e: e.reduce_sum(
            csum[:, cc:cc + 1], cs_p[:, cc, :], axis=AX.X))(c))
    b.bar_phase()
    b.on("vector", lambda e: e.tensor_tensor(b1uc[:], b1[:], uc[:], op=ALU.mult))
    b.bar_phase()
    if aps.get("_debug"):
        b.dma("gpsimd", aps["ucsum_dbg"][:, 0:8], uc[:])
        b.dma("gpsimd", aps["ucsum_dbg"][:, 8:16], csum[:])
        b.bar_phase()

    for nm in ("l1a_xio", "l1a_rsl", "l1a_xnf", "l1a_csc", "l1a_wr", "l1a_wa",
               "l1a_wstage", "l1a_xn"):
        b.free(nm)

    wstage2 = b.sbuf("l1a_wstage2", [128, 1024], F32)
    wd = b.sbuf("l1a_wd", [128, NCH, 1024], BF16)
    spA = b.sbuf("l1a_spA", [128, 8, 512], F32)
    spB = b.sbuf("l1a_spB", [128, 8, 512], F32)
    dlt = b.sbuf("l1a_dlt", [128, 8, 512], F32)
    ysl = b.sbuf("l1a_ysl", [128, 8, 512], F32)
    xio2 = b.sbuf("l1a_xio2", [128, 4, D], F32)
    wd_hbm = aps["WdT"].rearrange("(c p) o -> p c o", p=128)
    for c in range(NCH):
        b.dma("sync", wstage2[:], wd_hbm[:, c, :])
        b.bar_phase()
        b.on("vector", (lambda cc: lambda e: e.tensor_copy(
            wd[:, cc, :], wstage2[:]))(c))
        b.bar_phase()

    for nt in range(8):
        b.dma("sync", dstg[:], aps["din_hbm"][:, :, nt * 512:(nt + 1) * 512])
        b.bar_phase()

        def mmd(e):
            r = None
            for oc in range(8):
                for dic in range(NCH):
                    r = e.matmul(ps[:, oc * 512:(oc + 1) * 512],
                                 wd[:, dic, oc * 128:(oc + 1) * 128],
                                 dstg[:, dic, :],
                                 start=(dic == 0), stop=(dic == NCH - 1))
            return r
        b.on("tensor", mmd)
        b.bar_phase()
        for oc in range(8):
            b.on("scalar", (lambda o: lambda e: e.activation(
                spA[:, o, :], ps[:, o * 512:(o + 1) * 512], AF.Abs,
                bias=bdel[:, o:o + 1]))(oc))
        b.bar_phase()
        b.on("scalar", lambda e: e.activation(
            spB[:].rearrange("p c t -> p (c t)"),
            spA[:].rearrange("p c t -> p (c t)"), AF.Exp, scale=-1.0))
        b.bar_phase()
        b.on("vector", lambda e: e.tensor_scalar(
            spB[:].rearrange("p c t -> p (c t)"),
            spB[:].rearrange("p c t -> p (c t)"), 1.0, 1.0,
            op0=ALU.mult, op1=ALU.add))
        b.bar_phase()
        b.on("scalar", lambda e: e.activation(
            spA[:].rearrange("p c t -> p (c t)"),
            spB[:].rearrange("p c t -> p (c t)"), AF.Ln))
        b.bar_phase()
        for oc in range(8):
            b.on("scalar", (lambda o: lambda e: e.activation(
                spB[:, o, :], ps[:, o * 512:(o + 1) * 512], AF.Relu,
                bias=bdel[:, o:o + 1]))(oc))
        b.bar_phase()
        b.on("vector", lambda e: e.tensor_tensor(
            dlt[:].rearrange("p c t -> p (c t)"),
            spA[:].rearrange("p c t -> p (c t)"),
            spB[:].rearrange("p c t -> p (c t)"), op=ALU.add))
        b.bar_phase()
        for oc in range(8):
            b.on("scalar", (lambda o: lambda e: e.activation(
                spA[:, o, :], dlt[:, o, :], AF.Exp, scale=a1[:, o:o + 1]))(oc))
        b.bar_phase()
        for oc in range(8):
            b.on("vector", (lambda o: lambda e: e.tensor_scalar(
                ysl[:, o, :], spA[:, o, :], b1uc[:, o:o + 1], None,
                op0=ALU.mult))(oc))
            b.on("vector", (lambda o: lambda e: e.tensor_scalar(
                spB[:, o, :], dlt[:, o, :], b1[:, o:o + 1], None,
                op0=ALU.mult))(oc))
        b.bar_phase()
        for oc in range(8):
            b.on("vector", (lambda o: lambda e: e.tensor_tensor_scan(
                spA[:, o, :], ones512[:], spB[:, o, :], carry[:, o:o + 1],
                op0=ALU.mult, op1=ALU.add))(oc))
        b.bar_phase()
        for oc in range(8):
            b.on("vector", (lambda o: lambda e: e.tensor_copy(
                carry[:, o:o + 1], spA[:, o, 511:512]))(oc))
        b.bar_phase()
        for oc in range(8):
            b.on("vector", (lambda o: lambda e: e.scalar_tensor_tensor(
                ysl[:, o, :], spA[:, o, :], csum[:, o:o + 1], ysl[:, o, :],
                op0=ALU.mult, op1=ALU.add))(oc))
        b.bar_phase()
        b.dma("sync", xio2[:], xout_hbm[:, nt * 4:(nt + 1) * 4, :])

        def tr_all2(e):
            r = None
            for st in range(4):
                for oc in range(8):
                    r = e.transpose(
                        ps[:, st * 1024 + oc * 128: st * 1024 + (oc + 1) * 128],
                        ysl[:, oc, st * 128:(st + 1) * 128], identf[:])
            return r
        b.on("tensor", tr_all2)
        b.bar_phase()
        for st in range(4):
            b.on("vector", (lambda ss: lambda e: e.tensor_tensor(
                xio2[:, ss, :], xio2[:, ss, :], ps[:, ss * 1024:(ss + 1) * 1024],
                op=ALU.add))(st))
        b.bar_phase()
        b.dma("gpsimd", xout_hbm[:, nt * 4:(nt + 1) * 4, :], xio2[:])
        b.bar_phase()

    for nm in ("l1a_xio2", "l1a_ysl", "l1a_dlt", "l1a_spB", "l1a_spA", "l1a_wd",
               "l1a_wstage2"):
        b.free(nm)
    for nm in ("l1a_ln_eps", "l1a_ln_sq", "l1a_ln_bias", "l1a_ln_rstd",
               "l1a_ln_sd", "l1a_ln_var", "l1a_ln_m2", "l1a_ln_mu", "l1a_ln_ss",
               "l1a_ln_sums", "l1a_ones", "l1a_carry", "l1a_b1uc", "l1a_csum",
               "l1a_uc", "l1a_csp", "l1a_ucp", "l1a_dv", "l1a_b1", "l1a_a1",
               "l1a_bdel", "l1a_dstg"):
        b.free(nm)



def emit_l1ffn(b, aps, xin_hbm, xout_hbm, ps, identb, identf, stop_after=None):
    logT = b.sbuf("l1f_logT", [8, T], F32)
    cwT = b.sbuf("l1f_cwT", [8, T], F32)
    cwTb = b.sbuf("l1f_cwTb", [8, T], BF16)
    beb = b.sbuf("l1f_beb", [8, D], BF16)
    bef = b.sbuf("l1f_bef", [8, D], F32)
    wg_hi = b.sbuf("l1f_wghi", [128, NCH, 8], BF16)
    wg_lo = b.sbuf("l1f_wglo", [128, NCH, 8], BF16)
    wg_f = b.sbuf("l1f_wgf", [128, NCH, 8], F32)
    wg_t = b.sbuf("l1f_wgt", [128, NCH, 8], F32)
    onesb = b.sbuf("l1f_onesb", [1, 128], BF16)
    s = mk_lnslab(b, "l1f_ln")
    m1 = b.sbuf("l1f_m1", [128, 8], F32)
    m2 = b.sbuf("l1f_m2", [128, 8], F32)
    sg = b.sbuf("l1f_sg", [128, 8], F32)
    w1 = b.sbuf("l1f_w1", [128, 8], F32)
    w2 = b.sbuf("l1f_w2", [128, 8], F32)
    msk1 = b.sbuf("l1f_msk1", [128, 8, 8], F32)
    msk2 = b.sbuf("l1f_msk2", [128, 8, 8], F32)
    lgt = b.sbuf("l1f_lgt", [128, 8, 8], F32)
    cwg = b.sbuf("l1f_cwg", [128, 8, 8], F32)
    xn = b.sbuf("l1f_xn", [128, NCH, T], BF16)

    b.dma("sync", wg_f[:], aps["WgT"].rearrange("(c p) e -> p c e", p=128))
    b.dma("sync", bef[:], aps["be"])
    b.on("vector", lambda e: e.memset(onesb[:], 1.0))
    b.on("vector", lambda e: e.memset(s["eps"][:], 1e-5))
    b.bar_phase()
    b.on("vector", lambda e: e.tensor_copy(
        wg_hi.ap().rearrange("p c e -> p (c e)"),
        wg_f.ap().rearrange("p c e -> p (c e)")))
    b.on("vector", lambda e: e.tensor_copy(beb[:], bef[:]))
    b.bar_phase()
    b.on("vector", lambda e: e.tensor_copy(
        wg_t.ap().rearrange("p c e -> p (c e)"),
        wg_hi.ap().rearrange("p c e -> p (c e)")))
    b.bar_phase()
    b.on("vector", lambda e: e.tensor_tensor(
        wg_t.ap().rearrange("p c e -> p (c e)"),
        wg_f.ap().rearrange("p c e -> p (c e)"),
        wg_t.ap().rearrange("p c e -> p (c e)"), op=ALU.subtract))
    b.bar_phase()
    b.on("vector", lambda e: e.tensor_copy(
        wg_lo.ap().rearrange("p c e -> p (c e)"),
        wg_t.ap().rearrange("p c e -> p (c e)")))
    b.bar_phase()
    if stop_after == "prep":
        return

    xg = b.sbuf("l1f_xg", [128, 8, D], F32)
    xng = b.sbuf("l1f_xng", [128, 8, D], BF16)
    xlg = b.sbuf("l1f_xlg", [128, 8, D], BF16)
    xlo = b.sbuf("l1f_xlo", [128, NCH, 1024], BF16)
    for g in range(4):
        b.dma("sync", xg[:], xin_hbm[:, g * 8:(g + 1) * 8, :])
        b.bar_phase()
        emit_ln_group(b, s, xg, xng)
        for j in range(8):
            b.on("vector", (lambda jj: lambda e: e.tensor_scalar(
                s["sq"][:], xg[:, jj, :], s["rstd"][:, jj:jj + 1],
                s["bias"][:, jj:jj + 1], op0=ALU.mult, op1=ALU.add))(jj if False else j))
            b.on("vector", (lambda jj: lambda e: e.tensor_tensor(
                xlg[:, jj, :], s["sq"][:], xng[:, jj, :], op=ALU.subtract))(j))
        b.bar_phase()
        emit_tok2feat(b, ps, identb, xng, xn, g * 8)
        for j in range(8):
            pbf = ps[:, j * 512:(j + 1) * 512].bitcast(BF16)
            b.on("tensor", (lambda jj, pb: lambda e: [
                e.transpose(pb[:, c * 128:(c + 1) * 128],
                            xlg[:, jj, c * 128:(c + 1) * 128], identb[:])
                for c in range(NCH)][-1])(j, pbf))
        b.bar_phase()
        for j in range(8):
            pbf = ps[:, j * 512:(j + 1) * 512].bitcast(BF16)
            b.on("scalar", (lambda jj, pb: lambda e: e.activation(
                xlo[:, :, jj * 128:(jj + 1) * 128],
                pb.rearrange("p (c t) -> p c t", c=NCH), AF.Copy))(j, pbf))
        b.bar_phase()
        def gmm(gg):
            def f(e):
                r = None
                for half in range(2):
                    pb = ps[0:8, half * 512:(half + 1) * 512]
                    for dic in range(NCH):
                        xs_hi = xn[:, dic, gg * 1024 + half * 512:
                                   gg * 1024 + (half + 1) * 512]
                        xs_lo = xlo[:, dic, half * 512:(half + 1) * 512]
                        e.matmul(pb, wg_hi[:, dic, :], xs_hi,
                                 start=(dic == 0), stop=False)
                        e.matmul(pb, wg_lo[:, dic, :], xs_hi,
                                 start=False, stop=False)
                        r = e.matmul(pb, wg_hi[:, dic, :], xs_lo,
                                     start=False, stop=(dic == NCH - 1))
                return r
            return f
        if stop_after != "nogmm":
            b.on("tensor", gmm(g))
            b.bar_phase()
            if stop_after != "noevac":
                b.on("scalar", (lambda gg: lambda e: e.activation(
                    logT[:, gg * 1024:gg * 1024 + 512], ps[0:8, 0:512],
                    AF.Copy))(g))
                b.on("scalar", (lambda gg: lambda e: e.activation(
                    logT[:, gg * 1024 + 512:(gg + 1) * 1024],
                    ps[0:8, 512:1024], AF.Copy))(g))
                b.bar_phase()
    b.free("l1f_xlo")
    b.free("l1f_xlg")
    b.free("l1f_xng")
    b.free("l1f_xg")
    if stop_after in ("p1", "nogmm", "noevac"):
        return

    for g in range(4):
        def ltr(gg):
            def f(e):
                r = None
                for j in range(8):
                    n = gg * 8 + j
                    r = e.transpose(ps[:, j * 512:j * 512 + 8],
                                    logT[:, n * 128:(n + 1) * 128],
                                    identf[0:8, 0:8])
                return r
            return f
        b.on("tensor", ltr(g))
        b.bar_phase()
        for j in range(8):
            b.on("scalar", (lambda jj: lambda e: e.activation(
                lgt[:, jj, :], ps[:, jj * 512:jj * 512 + 8], AF.Copy))(j))
        b.bar_phase()
        for j in range(8):
            b.on("vector", (lambda jj: lambda e: e.reduce_max(
                m1[:, jj:jj + 1], lgt[:, jj, :], axis=AX.X))(j))
        b.bar_phase()
        for j in range(8):
            b.on("vector", (lambda jj: lambda e: e.tensor_single_scalar(
                msk1[:, jj, :], lgt[:, jj, :], m1[:, jj:jj + 1],
                op=ALU.is_equal))(j))
        b.bar_phase()
        for j in range(8):
            b.on("vector", (lambda jj: lambda e: e.scalar_tensor_tensor(
                cwg[:, jj, :], msk1[:, jj, :], -1e30, lgt[:, jj, :],
                op0=ALU.mult, op1=ALU.add))(j))
        b.bar_phase()
        for j in range(8):
            b.on("vector", (lambda jj: lambda e: e.reduce_max(
                m2[:, jj:jj + 1], cwg[:, jj, :], axis=AX.X))(j))
        b.bar_phase()
        for j in range(8):
            b.on("vector", (lambda jj: lambda e: e.tensor_single_scalar(
                msk2[:, jj, :], cwg[:, jj, :], m2[:, jj:jj + 1],
                op=ALU.is_equal))(j))
        b.bar_phase()
        b.on("vector", lambda e: e.tensor_tensor(sg[:], m2[:], m1[:],
                                                 op=ALU.subtract))
        b.bar_phase()
        b.on("scalar", lambda e: e.activation(sg[:], sg[:], AF.Exp))
        b.bar_phase()
        b.on("vector", lambda e: e.tensor_scalar(w1[:], sg[:], 1.0, 1.0,
                                                 op0=ALU.mult, op1=ALU.add))
        b.bar_phase()
        b.on("vector", lambda e: e.reciprocal(w1[:], w1[:]))
        b.bar_phase()
        b.on("vector", lambda e: e.tensor_tensor(w2[:], sg[:], w1[:],
                                                 op=ALU.mult))
        b.bar_phase()
        for j in range(8):
            b.on("vector", (lambda jj: lambda e: e.tensor_scalar(
                cwg[:, jj, :], msk1[:, jj, :], w1[:, jj:jj + 1], 0.0,
                op0=ALU.mult, op1=ALU.add))(j))
        b.bar_phase()
        for j in range(8):
            b.on("vector", (lambda jj: lambda e: e.scalar_tensor_tensor(
                cwg[:, jj, :], msk2[:, jj, :], w2[:, jj:jj + 1], cwg[:, jj, :],
                op0=ALU.mult, op1=ALU.add))(j))
        b.bar_phase()
        def ctr(gg):
            def f(e):
                r = None
                for j in range(8):
                    r = e.transpose(ps[0:8, j * 512:j * 512 + 128],
                                    cwg[:, j, :], identf[:])
                return r
            return f
        b.on("tensor", ctr(g))
        b.bar_phase()
        for j in range(8):
            n = g * 8 + j
            b.on("scalar", (lambda nn, jj: lambda e: e.activation(
                cwT[:, nn * 128:(nn + 1) * 128],
                ps[0:8, jj * 512:jj * 512 + 128], AF.Copy))(n, j))
        b.bar_phase()
    b.on("vector", lambda e: e.tensor_copy(cwTb[:], cwT[:]))
    b.bar_phase()
    if stop_after == "top2":
        return

    westg = b.sbuf("l1f_westg", [128, 4, D], F32)
    webst = b.sbuf("l1f_webst", [128, 4, D], BF16)
    we_hbm = aps["WeT"].rearrange("e (c p) o -> p e c o", p=128)
    for e_ in range(8):
        for hh in range(2):
            b.dma("sync", westg[:], we_hbm[:, e_, hh * 4:(hh + 1) * 4, :])
            b.bar_phase()
            b.on("vector", (lambda: lambda e: e.tensor_copy(
                webst.ap().rearrange("p c o -> p (c o)"),
                westg.ap().rearrange("p c o -> p (c o)")))())
            b.bar_phase()
            b.dma("gpsimd", aps["webf_hbm"][:, e_, hh * 4:(hh + 1) * 4, :],
                  webst[:])
            b.bar_phase()
    b.free("l1f_webst")
    b.free("l1f_westg")
    if stop_after == "wecast":
        return

    wes = b.sbuf("l1f_wes", [128, NCH, D], BF16)
    xe = b.sbuf("l1f_xe", [128, NCH, 512], BF16)
    rows = b.sbuf("l1f_rows", [1, 8 * 512], BF16)
    cwb8 = b.sbuf("l1f_cwb8", [128, 8, 512], BF16)
    ysl = b.sbuf("l1f_ysl", [128, 8, 512], F32)
    xio = b.sbuf("l1f_xio", [128, 4, D], F32)
    for nt in range(8):
        for e_ in range(8):
            b.dma("gpsimd", rows[0:1, e_ * 512:(e_ + 1) * 512],
                  cwTb[e_:e_ + 1, nt * 512:(nt + 1) * 512])
        b.bar_phase()
        def bmm(e):
            r = None
            for e_ in range(8):
                r = e.matmul(ps[:, e_ * 512:(e_ + 1) * 512], onesb[:],
                             rows[0:1, e_ * 512:(e_ + 1) * 512],
                             start=True, stop=True)
            return r
        b.on("tensor", bmm)
        b.bar_phase()
        for e_ in range(8):
            b.on("scalar", (lambda ee: lambda e: e.activation(
                cwb8[:, ee, :], ps[:, ee * 512:(ee + 1) * 512], AF.Copy))(e_))
        b.bar_phase()
        for e_ in range(8):
            b.dma("sync", wes[:], aps["webf_hbm"][:, e_, :, :])
            for c in range(NCH):
                b.on("vector", (lambda ee, cc, ntn: lambda e: e.tensor_tensor(
                    xe[:, cc, :], xn[:, cc, ntn * 512:(ntn + 1) * 512],
                    cwb8[:, ee, :], op=ALU.mult))(e_, c, nt))
            b.bar_phase()
            def emm(ee):
                def f(e):
                    r = None
                    for oc in range(8):
                        for dic in range(NCH):
                            r = e.matmul(ps[:, oc * 512:(oc + 1) * 512],
                                         wes[:, dic, oc * 128:(oc + 1) * 128],
                                         xe[:, dic, :],
                                         start=(ee == 0 and dic == 0),
                                         stop=False)
                    return r
                return f
            b.on("tensor", emm(e_))
            b.bar_phase()
        def biasmm(ntn):
            def f(e):
                r = None
                for oc in range(8):
                    r = e.matmul(ps[:, oc * 512:(oc + 1) * 512],
                                 beb[:, oc * 128:(oc + 1) * 128],
                                 cwTb[:, ntn * 512:(ntn + 1) * 512],
                                 start=False, stop=True)
                return r
            return f
        b.on("tensor", biasmm(nt))
        b.bar_phase()
        for oc in range(8):
            b.on("scalar", (lambda o: lambda e: e.activation(
                ysl[:, o, :], ps[:, o * 512:(o + 1) * 512], AF.Copy))(oc))
        b.bar_phase()
        b.dma("sync", xio[:], xin_hbm[:, nt * 4:(nt + 1) * 4, :])

        def trf(e):
            r = None
            for st in range(4):
                for oc in range(8):
                    r = e.transpose(
                        ps[:, st * 1024 + oc * 128: st * 1024 + (oc + 1) * 128],
                        ysl[:, oc, st * 128:(st + 1) * 128], identf[:])
            return r
        b.on("tensor", trf)
        b.bar_phase()
        for st in range(4):
            b.on("vector", (lambda ss: lambda e: e.tensor_tensor(
                xio[:, ss, :], xio[:, ss, :], ps[:, ss * 1024:(ss + 1) * 1024],
                op=ALU.add))(st))
        b.bar_phase()
        b.dma("gpsimd", xout_hbm[:, nt * 4:(nt + 1) * 4, :], xio[:])
        b.bar_phase()

    for nm in ("l1f_xio", "l1f_ysl", "l1f_cwb8", "l1f_rows", "l1f_xe",
               "l1f_wes", "l1f_xn", "l1f_cwg", "l1f_lgt", "l1f_msk2",
               "l1f_msk1", "l1f_w2", "l1f_w1", "l1f_sg", "l1f_m2", "l1f_m1"):
        b.free(nm)
    for nm in ("l1f_ln_eps", "l1f_ln_sq", "l1f_ln_bias", "l1f_ln_rstd",
               "l1f_ln_sd", "l1f_ln_var", "l1f_ln_m2", "l1f_ln_mu",
               "l1f_ln_ss", "l1f_ln_sums", "l1f_onesb", "l1f_wgt", "l1f_wgf",
               "l1f_wglo", "l1f_wghi", "l1f_bef", "l1f_beb", "l1f_cwTb",
               "l1f_cwT", "l1f_logT"):
        b.free(nm)


def l1ffn_host_inputs(inputs):
    Wg = np.asarray(inputs["L1_ffn_Wg"], np.float32)
    We = np.asarray(inputs["L1_ffn_We"], np.float32)
    be = np.asarray(inputs["L1_ffn_be"], np.float32)
    return {
        "WgT": np.ascontiguousarray(Wg.T),
        "WeT": np.ascontiguousarray(We.transpose(0, 2, 1)),
        "be": np.ascontiguousarray(be),
    }


def build_l1ffn_prog(stop_after=None):
    nc = bass.Bass("TRN2", target_bir_lowering=False)
    xd = nc.dram_tensor("x", [T, D], F32, kind="ExternalInput").ap()
    aps = {
        "WgT": nc.dram_tensor("WgT", [D, 8], F32, kind="ExternalInput").ap(),
        "WeT": nc.dram_tensor("WeT", [8, D, D], F32, kind="ExternalInput").ap(),
        "be": nc.dram_tensor("be", [8, D], F32, kind="ExternalInput").ap(),
        "webf_hbm": nc.dram_tensor("webf_hbm", [128, 8, NCH, D], BF16,
                                   kind="Internal").ap(),
    }
    identbd = nc.dram_tensor("identb", [128, 128], BF16, kind="ExternalInput").ap()
    identfd = nc.dram_tensor("identf", [128, 128], F32, kind="ExternalInput").ap()
    xod = nc.dram_tensor("xo", [T, D], F32, kind="ExternalOutput").ap()

    b = Bld(nc)
    identb = b.sbuf("identbS", [128, 128], BF16)
    identf = b.sbuf("identfS", [128, 128], F32)
    ps = b.psum("PS", [128, 4096], F32)
    b.dma("sync", identb[:], identbd)
    b.dma("sync", identf[:], identfd)
    b.bar_phase()
    xin_hbm = xd.rearrange("(n p) d -> p n d", p=128)
    xout_hbm = xod.rearrange("(n p) d -> p n d", p=128)
    emit_l1ffn(b, aps, xin_hbm, xout_hbm, ps, identb, identf, stop_after)
    b.finalize()
    return nc



def emit_l0attn(b, aps, xin_hbm, xout_hbm, ps, identb, identf):
    s = mk_lnslab(b, "l0a_ln")
    bq = b.sbuf("l0a_bq", [128, 24], F32)
    bk8 = b.sbuf("l0a_bk8", [128, 8], F32)
    wrx = b.sbuf("l0a_wrxf", [128, 512], F32)
    wrxb = b.sbuf("l0a_wrxb", [128, 512], BF16)
    brx = b.sbuf("l0a_brx", [128, 512], F32)
    rpc = b.sbuf("l0a_rpc", [128, 2, 8, 32], F32)
    den_p = b.sbuf("l0a_denp", [128, 2, 16], F32)
    recip = b.sbuf("l0a_recip", [128, 8], F32)
    xn = b.sbuf("l0a_xn", [128, NCH, T], BF16)
    b.dma("sync", bq[:], aps["bqkv_pp"])
    b.dma("sync", wrx[:], aps["wrx2"])
    b.dma("sync", brx[:], aps["brx_pp"])
    b.on("vector", lambda e: e.memset(s["eps"][:], 1e-5))
    b.bar_phase()
    b.on("vector", lambda e: e.tensor_scalar(bk8[:], bq[:, 8:16], 0.125, 0.0,
                                             op0=ALU.mult, op1=ALU.add))
    b.on("vector", lambda e: e.tensor_copy(wrxb[:], wrx[:]))
    b.bar_phase()

    xg = b.sbuf("l0a_xg", [128, 8, D], F32)
    xng = b.sbuf("l0a_xng", [128, 8, D], BF16)
    for g in range(4):
        b.dma("sync", xg[:], xin_hbm[:, g * 8:(g + 1) * 8, :])
        b.bar_phase()
        emit_ln_group(b, s, xg, xng)
        emit_tok2feat(b, ps, identb, xng, xn, g * 8)
    b.free("l0a_xng")
    b.free("l0a_xg")

    wstg = b.sbuf("l0a_wstg", [128, 8, 128], F32)
    wbf = b.sbuf("l0a_wbf", [128, 8, 128], BF16)
    qsl = b.sbuf("l0a_qsl", [128, 4, T], BF16)
    wq_hbm = aps["WqkvT"].rearrange("(c p) o -> p c o", p=128)
    for hg in range(2):
        for kind in range(3):
            for pcl in range(4):
                mc = kind * 8 + hg * 4 + pcl
                b.dma("sync", wstg[:], wq_hbm[:, :, mc * 128:(mc + 1) * 128])
                b.bar_phase()
                b.on("vector", lambda e: e.tensor_copy(
                    wbf.ap().rearrange("p c o -> p (c o)"),
                    wstg.ap().rearrange("p c o -> p (c o)")))
                b.bar_phase()

                def qkvmm(e):
                    r = None
                    for nt in range(8):
                        for dic in range(NCH):
                            r = e.matmul(ps[:, nt * 512:(nt + 1) * 512],
                                         wbf[:, dic, :],
                                         xn[:, dic, nt * 512:(nt + 1) * 512],
                                         start=(dic == 0),
                                         stop=(dic == NCH - 1))
                    return r
                b.on("tensor", qkvmm)
                b.bar_phase()
                for nt in range(8):
                    if kind == 1:
                        b.on("scalar", (lambda ntn, m=mc, p_=pcl: lambda e:
                             e.activation(qsl[:, p_, ntn * 512:(ntn + 1) * 512],
                                          ps[:, ntn * 512:(ntn + 1) * 512],
                                          AF.Identity,
                                          bias=bk8[:, m - 8:m - 7],
                                          scale=0.125))(nt))
                    else:
                        b.on("scalar", (lambda ntn, m=mc, p_=pcl: lambda e:
                             e.activation(qsl[:, p_, ntn * 512:(ntn + 1) * 512],
                                          ps[:, ntn * 512:(ntn + 1) * 512],
                                          AF.Identity,
                                          bias=bq[:, m:m + 1]))(nt))
                b.bar_phase()
            b.dma("gpsimd", aps["qkv_hbm"][hg, kind], qsl[:])
            b.bar_phase()
    b.free("l0a_qsl")
    b.free("l0a_wbf")
    b.free("l0a_wstg")
    b.free("l0a_xn")

    for hg in range(2):
        qs = b.sbuf(f"l0a_qs{hg}", [128, 4, T], BF16)
        ks = b.sbuf(f"l0a_ks{hg}", [128, 4, T], BF16)
        vt = b.sbuf(f"l0a_vt{hg}", [128, 32, 512], BF16)
        vl = b.sbuf(f"l0a_vl{hg}", [128, 4, T], BF16)
        b.dma("sync", qs[:], aps["qkv_hbm"][hg, 0])
        b.dma("sync", ks[:], aps["qkv_hbm"][hg, 1])
        b.dma("sync", vl[:], aps["qkv_hbm"][hg, 2])
        b.bar_phase()
        for pcl in range(4):
            def vtr(p_):
                def f(e):
                    r = None
                    for jsub in range(32):
                        bank = jsub // 4
                        r = e.transpose(
                            ps[:, bank * 512:(bank + 1) * 512].bitcast(BF16)
                            [:, (jsub % 4) * 128:(jsub % 4) * 128 + 128],
                            vl[:, p_, jsub * 128:(jsub + 1) * 128], identb[:])
                    return r
                return f
            b.on("tensor", vtr(pcl))
            b.bar_phase()
            for bank in range(8):
                b.on("scalar", (lambda p_, bk: lambda e: e.activation(
                    vt[:, bk * 4:(bk + 1) * 4, p_ * 128:(p_ + 1) * 128],
                    ps[:, bk * 512:(bk + 1) * 512].bitcast(BF16)[:, 0:512]
                    .rearrange("p (a t) -> p a t", a=4),
                    AF.Copy))(pcl, bank))
            b.bar_phase()
        b.free(f"l0a_vl{hg}")
        rp_sb = b.sbuf(f"l0a_rpsb{hg}", [128, 8, 512], F32)

        for qt in range(32):
            def rpmm(qtn):
                def f(e):
                    rv = None
                    for pcl in range(4):
                        for hh in range(2):
                            rv = e.matmul(
                                ps[:, (pcl * 2 + hh) * 512:
                                   (pcl * 2 + hh + 1) * 512],
                                qs[hh * 64:(hh + 1) * 64, pcl,
                                   qtn * 128:(qtn + 1) * 128],
                                wrxb[hh * 64:(hh + 1) * 64, :],
                                start=True, stop=True,
                                tile_position=(hh * 64, 0))
                    return rv
                return f
            b.on("tensor", rpmm(qt))
            b.bar_phase()
            for hl in range(8):
                b.on("vector", (lambda h: lambda e: e.tensor_tensor(
                    rp_sb[:, h, :], ps[:, h * 512:(h + 1) * 512], brx[:],
                    op=ALU.add))(hl))
            for hl in range(8):
                b.on("vector", (lambda h, qtn: lambda e: e.tensor_copy(
                    rpc[:, 0, h, qtn:qtn + 1], rp_sb[:, h, 0:1]))(hl, qt))
                b.on("vector", (lambda h, qtn: lambda e: e.tensor_copy(
                    rpc[:, 1, h, qtn:qtn + 1], rp_sb[:, h, 511:512]))(hl, qt))
            b.bar_phase()
            b.dma("gpsimd",
                  aps["rp_hbm"][hg * 8:(hg + 1) * 8, qt].rearrange(
                      "h p u -> p h u"),
                  rp_sb[:])
            b.bar_phase()
        b.free(f"l0a_rpsb{hg}")

        Pp = b.sbuf(f"l0a_Pp{hg}", [128, 2, 2, T], BF16)
        PT2 = b.sbuf(f"l0a_PT2{hg}", [128, 2, 32, 256], BF16)
        Eb2 = b.sbuf(f"l0a_Eb2{hg}", [128, 2, 384], F32)
        osl2 = b.sbuf(f"l0a_osl2{hg}", [128, 4, 256], F32)
        xio = b.sbuf(f"l0a_xio{hg}", [128, 2, D], F32)
        den_g = b.sbuf(f"l0a_deng{hg}", [128, 16], F32)

        for qg in range(16):
            for pc in range(4):
                pend = []
                for qtl in range(2):
                    qt = qg * 2 + qtl
                    q0 = qt * 128
                    jb0 = max(0, q0 - 128)
                    jb1 = min(T, q0 + 256)
                    par = qtl
                    b.on("vector", (lambda p_=par: lambda e: e.memset(
                        den_p.ap().rearrange("p h k -> p (h k)"), 0.0))())
                    for hh in range(2):
                        h_abs = hg * 8 + pc * 2 + hh
                        off = ((h_abs * 32 + qt) * 128 * 512) + (jb0 - q0 + 256)
                        b.dma("sync", Eb2[:, hh, 0:jb1 - jb0],
                              AP(aps["rp_hbm_h"], off, [[511, 128],
                                                        [1, jb1 - jb0]]))
                    b.bar_phase()

                    def smm(jcn, qtn=qt, pcn=pc):
                        st = (jcn % 2) * 2

                        def f(e):
                            rv = None
                            for hh in range(2):
                                rv = e.matmul(
                                    ps[:, (st + hh) * 512:(st + hh + 1) * 512],
                                    qs[hh * 64:(hh + 1) * 64, pcn,
                                       qtn * 128:(qtn + 1) * 128],
                                    ks[hh * 64:(hh + 1) * 64, pcn,
                                       jcn * 512:(jcn + 1) * 512],
                                    start=True, stop=True,
                                    tile_position=(hh * 64, 0))
                            return rv
                        return f

                    def banddve(jcn, jb0_=None, jb1_=None):
                        c0 = jcn * 512
                        bl0, bl1 = max(c0, jb0_), min(c0 + 512, jb1_)
                        st = (jcn % 2) * 2
                        for hh in range(2):
                            b.on("vector", (lambda h, a0=bl0, a1=bl1, cc=c0,
                                            jb=jb0_, s_=st: lambda e:
                                 e.tensor_tensor(
                                     ps[:, (s_ + h) * 512 + (a0 - cc):
                                        (s_ + h) * 512 + (a1 - cc)],
                                     ps[:, (s_ + h) * 512 + (a0 - cc):
                                        (s_ + h) * 512 + (a1 - cc)],
                                     Eb2[:, h, (a0 - jb):(a1 - jb)],
                                     op=ALU.add))(hh))

                    def expph(jcn, qtn=qt, jb0_=None, jb1_=None, par_=None):
                        c0, c1 = jcn * 512, (jcn + 1) * 512
                        bl0, bl1 = max(c0, jb0_), min(c1, jb1_)
                        st = (jcn % 2) * 2
                        for hh in range(2):
                            hl = pc * 2 + hh
                            pieces = []
                            if c0 < jb0_:
                                pieces.append((c0, min(c1, jb0_), 0))
                            if bl0 < bl1:
                                pieces.append((bl0, bl1, None))
                            if c1 > jb1_:
                                pieces.append((max(c0, jb1_), c1, 1))
                            for k_, (a0, a1, bi) in enumerate(pieces):
                                slot = jcn if len(pieces) == 1 else                                     8 + (jcn % 2) * 4 + k_
                                kw = {}
                                if bi is not None:
                                    kw["bias"] = rpc[:, bi, hl, qtn:qtn + 1]
                                b.on("scalar", (lambda h, x0=a0, x1=a1, cc=c0,
                                                sl=slot, s_=st, pp=par_,
                                                kww=dict(kw): lambda e:
                                     e.activation(
                                         Pp[:, pp, h, x0:x1],
                                         ps[:, (s_ + h) * 512 + (x0 - cc):
                                            (s_ + h) * 512 + (x1 - cc)],
                                         AF.Exp,
                                         accum_out=den_p[:, h, sl:sl + 1],
                                         **kww))(hh))

                    def inject():
                        if pend:
                            pend.pop(0)()

                    pendexp = None
                    for jc in range(8):
                        c0 = jc * 512
                        isband = max(c0, jb0) < min(c0 + 512, jb1)
                        if isband:
                            if pendexp is not None:
                                expph(pendexp, jb0_=jb0, jb1_=jb1, par_=par)
                                inject()
                                b.bar_phase()
                            b.on("tensor", smm(jc))
                            inject()
                            b.bar_phase()
                            banddve(jc, jb0_=jb0, jb1_=jb1)
                            b.bar_phase()
                            expph(jc, jb0_=jb0, jb1_=jb1, par_=par)
                            inject()
                            b.bar_phase()
                            pendexp = None
                        else:
                            b.on("tensor", smm(jc))
                            if pendexp is not None:
                                expph(pendexp, jb0_=jb0, jb1_=jb1, par_=par)
                            inject()
                            b.bar_phase()
                            pendexp = jc
                    if pendexp is not None:
                        expph(pendexp, jb0_=jb0, jb1_=jb1, par_=par)
                        inject()
                        b.bar_phase()
                    for hh in range(2):
                        idx = pc * 4 + qtl * 2 + hh
                        b.on("vector", (lambda h, ix=idx: lambda e:
                             e.reduce_sum(den_g[:, ix:ix + 1],
                                          den_p[:, h, :], axis=AX.X))(hh))
                    b.bar_phase()

                    def mkdmat(h_, pp_, qq_):
                        def f():
                            b.dmat("sync",
                                   PT2[:, h_, :, qq_ * 128:(qq_ + 1) * 128],
                                   Pp[:, pp_, h_, :])
                        return f
                    pend.append(mkdmat(0, par, qtl))
                    pend.append(mkdmat(1, par, qtl))
                while pend:
                    pend.pop(0)()
                    b.bar_phase()

                def avmm(pcn=pc):
                    def f(e):
                        rv = None
                        for jsub in range(32):
                            st = (jsub == 0)
                            sp = (jsub == 31)
                            rv = e.matmul(ps[0:64, 0:256],
                                          vt[:, jsub, pcn * 128:pcn * 128 + 64],
                                          PT2[:, 0, jsub, :],
                                          start=st, stop=sp,
                                          tile_position=(0, 0))
                            rv = e.matmul(ps[64:128, 0:256],
                                          vt[:, jsub,
                                             pcn * 128 + 64:pcn * 128 + 128],
                                          PT2[:, 1, jsub, :],
                                          start=st, stop=sp,
                                          tile_position=(0, 64))
                        return rv
                    return f
                b.on("tensor", avmm())
                b.bar_phase()
                b.on("scalar", (lambda p_: lambda e: e.activation(
                    osl2[:, p_, :], ps[:, 0:256], AF.Identity))(pc))
                b.bar_phase()

            src_x = xin_hbm if hg == 0 else xout_hbm
            b.dma("sync", xio[:], src_x[:, qg * 2:(qg + 1) * 2, :])
            b.on("vector", lambda e: e.reciprocal(den_g[:], den_g[:]))
            b.bar_phase()

            def otr(e):
                rv = None
                for ql in range(2):
                    for p_ in range(4):
                        rv = e.transpose(
                            ps[:, ql * 1024 + p_ * 128:
                               ql * 1024 + (p_ + 1) * 128],
                            osl2[:, p_, ql * 128:(ql + 1) * 128], identf[:])
                return rv
            b.on("tensor", otr)
            b.bar_phase()
            for ql in range(2):
                for p_ in range(4):
                    for hh in range(2):
                        idx = p_ * 4 + ql * 2 + hh
                        dcol = hg * 512 + p_ * 128 + hh * 64
                        b.on("vector", (lambda ix=idx, dc=dcol, q_=ql,
                                        pp_=p_, hx=hh: lambda e:
                             e.scalar_tensor_tensor(
                                 xio[:, q_, dc:dc + 64],
                                 ps[:, q_ * 1024 + pp_ * 128 + hx * 64:
                                    q_ * 1024 + pp_ * 128 + hx * 64 + 64],
                                 den_g[:, ix:ix + 1],
                                 xio[:, q_, dc:dc + 64],
                                 op0=ALU.mult, op1=ALU.add))())
            b.bar_phase()
            b.dma("gpsimd", xout_hbm[:, qg * 2:(qg + 1) * 2, :], xio[:])
            b.bar_phase()

        for nm in (f"l0a_deng{hg}", f"l0a_xio{hg}", f"l0a_osl2{hg}",
                   f"l0a_Eb2{hg}", f"l0a_PT2{hg}", f"l0a_Pp{hg}",
                   f"l0a_vt{hg}", f"l0a_ks{hg}", f"l0a_qs{hg}"):
            b.free(nm)

    for nm in ("l0a_recip", "l0a_denp", "l0a_rpc", "l0a_brx", "l0a_wrxb",
               "l0a_wrxf", "l0a_bk8", "l0a_bq", "l0a_ln_eps", "l0a_ln_sq",
               "l0a_ln_bias", "l0a_ln_rstd", "l0a_ln_sd", "l0a_ln_var",
               "l0a_ln_m2", "l0a_ln_mu", "l0a_ln_ss", "l0a_ln_sums"):
        b.free(nm)


def l0attn_host_inputs(inputs):
    Wqkv = np.asarray(inputs["L0_attn_Wqkv"], np.float32)
    bqkv = np.asarray(inputs["L0_attn_bqkv"], np.float32)
    Wr = np.asarray(inputs["L0_attn_Wr"], np.float32)
    br = np.asarray(inputs["L0_attn_br"], np.float32)
    perm = np.array([h * 192 + kind * 64 + c
                     for kind in range(3) for h in range(16) for c in range(64)])
    WqkvT = np.ascontiguousarray(Wqkv[perm, :].T)
    bperm = bqkv[perm]
    idx = np.clip(np.arange(512) - 128, 0, 256)
    Wr_ext = Wr[idx, :].T
    return {
        "WqkvT": WqkvT,
        "bqkv_pp": np.ascontiguousarray(bperm.reshape(24, 128).T),
        "wrx2": np.ascontiguousarray(np.vstack([Wr_ext, Wr_ext])),
        "brx_pp": np.ascontiguousarray(np.tile(br[idx][None, :], (128, 1))),
    }


def build_l0attn_prog():
    nc = bass.Bass("TRN2", target_bir_lowering=False)
    xd = nc.dram_tensor("x", [T, D], F32, kind="ExternalInput").ap()
    rp_h = nc.dram_tensor("rp_hbm", [16, 32, 128, 512], F32, kind="Internal")
    aps = {
        "WqkvT": nc.dram_tensor("WqkvT", [D, 3072], F32, kind="ExternalInput").ap(),
        "bqkv_pp": nc.dram_tensor("bqkv_pp", [128, 24], F32, kind="ExternalInput").ap(),
        "wrx2": nc.dram_tensor("wrx2", [128, 512], F32, kind="ExternalInput").ap(),
        "brx_pp": nc.dram_tensor("brx_pp", [128, 512], F32, kind="ExternalInput").ap(),
        "rp_hbm": rp_h.ap(),
        "rp_hbm_h": rp_h,
        "qkv_hbm": nc.dram_tensor("qkv_hbm", [2, 3, 128, 4, T], BF16,
                                  kind="Internal").ap(),
    }
    identbd = nc.dram_tensor("identb", [128, 128], BF16, kind="ExternalInput").ap()
    identfd = nc.dram_tensor("identf", [128, 128], F32, kind="ExternalInput").ap()
    xod = nc.dram_tensor("xo", [T, D], F32, kind="ExternalOutput").ap()

    b = Bld(nc)
    identb = b.sbuf("identbS", [128, 128], BF16)
    identf = b.sbuf("identfS", [128, 128], F32)
    ps = b.psum("PS", [128, 4096], F32)
    b.dma("sync", identb[:], identbd)
    b.dma("sync", identf[:], identfd)
    b.bar_phase()
    xin_hbm = xd.rearrange("(n p) d -> p n d", p=128)
    xout_hbm = xod.rearrange("(n p) d -> p n d", p=128)
    emit_l0attn(b, aps, xin_hbm, xout_hbm, ps, identb, identf)
    b.finalize()
    return nc



def _shift_load(b, eng, dst, z_flat, n):
    t0 = n * 128
    b.dma(eng, dst[:, 0:512], z_flat[t0:t0 + 128, 0:512])
    if n == 0:
        b.on("vector", lambda e: e.memset(dst[0:1, 512:1024], 0.0))
        b.dma(eng, dst[1:128, 512:1024], z_flat[0:127, 512:1024])
    else:
        b.dma(eng, dst[:, 512:1024], z_flat[t0 - 1:t0 + 127, 512:1024])


def emit_l0ffn_attnhalf(b, aps, li, ps, identb, identf, z_in, z2, dbg=None):
    qs_pp = b.sbuf("lfa_qspp", [128, 1], F32)
    ks_pp = b.sbuf("lfa_kspp", [128, 1], F32)
    perm = b.sbuf("lfa_perm", [128, 128], BF16)
    ind2 = b.sbuf("lfa_ind2", [128, 2], BF16)
    bc2 = b.sbuf("lfa_bc2", [2, 128], BF16)
    mski = b.sbuf("lfa_mski", [128, 384], F32)
    msk0a = b.sbuf("lfa_msk0a", [128, 384], F32)
    msk0b = b.sbuf("lfa_msk0b", [128, 384], F32)
    wq = b.sbuf("lfa_wq", [128, NCH, 3072], BF16)
    wo = b.sbuf("lfa_wo", [128, NCH, 1024], BF16)
    kc = b.sbuf("lfa_kc", [128, NCH, 512], BF16)
    vc = b.sbuf("lfa_vc", [128, 4, 1024], BF16)
    tabs = b.sbuf("lfa_tabs", [128, 4, 256], F32)
    den = b.sbuf("lfa_den", [128, 16], F32)
    nmx = b.sbuf("lfa_nmx", [128, 16], F32)
    rec = b.sbuf("lfa_rec", [128, 16], F32)
    wstg = b.sbuf("lfa_wstg", [128, 3072], F32)

    b.dma("sync", qs_pp[:], aps["qs_pp"][li])
    b.dma("sync", ks_pp[:], aps["ks_pp"][li])
    b.dma("sync", perm[:], aps["perm"])
    b.dma("sync", ind2[:], aps["ind2"])
    b.dma("sync", bc2[:], aps["bc2"])
    b.dma("sync", mski[:], aps["mask_i"])
    b.dma("sync", msk0a[:], aps["mask_0a"])
    b.dma("sync", msk0b[:], aps["mask_0b"])
    b.on("vector", lambda e: e.memset(kc.ap().rearrange("p c v -> p (c v)"), 0.0))
    b.on("vector", lambda e: e.memset(vc.ap().rearrange("p s v -> p (s v)"), 0.0))
    b.bar_phase()
    for dic in range(NCH):
        b.dma("sync", wstg[:], aps["Wqkv"][li].rearrange(
            "(c p) o -> p c o", p=128)[:, dic, :])
        b.bar_phase()
        b.on("vector", (lambda d: lambda e: e.tensor_copy(
            wq[:, d, :], wstg[:]))(dic))
        b.bar_phase()
        b.dma("sync", wstg[:, 0:1024], aps["Wo"][li].rearrange(
            "(c p) o -> p c o", p=128)[:, dic, :])
        b.bar_phase()
        b.on("vector", (lambda d: lambda e: e.tensor_copy(
            wo[:, d, :], wstg[:, 0:1024]))(dic))
        b.bar_phase()
    b.free("lfa_wstg")

    ush = b.sbuf("lfa_ush", [128, 2, D], F32)
    qf = b.sbuf("lfa_qf", [128, NCH, 256], F32)
    kf = b.sbuf("lfa_kf", [128, NCH, 256], F32)
    rq = b.sbuf("lfa_rq", [2, 2048], F32)
    rqb = b.sbuf("lfa_rqb", [2, 2048], BF16)
    rql = b.sbuf("lfa_rql", [2, 2048], BF16)
    rqt = b.sbuf("lfa_rqt", [2, 2048], F32)
    t1 = b.sbuf("lfa_t1", [128, NCH, 256], F32)
    tb = b.sbuf("lfa_tb", [128, NCH, 256], BF16)
    qpb = b.sbuf("lfa_qpb", [128, NCH, 256], BF16)
    sq2 = tb
    vev = qpb
    uf = tb
    P16 = b.sbuf("lfa_P16", [128, 16, 384], BF16)
    PT16 = b.sbuf("lfa_PT16", [128, 48, 128], BF16)
    ao = b.sbuf("lfa_ao", [128, NCH, 256], BF16)
    wsl = b.sbuf("lfa_wsl", [128, NCH, 256], F32)
    ub = wsl.ap()[:, 0:4, :].bitcast(BF16).rearrange(
        "p a t -> p (a t)").rearrange("p (a d) -> p a d", a=2)
    zres = b.sbuf("lfa_zres", [128, 2, D], F32)

    z_flat = z_in
    zo_flat = z2
    ush2 = b.sbuf("lfa_ush2", [128, 2, D], F32)

    def issue_win_loads(wn, ubuf):
        _shift_load(b, "sync", ubuf[:, 0, :], z_flat, wn * 2)
        _shift_load(b, "sync", ubuf[:, 1, :], z_flat, wn * 2 + 1)

    issue_win_loads(0, ush)
    b.bar_phase()
    for w in range(16):
        w0 = w * 256
        ubuf = ush if w % 2 == 0 else ush2
        nbuf = ush2 if w % 2 == 0 else ush
        b.dma("gpsimd", tabs[:],
              aps["rotab"][:, :, w0:w0 + 256].rearrange("k p t -> p k t"))
        b.on("scalar", (lambda ub_=ubuf: lambda e: e.activation(
            ub.rearrange("p a d -> p (a d)"),
            ub_.ap().rearrange("p a d -> p (a d)"), AF.Copy))())
        b.bar_phase()
        for j in range(2):
            pbf = ps[:, j * 512:(j + 1) * 512].bitcast(BF16)
            b.on("tensor", (lambda jj, pb: lambda e: [
                e.transpose(pb[:, c * 128:(c + 1) * 128],
                            ub[:, jj, c * 128:(c + 1) * 128], identb[:])
                for c in range(NCH)][-1])(j, pbf))
        b.bar_phase()
        for j in range(2):
            pbf = ps[:, j * 512:(j + 1) * 512].bitcast(BF16)
            b.on("scalar", (lambda jj, pb: lambda e: e.activation(
                uf[:, :, jj * 128:(jj + 1) * 128],
                pb.rearrange("p (c t) -> p c t", c=NCH), AF.Copy))(j, pbf))
        b.bar_phase()
        def qkmm(r_):
            def f(e):
                rv = None
                base = (r_ % 2) * 2048
                for m8 in range(8):
                    mc = r_ * 8 + m8
                    for dic in range(NCH):
                        rv = e.matmul(ps[:, base + m8 * 256:base + m8 * 256 + 256],
                                      wq[:, dic, mc * 128:(mc + 1) * 128],
                                      uf[:, dic, :],
                                      start=(dic == 0),
                                      stop=(dic == NCH - 1))
                return rv
            return f

        def qkevac(r_):
            base = (r_ % 2) * 2048
            if r_ < 2:
                dst = qf if r_ == 0 else kf
                for m8 in range(8):
                    b.on("scalar", (lambda m, d=dst, ba=base: lambda e:
                         e.activation(d[:, m, :],
                                      ps[:, ba + m * 256:ba + m * 256 + 256],
                                      AF.Copy))(m8))
            else:
                b.on("vector", lambda e: e.tensor_copy(
                    vc.ap().rearrange("p s v -> p (s v)")[:, 0:2048],
                    vc.ap().rearrange("p s v -> p (s v)")[:, 2048:4096]))
                b.on("vector", lambda e: e.tensor_copy(
                    kc[:, :, 0:256], kc[:, :, 256:512]))
                for m8 in range(8):
                    b.on("scalar", (lambda m, ba=base: lambda e: e.activation(
                        vev[:, m, :], ps[:, ba + m * 256:ba + m * 256 + 256],
                        AF.Copy))(m8))

        b.on("tensor", qkmm(0))
        if w < 15:
            issue_win_loads(w + 1, nbuf)
        b.bar_phase()
        b.on("tensor", qkmm(1))
        qkevac(0)
        b.bar_phase()
        b.on("tensor", qkmm(2))
        qkevac(1)
        b.bar_phase()
        qkevac(2)
        b.bar_phase()

        def vtr(e):
            rv = None
            for m8 in range(8):
                for jl in range(2):
                    rv = e.transpose(
                        ps[:, 2048 + jl * 512:2048 + (jl + 1) * 512]
                        .bitcast(BF16)[:, m8 * 128:(m8 + 1) * 128],
                        vev[:, m8, jl * 128:(jl + 1) * 128], identb[:])
            return rv
        b.on("tensor", vtr)
        b.bar_phase()
        for jl in range(2):
            b.on("scalar", (lambda j: lambda e: e.activation(
                vc[:, 2 + j, :],
                ps[:, 2048 + j * 512:2048 + (j + 1) * 512].bitcast(BF16),
                AF.Copy))(jl))
        b.bar_phase()
        b.on("scalar", lambda e: e.activation(
            sq2.ap().rearrange("p c t -> p (c t)"),
            qf.ap().rearrange("p c t -> p (c t)"), AF.Square))
        b.bar_phase()

        def ssq_q(e):
            rv = None
            for pc in range(8):
                rv = e.matmul(ps[0:2, pc * 256:(pc + 1) * 256],
                              ind2[:], sq2[:, pc, :], start=True, stop=True)
            return rv
        b.on("tensor", ssq_q)
        b.bar_phase()
        b.on("scalar", lambda e: e.activation(rqt[0:2, 0:2048],
                                              ps[0:2, 0:2048], AF.Ln))
        b.on("scalar", lambda e: e.activation(
            sq2.ap().rearrange("p c t -> p (c t)"),
            kf.ap().rearrange("p c t -> p (c t)"), AF.Square))
        b.bar_phase()

        def ssq_k(e):
            rv = None
            for pc in range(8):
                rv = e.matmul(ps[0:2, 2048 + pc * 256:2048 + (pc + 1) * 256],
                              ind2[:], sq2[:, pc, :], start=True, stop=True)
            return rv
        b.on("tensor", ssq_k)
        b.on("scalar", lambda e: e.activation(rqt[0:2, 0:2048],
                                              rqt[0:2, 0:2048], AF.Exp,
                                              scale=-0.5))
        b.bar_phase()
        b.on("scalar", lambda e: e.activation(rq[0:2, 0:2048],
                                              ps[0:2, 2048:4096], AF.Ln))
        b.bar_phase()
        b.on("scalar", lambda e: e.activation(rq[0:2, 0:2048],
                                              rq[0:2, 0:2048], AF.Exp,
                                              scale=-0.5))
        b.bar_phase()
        b.on("vector", lambda e: e.tensor_copy(rqb[0:2, 0:2048], rqt[0:2, 0:2048]))
        b.bar_phase()
        b.on("vector", lambda e: e.tensor_tensor(rqt[0:2, 0:2048],
                                                 rqt[0:2, 0:2048],
                                                 rqb[0:2, 0:2048],
                                                 op=ALU.subtract))
        b.bar_phase()
        b.on("vector", lambda e: e.tensor_copy(rql[0:2, 0:2048], rqt[0:2, 0:2048]))
        b.bar_phase()

        for side in range(2):
            tabA = tabs[:, 0 if side == 0 else 2, :]
            tabB = tabs[:, 1 if side == 0 else 3, :]
            spp = qs_pp if side == 0 else ks_pp
            srcf = qf if side == 0 else kf
            if side == 1:
                b.on("vector", lambda e: e.tensor_copy(rqb[0:2, 0:2048],
                                                       rq[0:2, 0:2048]))
                b.bar_phase()
                b.on("vector", lambda e: e.tensor_tensor(rq[0:2, 0:2048],
                                                         rq[0:2, 0:2048],
                                                         rqb[0:2, 0:2048],
                                                         op=ALU.subtract))
                b.bar_phase()
                b.on("vector", lambda e: e.tensor_copy(rql[0:2, 0:2048],
                                                       rq[0:2, 0:2048]))
                b.bar_phase()

            def fbcmm(e):
                rv = None
                for pc in range(8):
                    rv = e.matmul(ps[:, pc * 256:(pc + 1) * 256], bc2[:],
                                  rqb[0:2, pc * 256:(pc + 1) * 256],
                                  start=True, stop=False)
                    rv = e.matmul(ps[:, pc * 256:(pc + 1) * 256], bc2[:],
                                  rql[0:2, pc * 256:(pc + 1) * 256],
                                  start=False, stop=True)
                return rv
            b.on("tensor", fbcmm)
            b.bar_phase()
            if dbg is not None and w == 8 and side == 0:
                b.on("scalar", lambda e: e.activation(
                    dbg["_fbcs"][:], ps[:, 0:256], AF.Copy))
                b.bar_phase()
                b.dma("gpsimd", dbg["fbc"], dbg["_fbcs"][:])
                b.dma("gpsimd", dbg["qspp_d"], qs_pp[:])
                b.bar_phase()
            for pc in range(8):
                b.on("vector", (lambda p_, sf=srcf, sp=spp: lambda e:
                     e.scalar_tensor_tensor(
                         t1[:, p_, :], sf[:, p_, :], sp[:, 0:1],
                         ps[:, p_ * 256:(p_ + 1) * 256],
                         op0=ALU.mult, op1=ALU.mult))(pc))
            b.bar_phase()
            for pc in range(8):
                b.on("vector", (lambda p_, tB=tabB: lambda e: e.tensor_tensor(
                    tb[:, p_, :], t1[:, p_, :], tB, op=ALU.mult))(pc))
            b.bar_phase()
            for pc in range(8):
                b.on("vector", (lambda p_, tA=tabA: lambda e: e.tensor_tensor(
                    t1[:, p_, :], t1[:, p_, :], tA, op=ALU.mult))(pc))
            b.bar_phase()

            if dbg is not None and w == 8 and side == 0:
                b.dma("gpsimd", dbg["tb"], tb[:])
                b.dma("gpsimd", dbg["t1"], t1[:])
                b.dma("gpsimd", dbg["qf"], qf[:])
                b.dma("gpsimd", dbg["rq2"], rqb[:])
                b.bar_phase()

            def permm(e):
                rv = None
                for pc in range(8):
                    rv = e.matmul(ps[:, pc * 256:(pc + 1) * 256], perm[:],
                                  tb[:, pc, :], start=True, stop=True)
                return rv
            b.on("tensor", permm)
            b.bar_phase()
            for pc in range(8):
                if side == 0:
                    dstap = qpb[:, pc, :]
                else:
                    dstap = kc[:, pc, 256:512]
                b.on("vector", (lambda p_, da=dstap: lambda e: e.tensor_tensor(
                    da, t1[:, p_, :], ps[:, p_ * 256:(p_ + 1) * 256],
                    op=ALU.add))(pc))
            b.bar_phase()

        for qtl in range(2):
            msk = mski if w > 0 else (msk0a if qtl == 0 else msk0b)
            jlo = qtl * 128
            for rnd in range(2):
                def smm(r_, qq=qtl, jl=jlo):
                    def f(e):
                        rv = None
                        for p4 in range(4):
                            pc = r_ * 4 + p4
                            for hh in range(2):
                                rv = e.matmul(
                                    ps[:, (p4 * 2 + hh) * 512:
                                       (p4 * 2 + hh) * 512 + 384],
                                    qpb[hh * 64:(hh + 1) * 64, pc,
                                        qq * 128:(qq + 1) * 128],
                                    kc[hh * 64:(hh + 1) * 64, pc,
                                       jl:jl + 384],
                                    start=True, stop=True,
                                    tile_position=(hh * 64, 0))
                        return rv
                    return f
                b.on("tensor", smm(rnd))
                b.bar_phase()
                for p4 in range(4):
                    for hh in range(2):
                        hl = (rnd * 4 + p4) * 2 + hh
                        bk = (p4 * 2 + hh)
                        b.on("vector", (lambda h, k: lambda e: e.reduce_max(
                            nmx[:, h:h + 1],
                            ps[:, k * 512:k * 512 + 384], axis=AX.X,
                            negate=True))(hl, bk))
                for p4 in range(4):
                    for hh in range(2):
                        hl = (rnd * 4 + p4) * 2 + hh
                        bk = (p4 * 2 + hh)
                        b.on("vector", (lambda h, k, m=msk: lambda e:
                             e.scalar_tensor_tensor(
                                 ps[:, k * 512:k * 512 + 384],
                                 ps[:, k * 512:k * 512 + 384],
                                 nmx[:, h:h + 1], m[:],
                                 op0=ALU.add, op1=ALU.add))(hl, bk))
                b.bar_phase()
                for p4 in range(4):
                    for hh in range(2):
                        hl = (rnd * 4 + p4) * 2 + hh
                        bk = (p4 * 2 + hh)
                        b.on("scalar", (lambda h, k: lambda e: e.activation(
                            P16[:, h, :], ps[:, k * 512:k * 512 + 384],
                            AF.Exp, accum_out=den[:, h:h + 1]))(hl, bk))
                b.bar_phase()
            b.on("vector", lambda e: e.reciprocal(rec[:], den[:]))
            b.bar_phase()
            for hl in range(16):
                b.on("vector", (lambda h: lambda e: e.tensor_scalar(
                    P16[:, h, :], P16[:, h, :], rec[:, h:h + 1], None,
                    op0=ALU.mult))(hl))
            b.bar_phase()

            b.dmat("sync", PT16.ap(),
                   P16.ap().rearrange("p h t -> p (h t)"))
            b.bar_phase()

            def avmm(qq=qtl):
                def f(e):
                    rv = None
                    for pc in range(8):
                        for js in range(3):
                            slot = qq + js
                            st = (js == 0)
                            sp = (js == 2)
                            h0 = pc * 2
                            rv = e.matmul(
                                ps[0:64, pc * 512:pc * 512 + 128],
                                vc[:, slot, h0 * 64:h0 * 64 + 64],
                                PT16[:, h0 * 3 + js, :],
                                start=st, stop=sp, tile_position=(0, 0))
                            rv = e.matmul(
                                ps[64:128, pc * 512:pc * 512 + 128],
                                vc[:, slot, h0 * 64 + 64:h0 * 64 + 128],
                                PT16[:, (h0 + 1) * 3 + js, :],
                                start=st, stop=sp, tile_position=(0, 64))
                    return rv
                return f
            b.on("tensor", avmm())
            b.bar_phase()
            for pc in range(8):
                b.on("scalar", (lambda p_, qq=qtl: lambda e: e.activation(
                    ao[:, p_, qq * 128:(qq + 1) * 128],
                    ps[:, p_ * 512:p_ * 512 + 128], AF.Identity))(pc))
            b.bar_phase()

        if dbg is not None and w == 8:
            b.dma("gpsimd", dbg["tabs"], tabs[:])
            b.dma("gpsimd", dbg["qpb"], qpb[:])
            b.dma("gpsimd", dbg["kc"], kc[:])
            b.dma("gpsimd", dbg["vc"], vc[:])
            b.dma("gpsimd", dbg["P16"], P16[:])
            b.dma("gpsimd", dbg["ao"], ao[:])
            b.dma("gpsimd", dbg["uf"], uf[:])
            b.bar_phase()
        def womm(e):
            rv = None
            for oc in range(8):
                for dic in range(NCH):
                    rv = e.matmul(ps[:, oc * 512:oc * 512 + 256],
                                  wo[:, dic, oc * 128:(oc + 1) * 128],
                                  ao[:, dic, :],
                                  start=(dic == 0), stop=(dic == NCH - 1))
            return rv
        b.on("tensor", womm)
        b.bar_phase()
        for oc in range(8):
            b.on("scalar", (lambda o: lambda e: e.activation(
                wsl[:, o, :], ps[:, o * 512:o * 512 + 256], AF.Copy))(oc))
        b.bar_phase()
        b.dma("sync", zres[:, 0, :], z_flat[w0:w0 + 128, :])
        b.dma("sync", zres[:, 1, :], z_flat[w0 + 128:w0 + 256, :])

        def wotr(e):
            rv = None
            for oc in range(8):
                for jl in range(2):
                    rv = e.transpose(
                        ps[:, jl * 2048 + oc * 128:jl * 2048 + (oc + 1) * 128],
                        wsl[:, oc, jl * 128:(jl + 1) * 128], identf[:])
            return rv
        b.on("tensor", wotr)
        b.bar_phase()
        for jl in range(2):
            b.on("vector", (lambda j: lambda e: e.tensor_tensor(
                zres[:, j, :], zres[:, j, :],
                ps[:, j * 2048:j * 2048 + 1024], op=ALU.add))(jl))
        b.bar_phase()
        b.dma("gpsimd", zo_flat[w0:w0 + 128, :], zres[:, 0, :])
        b.dma("gpsimd", zo_flat[w0 + 128:w0 + 256, :], zres[:, 1, :])
        b.bar_phase()

    for nm in ("lfa_ush2", "lfa_zres", "lfa_wsl", "lfa_ao", "lfa_PT16", "lfa_P16",
               "lfa_qpb", "lfa_tb", "lfa_t1", "lfa_rqt", "lfa_rql", "lfa_rqb",
               "lfa_rq", "lfa_kf", "lfa_qf",
               "lfa_ush", "lfa_rec", "lfa_nmx",
               "lfa_den", "lfa_tabs", "lfa_vc", "lfa_kc", "lfa_wo", "lfa_wq",
               "lfa_msk0b", "lfa_msk0a", "lfa_mski", "lfa_bc2", "lfa_ind2",
               "lfa_perm", "lfa_kspp", "lfa_qspp"):
        b.free(nm)


def emit_l0ffn_ffnhalf(b, aps, li, ps, identb, identf, z2, z_out):
    s = mk_lnslab(b, "lff_ln")
    gcol = b.sbuf("lff_gcol", [128, 8], F32)
    bcol = b.sbuf("lff_bcol", [128, 8], BF16)
    b1e = b.sbuf("lff_b1e", [128, 44], F32)
    b1p = b.sbuf("lff_b1p", [128, 44], F32)
    b2c = b.sbuf("lff_b2c", [128, 8], F32)
    w2bf = b.sbuf("lff_w2bf", [128, 22, 1024], BF16)

    b.dma("sync", gcol[:], aps["g_pp"][li])
    b.dma("sync", b1p[:], aps["b1_pp"][li])
    b.dma("sync", b2c[:], aps["b2_pp"][li])
    b.on("vector", lambda e: e.memset(s["eps"][:], 1e-5))
    b.bar_phase()
    bstg = b.sbuf("lff_bstg", [128, 8], F32)
    b.dma("sync", bstg[:], aps["beta_pp"][li])
    b.bar_phase()
    b.on("vector", lambda e: e.tensor_copy(bcol[:], bstg[:]))
    b.bar_phase()

    stg = b.sbuf("lff_stg", [128, 5632], F32)
    tmpb = b.sbuf("lff_tmpb", [128, 5632], BF16)
    w1g = b.sbuf("lff_w1g", [128, 5632], BF16)
    for dic in range(NCH):
        b.dma("sync", stg[:], aps["W1p"][li].rearrange(
            "(c p) o -> p c o", p=128)[:, dic, :])
        b.bar_phase()
        b.on("vector", (lambda d: lambda e: e.tensor_scalar(
            w1g[:], stg[:], gcol[:, d:d + 1], None, op0=ALU.mult))(dic))
        b.on("vector", lambda e: e.tensor_copy(tmpb[:], stg[:]))
        b.bar_phase()

        def b1emm(d):
            def f(e):
                rv = None
                for mc in range(44):
                    rv = e.matmul(ps[:, mc:mc + 1],
                                  tmpb[:, mc * 128:(mc + 1) * 128],
                                  bcol[:, d:d + 1],
                                  start=(d == 0), stop=(d == NCH - 1))
                return rv
            return f
        b.on("tensor", b1emm(dic))
        b.bar_phase()
        b.dma("gpsimd", aps["w1bf_hbm"][:, dic, :], w1g[:])
        b.bar_phase()
    b.on("vector", lambda e: e.tensor_tensor(b1e[:], ps[:, 0:44], b1p[:],
                                             op=ALU.add))
    b.bar_phase()
    b.free("lff_w1g")
    b.free("lff_tmpb")
    b.free("lff_stg")

    stg2 = b.sbuf("lff_stg2", [128, 4, 1024], F32)
    w2_hbm = aps["W2p"][li].rearrange("(c p) o -> p c o", p=128)
    for g in range(6):
        c0 = g * 4
        nc_ = min(4, 22 - c0)
        b.dma("sync", stg2[:, 0:nc_, :], w2_hbm[:, c0:c0 + nc_, :])
        b.bar_phase()
        b.on("vector", (lambda cc0=c0, nn=nc_: lambda e: e.tensor_copy(
            w2bf[:, cc0:cc0 + nn, :].rearrange("p c o -> p (c o)"),
            stg2[:, 0:nn, :].rearrange("p c o -> p (c o)")))())
        b.bar_phase()
    b.free("lff_stg2")

    zsh = b.sbuf("lff_zsh", [128, 4, D], F32)
    xng = b.sbuf("lff_xng", [128, 4, D], BF16)
    xf = b.sbuf("lff_xf", [128, NCH, 512], BF16)
    w1r = b.sbuf("lff_w1r", [128, NCH, 512], BF16)
    w1r2 = b.sbuf("lff_w1r2", [128, NCH, 512], BF16)
    sa = b.sbuf("lff_sa", [128, 4, 512], F32)
    sg = b.sbuf("lff_sg", [128, 4, 512], F32)
    hb = b.sbuf("lff_hb", [128, 22, 512], BF16)
    wsl2 = b.sbuf("lff_wsl2", [128, NCH, 512], F32)
    zres2 = b.sbuf("lff_zres2", [128, 4, D], F32)

    z2_flat = z2
    zo_flat = z_out
    for nt in range(8):
        for j in range(4):
            _shift_load(b, "sync", zsh[:, j, :], z2_flat, nt * 4 + j)
        b.bar_phase()
        emit_ln_group(b, s, zsh, xng, nj=4)
        for j in range(4):
            pbf = ps[:, j * 512:(j + 1) * 512].bitcast(BF16)
            b.on("tensor", (lambda jj, pb: lambda e: [
                e.transpose(pb[:, c * 128:(c + 1) * 128],
                            xng[:, jj, c * 128:(c + 1) * 128], identb[:])
                for c in range(NCH)][-1])(j, pbf))
        b.bar_phase()
        for j in range(4):
            pbf = ps[:, j * 512:(j + 1) * 512].bitcast(BF16)
            b.on("scalar", (lambda jj, pb: lambda e: e.activation(
                xf[:, :, jj * 128:(jj + 1) * 128],
                pb.rearrange("p (c t) -> p c t", c=NCH), AF.Copy))(j, pbf))
        b.bar_phase()
        w1v = aps["w1bf_hbm"].rearrange("p c (m x) -> p c m x", x=128)

        def w1dma(r_):
            buf = w1r if r_ % 2 == 0 else w1r2
            b.dma("sync", buf[:, :, 0:256].rearrange(
                "p c (a x) -> p c a x", a=2), w1v[:, :, r_ * 2:r_ * 2 + 2, :])
            b.dma("sync", buf[:, :, 256:512].rearrange(
                "p c (a x) -> p c a x", a=2),
                w1v[:, :, 22 + r_ * 2:22 + r_ * 2 + 2, :])

        def w1mm(r_):
            buf = w1r if r_ % 2 == 0 else w1r2
            base = (r_ % 2) * 4

            def f(e):
                rv = None
                for k in range(2):
                    for half in range(2):
                        bk = base + k * 2 + half
                        for dic in range(NCH):
                            rv = e.matmul(
                                ps[:, bk * 512:(bk + 1) * 512],
                                buf[:, dic, half * 256 + k * 128:
                                    half * 256 + (k + 1) * 128],
                                xf[:, dic, :],
                                start=(dic == 0), stop=(dic == NCH - 1))
                return rv
            return f

        def w1evac(r_):
            base = (r_ % 2) * 4
            for k in range(2):
                mc = r_ * 2 + k
                sl = (r_ % 2) * 2 + k
                b.on("scalar", (lambda s_, m, bk: lambda e: e.activation(
                    sa[:, s_, :], ps[:, bk * 512:(bk + 1) * 512],
                    AF.Identity, bias=b1e[:, m:m + 1]))(sl, mc, base + k * 2))
                b.on("scalar", (lambda s_, m, bk: lambda e: e.activation(
                    sg[:, s_, :], ps[:, bk * 512:(bk + 1) * 512],
                    AF.Gelu, bias=b1e[:, m + 22:m + 23]))(sl, mc,
                                                          base + k * 2 + 1))

        def w1hmult(r_):
            for k in range(2):
                mc = r_ * 2 + k
                sl = (r_ % 2) * 2 + k
                b.on("vector", (lambda s_, m: lambda e: e.tensor_tensor(
                    hb[:, m, :], sa[:, s_, :], sg[:, s_, :],
                    op=ALU.mult))(sl, mc))

        w1dma(0)
        b.bar_phase()
        for r in range(11):
            b.on("tensor", w1mm(r))
            if r < 10:
                w1dma(r + 1)
            if r >= 1:
                w1evac(r - 1)
            if r >= 2:
                w1hmult(r - 2)
            b.bar_phase()
        w1evac(10)
        w1hmult(9)
        b.bar_phase()
        w1hmult(10)
        b.bar_phase()

        def w2mm(e):
            rv = None
            for oc in range(8):
                for hc in range(22):
                    rv = e.matmul(ps[:, oc * 512:(oc + 1) * 512],
                                  w2bf[:, hc, oc * 128:(oc + 1) * 128],
                                  hb[:, hc, :],
                                  start=(hc == 0), stop=(hc == 21))
            return rv
        b.on("tensor", w2mm)
        b.bar_phase()
        for oc in range(8):
            b.on("scalar", (lambda o: lambda e: e.activation(
                wsl2[:, o, :], ps[:, o * 512:(o + 1) * 512], AF.Identity,
                bias=b2c[:, o:o + 1]))(oc))
        b.bar_phase()
        for j in range(4):
            t0 = (nt * 4 + j) * 128
            b.dma("sync", zres2[:, j, :], z2_flat[t0:t0 + 128, :])

        def w2tr(e):
            rv = None
            for j in range(4):
                for oc in range(8):
                    rv = e.transpose(
                        ps[:, j * 1024 + oc * 128:j * 1024 + (oc + 1) * 128],
                        wsl2[:, oc, j * 128:(j + 1) * 128], identf[:])
            return rv
        b.on("tensor", w2tr)
        b.bar_phase()
        for j in range(4):
            b.on("vector", (lambda jj: lambda e: e.tensor_tensor(
                zres2[:, jj, :], zres2[:, jj, :],
                ps[:, jj * 1024:(jj + 1) * 1024], op=ALU.add))(j))
        b.bar_phase()
        for j in range(4):
            t0 = (nt * 4 + j) * 128
            b.dma("gpsimd", zo_flat[t0:t0 + 128, :], zres2[:, j, :])
        b.bar_phase()

    for nm in ("lff_zres2", "lff_wsl2", "lff_hb", "lff_sg", "lff_sa",
               "lff_w1r2", "lff_w1r", "lff_xf", "lff_xng", "lff_zsh", "lff_bstg",
               "lff_w2bf", "lff_b2c", "lff_b1p", "lff_b1e", "lff_bcol",
               "lff_gcol", "lff_ln_eps", "lff_ln_sq", "lff_ln_bias",
               "lff_ln_rstd", "lff_ln_sd", "lff_ln_var", "lff_ln_m2",
               "lff_ln_mu", "lff_ln_ss", "lff_ln_sums"):
        b.free(nm)


def emit_ln_to_z(b, aps, xin_hbm, z_out, ps, identb):
    s = mk_lnslab(b, "lz_ln")
    xg = b.sbuf("lz_xg", [128, 8, D], F32)
    zng = b.sbuf("lz_zng", [128, 8, D], F32)
    b.on("vector", lambda e: e.memset(s["eps"][:], 1e-5))
    b.bar_phase()
    z_flat = z_out
    for g in range(4):
        b.dma("sync", xg[:], xin_hbm[:, g * 8:(g + 1) * 8, :])
        b.bar_phase()
        emit_ln_group(b, s, xg, zng)
        for j in range(8):
            n = g * 8 + j
            b.dma("gpsimd", z_flat[n * 128:(n + 1) * 128, :], zng[:, j, :])
        b.bar_phase()
    for nm in ("lz_zng", "lz_xg", "lz_ln_eps", "lz_ln_sq", "lz_ln_bias",
               "lz_ln_rstd", "lz_ln_sd", "lz_ln_var", "lz_ln_m2", "lz_ln_mu",
               "lz_ln_ss", "lz_ln_sums"):
        b.free(nm)


def emit_final_add(b, aps, xin_hbm, xout_hbm, z_in):
    xg = b.sbuf("lfz_xg", [128, 8, D], F32)
    zg = b.sbuf("lfz_zg", [128, 8, D], F32)
    z_flat = z_in
    for g in range(4):
        b.dma("sync", xg[:], xin_hbm[:, g * 8:(g + 1) * 8, :])
        b.dma("sync", zg[:],
              z_flat[g * 1024:(g + 1) * 1024, :].rearrange(
                  "(a p) d -> p a d", p=128))
        b.bar_phase()
        b.on("vector", lambda e: e.tensor_tensor(
            xg.ap().rearrange("p a d -> p (a d)"),
            xg.ap().rearrange("p a d -> p (a d)"),
            zg.ap().rearrange("p a d -> p (a d)"), op=ALU.add))
        b.bar_phase()
        b.dma("gpsimd", xout_hbm[:, g * 8:(g + 1) * 8, :], xg[:])
        b.bar_phase()
    b.free("lfz_zg")
    b.free("lfz_xg")


def _l0ffn_aps(nc):
    return {
        "Wqkv": nc.dram_tensor("f_Wqkv", [6, D, 3072], F32, kind="ExternalInput").ap(),
        "Wo": nc.dram_tensor("f_Wo", [6, D, D], F32, kind="ExternalInput").ap(),
        "W1p": nc.dram_tensor("f_W1p", [6, D, 5632], F32, kind="ExternalInput").ap(),
        "W2p": nc.dram_tensor("f_W2p", [6, 2816, D], F32, kind="ExternalInput").ap(),
        "b1_pp": nc.dram_tensor("f_b1pp", [6, 128, 44], F32, kind="ExternalInput").ap(),
        "b2_pp": nc.dram_tensor("f_b2pp", [6, 128, 8], F32, kind="ExternalInput").ap(),
        "g_pp": nc.dram_tensor("f_gpp", [6, 128, 8], F32, kind="ExternalInput").ap(),
        "beta_pp": nc.dram_tensor("f_betapp", [6, 128, 8], F32, kind="ExternalInput").ap(),
        "qs_pp": nc.dram_tensor("f_qspp", [6, 128, 1], F32, kind="ExternalInput").ap(),
        "ks_pp": nc.dram_tensor("f_kspp", [6, 128, 1], F32, kind="ExternalInput").ap(),
        "rotab": nc.dram_tensor("f_rotab", [4, 128, T], F32, kind="ExternalInput").ap(),
        "perm": nc.dram_tensor("f_perm", [128, 128], BF16, kind="ExternalInput").ap(),
        "ind2": nc.dram_tensor("f_ind2", [128, 2], BF16, kind="ExternalInput").ap(),
        "bc2": nc.dram_tensor("f_bc2", [2, 128], BF16, kind="ExternalInput").ap(),
        "mask_i": nc.dram_tensor("f_maski", [128, 384], F32, kind="ExternalInput").ap(),
        "mask_0a": nc.dram_tensor("f_mask0a", [128, 384], F32, kind="ExternalInput").ap(),
        "mask_0b": nc.dram_tensor("f_mask0b", [128, 384], F32, kind="ExternalInput").ap(),
        "w1bf_hbm": nc.dram_tensor("f_w1bf", [128, NCH, 5632], BF16, kind="Internal").ap(),
    }


def l0ffn_host_inputs(inputs):
    Wqkv = np.asarray(inputs["L0_ffn_Wqkv"], np.float32)
    Wo = np.asarray(inputs["L0_ffn_Wo"], np.float32)
    W1 = np.asarray(inputs["L0_ffn_W1"], np.float32)
    b1 = np.asarray(inputs["L0_ffn_b1"], np.float32)
    W2 = np.asarray(inputs["L0_ffn_W2"], np.float32)
    b2 = np.asarray(inputs["L0_ffn_b2"], np.float32)
    g = np.asarray(inputs["L0_ffn_ln_g"], np.float32)
    beta = np.asarray(inputs["L0_ffn_ln_b"], np.float32)
    qsc = np.asarray(inputs["L0_ffn_q_scale"], np.float32)
    ksc = np.asarray(inputs["L0_ffn_k_scale"], np.float32)

    W1p = np.zeros((6, D, 5632), np.float32)
    W1p[:, :, 0:2730] = W1[:, :, 0:2730]
    W1p[:, :, 2816:2816 + 2730] = W1[:, :, 2730:5460]
    b1p = np.zeros((6, 5632), np.float32)
    b1p[:, 0:2730] = b1[:, 0:2730]
    b1p[:, 2816:2816 + 2730] = b1[:, 2730:5460]
    W2p = np.zeros((6, 2816, D), np.float32)
    W2p[:, 0:2730, :] = W2

    DIM_HEAD, WINDOW = 64, 256
    half = np.arange(0, DIM_HEAD, 2, dtype=np.float64)
    inv_freq = 1.0 / 10000.0 ** (half / DIM_HEAD)
    t = np.arange(T, dtype=np.float64)
    freqs = t[:, None] * inv_freq[None, :]
    freqs = np.concatenate([freqs, freqs], -1)
    base_scale = (half + 0.4 * DIM_HEAD) / (1.4 * DIM_HEAD)
    power = (t - T // 2) / (WINDOW // 2)
    scale = base_scale[None, :] ** power[:, None]
    scale = np.concatenate([scale, scale], -1)
    cos, sin = np.cos(freqs), np.sin(freqs)
    Aq = (cos * scale).T
    Bq = (sin * scale).T
    Ak = (cos / scale).T
    Bk = (sin / scale).T
    sig = np.where(np.arange(64) < 32, -1.0, 1.0)
    sgm = (np.arange(64) + 32) % 64
    Bqs = sig[sgm][:, None] * Bq[sgm, :]
    Bks = sig[sgm][:, None] * Bk[sgm, :]
    def dup(a):
        return np.vstack([a, a]).astype(np.float32)
    rotab = np.stack([dup(Aq) * 8.0, dup(Bqs) * 8.0, dup(Ak), dup(Bks)])

    permM = np.zeros((128, 128), np.float32)
    for m in range(128):
        sm = (m // 64) * 64 + ((m % 64) + 32) % 64
        permM[sm, m] = 1.0
    ind2 = np.zeros((128, 2), np.float32)
    ind2[0:64, 0] = 1.0
    ind2[64:128, 1] = 1.0
    bc2 = np.zeros((2, 128), np.float32)
    bc2[0, 0:64] = 1.0
    bc2[1, 64:128] = 1.0

    NEG = np.float32(-3.38e38)
    p = np.arange(128)[:, None]
    jl = np.arange(384)[None, :]
    mi = np.where((jl >= p) & (jl <= p + 256), 0.0, NEG).astype(np.float32)
    m0a = np.where((jl >= 256) & (jl <= 256 + p), 0.0, NEG).astype(np.float32)
    m0b = np.where((jl >= 128) & (jl <= 256 + p), 0.0, NEG).astype(np.float32)

    def pp(a, w):
        return np.ascontiguousarray(a.reshape(6, w, 128).transpose(0, 2, 1))

    return {
        "f_Wqkv": Wqkv, "f_Wo": Wo, "f_W1p": W1p, "f_W2p": W2p,
        "f_b1pp": pp(b1p, 44), "f_b2pp": pp(b2, 8), "f_gpp": pp(g, 8),
        "f_betapp": pp(beta, 8),
        "f_qspp": np.ascontiguousarray(
            np.concatenate([qsc, qsc], 1)[:, :, None]),
        "f_kspp": np.ascontiguousarray(
            np.concatenate([ksc, ksc], 1)[:, :, None]),
        "f_rotab": rotab, "f_perm": bfc(permM), "f_ind2": bfc(ind2),
        "f_bc2": bfc(bc2), "f_maski": mi, "f_mask0a": m0a, "f_mask0b": m0b,
    }


def build_l0ffn_layer_prog(li, first=False, last=False, debug_z2=False, debug_w=False):
    nc = bass.Bass("TRN2", target_bir_lowering=False)
    aps = _l0ffn_aps(nc)
    if first:
        xd = nc.dram_tensor("x", [T, D], F32, kind="ExternalInput").ap()
        zin = nc.dram_tensor("zmid", [T, D], F32, kind="Internal").ap()
    else:
        xd = nc.dram_tensor("x", [T, D], F32, kind="ExternalInput").ap()             if last else None
        zin = nc.dram_tensor("zin", [T, D], F32, kind="ExternalInput").ap()
    z2 = nc.dram_tensor("z2", [T, D], F32,
                        kind="ExternalOutput" if debug_z2 else "Internal").ap()
    if last:
        zout = nc.dram_tensor("zfin", [T, D], F32, kind="Internal").ap()
        xod = nc.dram_tensor("xo", [T, D], F32, kind="ExternalOutput").ap()
    else:
        zout = nc.dram_tensor("zout", [T, D], F32, kind="ExternalOutput").ap()
        xod = None
    identbd = nc.dram_tensor("identb", [128, 128], BF16, kind="ExternalInput").ap()
    identfd = nc.dram_tensor("identf", [128, 128], F32, kind="ExternalInput").ap()

    b = Bld(nc)
    identb = b.sbuf("identbS", [128, 128], BF16)
    identf = b.sbuf("identfS", [128, 128], F32)
    ps = b.psum("PS", [128, 4096], F32)
    b.dma("sync", identb[:], identbd)
    b.dma("sync", identf[:], identfd)
    b.bar_phase()
    if first:
        xin_hbm = xd.rearrange("(n p) d -> p n d", p=128)
        emit_ln_to_z(b, aps, xin_hbm, zin, ps, identb)
    dbg = None
    if debug_w:
        dbg = {
            "qpb": nc.dram_tensor("dbg_qpb", [128, NCH, 256], BF16, kind="ExternalOutput").ap(),
            "kc": nc.dram_tensor("dbg_kc", [128, NCH, 512], BF16, kind="ExternalOutput").ap(),
            "vc": nc.dram_tensor("dbg_vc", [128, 4, 1024], BF16, kind="ExternalOutput").ap(),
            "P16": nc.dram_tensor("dbg_P16", [128, 16, 384], BF16, kind="ExternalOutput").ap(),
            "ao": nc.dram_tensor("dbg_ao", [128, NCH, 256], BF16, kind="ExternalOutput").ap(),
            "uf": nc.dram_tensor("dbg_uf", [128, NCH, 256], BF16, kind="ExternalOutput").ap(),
            "tabs": nc.dram_tensor("dbg_tabs", [128, 4, 256], F32, kind="ExternalOutput").ap(),
            "tb": nc.dram_tensor("dbg_tb", [128, NCH, 256], BF16, kind="ExternalOutput").ap(),
            "t1": nc.dram_tensor("dbg_t1", [128, NCH, 256], F32, kind="ExternalOutput").ap(),
            "qf": nc.dram_tensor("dbg_qf", [128, NCH, 256], F32, kind="ExternalOutput").ap(),
            "rq2": nc.dram_tensor("dbg_rq2", [2, 2048], BF16, kind="ExternalOutput").ap(),
            "fbc": nc.dram_tensor("dbg_fbc", [128, 256], F32, kind="ExternalOutput").ap(),
            "qspp_d": nc.dram_tensor("dbg_qspp", [128, 1], F32, kind="ExternalOutput").ap(),
        }
        dbg["_fbcs"] = b.sbuf("dbg_fbcs", [128, 256], F32)
    emit_l0ffn_attnhalf(b, aps, li, ps, identb, identf, zin, z2, dbg)
    emit_l0ffn_ffnhalf(b, aps, li, ps, identb, identf, z2, zout)
    if last:
        xin_hbm = xd.rearrange("(n p) d -> p n d", p=128)
        xout_hbm = xod.rearrange("(n p) d -> p n d", p=128)
        emit_final_add(b, aps, xin_hbm, xout_hbm, zout)
    b.finalize()
    return nc


def l1attn_host_inputs(inputs):
    Wp = np.asarray(inputs["L1_attn_W_proj"], np.float32)
    Wd = np.asarray(inputs["L1_attn_W_delta"], np.float32)
    bd = np.asarray(inputs["L1_attn_b_delta"], np.float32)
    A1 = np.asarray(inputs["L1_attn_A"], np.float32)[0, 0]
    B1 = np.asarray(inputs["L1_attn_Bp"], np.float32)[0, :, 0]
    Dv = np.asarray(inputs["L1_attn_D"], np.float32)
    return {
        "WpT": np.ascontiguousarray(Wp.T),
        "WdT": np.ascontiguousarray(Wd.T),
        "bdel_pp": np.ascontiguousarray(bd.reshape(8, 128).T),
        "A1_pp": np.ascontiguousarray(A1.reshape(8, 128).T),
        "B1_pp": np.ascontiguousarray(B1.reshape(8, 128).T),
        "Dv": np.broadcast_to(Dv.reshape(1, 1), (128, 1)).copy(),
    }


def common_host_inputs():
    return {
        "identb": bfc(np.eye(128, dtype=np.float32)),
        "identf": np.eye(128, dtype=np.float32),
    }



def build_l1attn_prog(debug=False):
    nc = bass.Bass("TRN2", target_bir_lowering=False)
    xd = nc.dram_tensor("x", [T, D], F32, kind="ExternalInput").ap()
    aps = {
        "WpT": nc.dram_tensor("WpT", [D, 4096], F32, kind="ExternalInput").ap(),
        "WdT": nc.dram_tensor("WdT", [D, D], F32, kind="ExternalInput").ap(),
        "bdel_pp": nc.dram_tensor("bdel_pp", [128, 8], F32, kind="ExternalInput").ap(),
        "A1_pp": nc.dram_tensor("A1_pp", [128, 8], F32, kind="ExternalInput").ap(),
        "B1_pp": nc.dram_tensor("B1_pp", [128, 8], F32, kind="ExternalInput").ap(),
        "Dv": nc.dram_tensor("Dv", [128, 1], F32, kind="ExternalInput").ap(),
    }
    aps["din_hbm"] = nc.dram_tensor("din_hbm", [128, NCH, T], BF16,
                                    kind="ExternalOutput" if debug else "Internal").ap()
    aps["_debug"] = debug
    identbd = nc.dram_tensor("identb", [128, 128], BF16, kind="ExternalInput").ap()
    identfd = nc.dram_tensor("identf", [128, 128], F32, kind="ExternalInput").ap()
    xod = nc.dram_tensor("xo", [T, D], F32, kind="ExternalOutput").ap()

    b = Bld(nc)
    identb = b.sbuf("identbS", [128, 128], BF16)
    identf = b.sbuf("identfS", [128, 128], F32)
    ps = b.psum("PS", [128, 4096], F32)
    b.dma("sync", identb[:], identbd)
    b.dma("sync", identf[:], identfd)
    b.bar_phase()
    if debug:
        aps["xn_dbg"] = nc.dram_tensor("xn_dbg", [128, NCH, T], BF16,
                                       kind="ExternalOutput").ap()
        aps["ucsum_dbg"] = nc.dram_tensor("ucsum_dbg", [128, 16], F32,
                                          kind="ExternalOutput").ap()
    xin_hbm = xd.rearrange("(n p) d -> p n d", p=128)
    xout_hbm = xod.rearrange("(n p) d -> p n d", p=128)
    emit_l1attn(b, aps, xin_hbm, xout_hbm, ps, identb, identf)
    b.finalize()
    return nc


def build_fused_prog():
    nc = bass.Bass("TRN2", target_bir_lowering=False)
    xd = nc.dram_tensor("x", [T, D], F32, kind="ExternalInput").ap()
    xod = nc.dram_tensor("xo", [T, D], F32, kind="ExternalOutput").ap()
    identbd = nc.dram_tensor("identb", [128, 128], BF16, kind="ExternalInput").ap()
    identfd = nc.dram_tensor("identf", [128, 128], F32, kind="ExternalInput").ap()

    x1 = nc.dram_tensor("x1_h", [T, D], F32, kind="Internal").ap()
    x2 = nc.dram_tensor("x2_h", [T, D], F32, kind="Internal").ap()
    x3 = nc.dram_tensor("x3_h", [T, D], F32, kind="Internal").ap()
    zA = nc.dram_tensor("zA_h", [T, D], F32, kind="Internal").ap()
    z2 = nc.dram_tensor("z2_h", [T, D], F32, kind="Internal").ap()

    rp_h = nc.dram_tensor("rp_hbm", [16, 32, 128, 512], F32, kind="Internal")
    aps0 = {
        "WqkvT": nc.dram_tensor("WqkvT", [D, 3072], F32, kind="ExternalInput").ap(),
        "bqkv_pp": nc.dram_tensor("bqkv_pp", [128, 24], F32, kind="ExternalInput").ap(),
        "wrx2": nc.dram_tensor("wrx2", [128, 512], F32, kind="ExternalInput").ap(),
        "brx_pp": nc.dram_tensor("brx_pp", [128, 512], F32, kind="ExternalInput").ap(),
        "rp_hbm": rp_h.ap(),
        "rp_hbm_h": rp_h,
        "qkv_hbm": nc.dram_tensor("qkv_hbm", [2, 3, 128, 4, T], BF16,
                                  kind="Internal").ap(),
    }
    apsf = _l0ffn_aps(nc)
    aps1a = {
        "WpT": nc.dram_tensor("WpT", [D, 4096], F32, kind="ExternalInput").ap(),
        "WdT": nc.dram_tensor("WdT", [D, D], F32, kind="ExternalInput").ap(),
        "bdel_pp": nc.dram_tensor("bdel_pp", [128, 8], F32, kind="ExternalInput").ap(),
        "A1_pp": nc.dram_tensor("A1_pp", [128, 8], F32, kind="ExternalInput").ap(),
        "B1_pp": nc.dram_tensor("B1_pp", [128, 8], F32, kind="ExternalInput").ap(),
        "Dv": nc.dram_tensor("Dv", [128, 1], F32, kind="ExternalInput").ap(),
        "din_hbm": nc.dram_tensor("din_hbm", [128, NCH, T], BF16,
                                  kind="Internal").ap(),
    }
    aps1f = {
        "WgT": nc.dram_tensor("WgT", [D, 8], F32, kind="ExternalInput").ap(),
        "WeT": nc.dram_tensor("WeT", [8, D, D], F32, kind="ExternalInput").ap(),
        "be": nc.dram_tensor("be", [8, D], F32, kind="ExternalInput").ap(),
        "webf_hbm": nc.dram_tensor("webf_hbm", [128, 8, NCH, D], BF16,
                                   kind="Internal").ap(),
    }

    b = Bld(nc)
    identb = b.sbuf("identbS", [128, 128], BF16)
    identf = b.sbuf("identfS", [128, 128], F32)
    ps = b.psum("PS", [128, 4096], F32)
    b.dma("sync", identb[:], identbd)
    b.dma("sync", identf[:], identfd)
    b.bar_phase()

    xin = xd.rearrange("(n p) d -> p n d", p=128)
    x1v = x1.rearrange("(n p) d -> p n d", p=128)
    x2v = x2.rearrange("(n p) d -> p n d", p=128)
    x3v = x3.rearrange("(n p) d -> p n d", p=128)
    xov = xod.rearrange("(n p) d -> p n d", p=128)

    emit_l0attn(b, aps0, xin, x1v, ps, identb, identf)
    emit_ln_to_z(b, apsf, x1v, zA, ps, identb)
    for li in range(6):
        emit_l0ffn_attnhalf(b, apsf, li, ps, identb, identf, zA, z2)
        emit_l0ffn_ffnhalf(b, apsf, li, ps, identb, identf, z2, zA)
    emit_final_add(b, apsf, x1v, x2v, zA)
    emit_l1attn(b, aps1a, x2v, x3v, ps, identb, identf)
    emit_l1ffn(b, aps1f, x3v, xov, ps, identb, identf)
    b.finalize()
    return nc


_PROGS = {}


def _prog(key, fn):
    if key not in _PROGS:
        _PROGS[key] = fn()
    return _PROGS[key]


FUSED = True


def kernel(**inputs):
    if FUSED:
        return kernel_fused(**inputs)
    return kernel_unfused(**inputs)


def kernel_fused(**inputs):
    x = np.ascontiguousarray(np.asarray(inputs["L0_attn_x"], np.float32))
    ncore = 8
    shared = {}
    shared.update(common_host_inputs())
    shared.update(l0attn_host_inputs(inputs))
    shared.update(l0ffn_host_inputs(inputs))
    shared.update(l1attn_host_inputs(inputs))
    shared.update(l1ffn_host_inputs(inputs))
    nc = _prog("fused", build_fused_prog)
    in_maps = []
    for c in range(ncore):
        m = dict(shared)
        m["x"] = x[c]
        in_maps.append(m)
    res = run_bass_kernel_spmd(nc, in_maps, core_ids=list(range(ncore)))
    return np.stack([np.asarray(res.results[c]["xo"]) for c in range(ncore)])


def kernel_unfused(**inputs):
    x = np.ascontiguousarray(np.asarray(inputs["L0_attn_x"], np.float32))
    ncore = 8
    com = common_host_inputs()
    h0a = l0attn_host_inputs(inputs)
    h0f = l0ffn_host_inputs(inputs)
    h1a = l1attn_host_inputs(inputs)
    h1f = l1ffn_host_inputs(inputs)

    def launch(nc, per_core_extra, shared):
        in_maps = []
        for c in range(ncore):
            m = dict(shared)
            m.update(com)
            m.update(per_core_extra[c])
            in_maps.append(m)
        res = run_bass_kernel_spmd(nc, in_maps, core_ids=list(range(ncore)))
        return res.results

    nc = _prog("l0a", build_l0attn_prog)
    r = launch(nc, [{"x": x[c]} for c in range(ncore)], h0a)
    x1 = [np.asarray(r[c]["xo"]) for c in range(ncore)]

    z = None
    for li in range(6):
        first, last = li == 0, li == 5
        nc = _prog(f"l0f{li}", lambda li=li, f=first, l=last:
                   build_l0ffn_layer_prog(li, first=f, last=l))
        if first:
            per = [{"x": x1[c]} for c in range(ncore)]
        elif last:
            per = [{"x": x1[c], "zin": z[c]} for c in range(ncore)]
        else:
            per = [{"zin": z[c]} for c in range(ncore)]
        r = launch(nc, per, h0f)
        if last:
            x2 = [np.asarray(r[c]["xo"]) for c in range(ncore)]
        else:
            z = [np.asarray(r[c]["zout"]) for c in range(ncore)]

    nc = _prog("l1a", build_l1attn_prog)
    r = launch(nc, [{"x": x2[c]} for c in range(ncore)], h1a)
    x3 = [np.asarray(r[c]["xo"]) for c in range(ncore)]

    nc = _prog("l1f", build_l1ffn_prog)
    r = launch(nc, [{"x": x3[c]} for c in range(ncore)], h1f)
    out = np.stack([np.asarray(r[c]["xo"]) for c in range(ncore)])
    return out


# version 08 (final)
# speedup vs baseline: 1.0320x; 1.0298x over previous
import contextlib
import numpy as np
import ml_dtypes
import concourse.bass as bass
import concourse.mybir as mybir
from concourse.bass_utils import run_bass_kernel_spmd
from concourse.ap import AP

F32 = mybir.dt.float32
BF16 = mybir.dt.bfloat16
AF = mybir.ActivationFunctionType
ALU = mybir.AluOpType
AX = mybir.AxisListType

T = 4096
D = 1024
NT = 32
NCH = 8
ENGINES = ("sync", "scalar", "vector", "tensor", "gpsimd")


def bfc(x):
    return np.ascontiguousarray(x).astype(ml_dtypes.bfloat16)


class Bld:

    def __init__(self, nc):
        self.nc = nc
        self.stack = contextlib.ExitStack()
        self.bar = self.sem("bar")
        self.dma_sem = {e: self.sem("dsem_" + e) for e in ENGINES}
        self.dma_cnt = {e: 0 for e in ENGINES}
        self.phases = []
        self.open_bufs = {}
        self._new()
        self.dum = {e: self.sbuf("_dum_" + e, [1, 2], F32)
                    for e in ("vector", "scalar", "gpsimd")}

    def sem(self, name):
        return self.stack.enter_context(self.nc.semaphore(name))

    _namectr = [0]

    def sbuf(self, name, shape, dtype):
        assert name not in self.open_bufs, name
        Bld._namectr[0] += 1
        cm = self.nc.sbuf_tensor(f"{name}_N{Bld._namectr[0]}", list(shape), dtype)
        t = cm.__enter__()
        self.open_bufs[name] = cm
        return t

    def free(self, name):
        cm = self.open_bufs.pop(name)
        cm.__exit__(None, None, None)

    def psum(self, name, shape, dtype):
        cm = self.nc.psum_tensor(name, list(shape), dtype)
        t = cm.__enter__()
        self.open_bufs[name] = cm
        return t

    def _new(self):
        self.cur = {e: [] for e in ENGINES}
        self.cur["_dmawm"] = {}
        self.phases.append(self.cur)

    def bar_phase(self):
        self._new()

    def on(self, eng, fn):
        self.cur[eng].append(fn)

    def dma(self, eng, out, in_, **kw):
        semh = self.dma_sem[eng]
        self.dma_cnt[eng] += 16
        tgt = self.dma_cnt[eng]

        def f(e):
            e.dma_start(out=out, in_=in_, **kw).then_inc(semh, 16)
        self.on(eng, f)
        self.cur["_dmawm"][eng] = tgt

    def dmat(self, eng, out, in_):
        semh = self.dma_sem[eng]
        self.dma_cnt[eng] += 16
        tgt = self.dma_cnt[eng]

        def f(e):
            e.dma_start_transpose(out, in_).then_inc(semh, 16)
        self.on(eng, f)
        self.cur["_dmawm"][eng] = tgt

    def finalize(self):
        dum = self.dum
        block = self.stack.enter_context(self.nc.Block())
        bar = self.bar
        dma_sem = self.dma_sem
        phases = self.phases

        def mk(eng_name):
            def body(eng):
                for pi, ph in enumerate(phases):
                    last = None
                    for fn in ph[eng_name]:
                        r = fn(eng)
                        if r is not None:
                            last = r
                    wm = ph["_dmawm"].get(eng_name)
                    if wm is not None:
                        eng.wait_ge(dma_sem[eng_name], wm)
                        last = None
                    if ph[eng_name] and eng_name == "vector":
                        eng.memset(dum[eng_name][:], 0.0).then_inc(bar, 1)
                    elif ph[eng_name] and eng_name == "gpsimd":
                        eng.memset(dum[eng_name][:], 0.0).then_inc(bar, 1)
                    elif ph[eng_name] and eng_name == "scalar":
                        eng.copy(dum[eng_name][:], dum[eng_name][:]).then_inc(bar, 1)
                    elif eng_name == "tensor" and last is not None:
                        last.then_inc(bar, 1)
                    else:
                        eng.nop().then_inc(bar, 1)
                    eng.wait_ge(bar, len(ENGINES) * (pi + 1))
            return body

        for eng_name in ENGINES:
            getattr(block, eng_name)(mk(eng_name))
        return len(phases)



def mk_lnslab(b, tag):
    return {
        "sums": b.sbuf(tag + "_sums", [128, 8], F32),
        "ss": b.sbuf(tag + "_ss", [128, 8], F32),
        "mu": b.sbuf(tag + "_mu", [128, 8], F32),
        "m2": b.sbuf(tag + "_m2", [128, 8], F32),
        "var": b.sbuf(tag + "_var", [128, 8], F32),
        "sd": b.sbuf(tag + "_sd", [128, 8], F32),
        "rstd": b.sbuf(tag + "_rstd", [128, 8], F32),
        "bias": b.sbuf(tag + "_bias", [128, 8], F32),
        "sq": b.sbuf(tag + "_sq", [128, D], F32),
        "eps": b.sbuf(tag + "_eps", [128, 1], F32),
    }


def emit_ln_group(b, s, xg, xng, nj=8, eps=1e-5):
    for j in range(nj):
        b.on("vector", (lambda jj: lambda e: e.reduce_sum(
            s["sums"][:, jj:jj + 1], xg[:, jj, :], axis=AX.X))(j))
        b.on("vector", (lambda jj: lambda e: e.tensor_tensor(
            s["sq"][:], xg[:, jj, :], xg[:, jj, :], op=ALU.mult))(j))
        b.on("vector", (lambda jj: lambda e: e.reduce_sum(
            s["ss"][:, jj:jj + 1], s["sq"][:], axis=AX.X))(j))
    b.on("vector", lambda e: e.tensor_scalar(s["mu"][:, :nj], s["sums"][:, :nj],
                                             1.0 / D, 0.0, op0=ALU.mult,
                                             op1=ALU.add))
    b.bar_phase()
    b.on("vector", lambda e: e.tensor_tensor(s["m2"][:, :nj], s["mu"][:, :nj],
                                             s["mu"][:, :nj], op=ALU.mult))
    b.bar_phase()
    b.on("vector", lambda e: e.scalar_tensor_tensor(
        s["var"][:, :nj], s["ss"][:, :nj], 1.0 / D, s["m2"][:, :nj],
        op0=ALU.mult, op1=ALU.subtract))
    b.bar_phase()
    b.on("scalar", lambda e: e.activation(s["sd"][:, :nj], s["var"][:, :nj],
                                          AF.Sqrt, bias=s["eps"][:]))
    b.bar_phase()
    b.on("vector", lambda e: e.reciprocal(s["rstd"][:, :nj], s["sd"][:, :nj]))
    b.bar_phase()
    b.on("vector", lambda e: e.scalar_tensor_tensor(
        s["bias"][:, :nj], s["mu"][:, :nj], -1.0, s["rstd"][:, :nj],
        op0=ALU.mult, op1=ALU.mult))
    b.bar_phase()
    for j in range(nj):
        b.on("vector", (lambda jj: lambda e: e.tensor_scalar(
            xng[:, jj, :], xg[:, jj, :], s["rstd"][:, jj:jj + 1],
            s["bias"][:, jj:jj + 1], op0=ALU.mult, op1=ALU.add))(j))
    b.bar_phase()


def emit_tok2feat(b, ps, identb, xng, feat_slab, n0, nj=8):
    for j in range(nj):
        pbf = ps[:, j * 512:(j + 1) * 512].bitcast(BF16)

        def mkt(jj, pb):
            def f(e):
                r = None
                for c in range(NCH):
                    r = e.transpose(pb[:, c * 128:(c + 1) * 128],
                                    xng[:, jj, c * 128:(c + 1) * 128], identb[:])
                return r
            return f
        b.on("tensor", mkt(j, pbf))
    b.bar_phase()
    for j in range(nj):
        pbf = ps[:, j * 512:(j + 1) * 512].bitcast(BF16)
        n = n0 + j

        def mke(nn, pb):
            def f(e):
                e.activation(
                    feat_slab[:, :, nn * 128:(nn + 1) * 128],
                    pb.rearrange("p (c t) -> p c t", c=NCH), AF.Copy)
            return f
        b.on("scalar", mke(n, pbf))
    b.bar_phase()


def emit_wload(b, dst_bf, src_hbm_view, stage, nchunks=NCH, scale_pp=None):
    wcols = stage.ap().shape[1]
    for c in range(nchunks):
        b.dma("sync", stage[:], src_hbm_view[:, c, :])
        b.bar_phase()
        if scale_pp is None:
            b.on("vector", (lambda cc: lambda e: e.tensor_copy(
                dst_bf[:, cc, :], stage[:]))(c))
        else:
            b.on("vector", (lambda cc: lambda e: e.tensor_scalar(
                dst_bf[:, cc, :], stage[:], scale_pp[:, cc:cc + 1], None,
                op0=ALU.mult))(c))
        b.bar_phase()



def emit_l1attn(b, aps, xin_hbm, xout_hbm, ps, identb, identf):
    dstg = b.sbuf("l1a_dstg", [128, NCH, 512], BF16)
    bdel = b.sbuf("l1a_bdel", [128, 8], F32)
    a1 = b.sbuf("l1a_a1", [128, 8], F32)
    b1 = b.sbuf("l1a_b1", [128, 8], F32)
    dv = b.sbuf("l1a_dv", [128, 1], F32)
    uc_p = b.sbuf("l1a_ucp", [128, 8, 8], F32)
    cs_p = b.sbuf("l1a_csp", [128, 8, 8], F32)
    uc = b.sbuf("l1a_uc", [128, 8], F32)
    csum = b.sbuf("l1a_csum", [128, 8], F32)
    b1uc = b.sbuf("l1a_b1uc", [128, 8], F32)
    carry = b.sbuf("l1a_carry", [128, 8], F32)
    ones512 = b.sbuf("l1a_ones", [128, 512], F32)
    s = mk_lnslab(b, "l1a_ln")
    xn = b.sbuf("l1a_xn", [128, NCH, T], BF16)

    b.dma("sync", bdel[:], aps["bdel_pp"])
    b.dma("sync", a1[:], aps["A1_pp"])
    b.dma("sync", b1[:], aps["B1_pp"])
    b.dma("sync", dv[:], aps["Dv"])
    b.on("vector", lambda e: e.memset(ones512[:], 1.0))
    b.on("vector", lambda e: e.memset(carry[:], 0.0))
    b.on("vector", lambda e: e.memset(s["eps"][:], 1e-5))
    b.bar_phase()

    xg = b.sbuf("l1a_xg", [128, 8, D], F32)
    xng = b.sbuf("l1a_xng", [128, 8, D], BF16)
    for g in range(4):
        b.dma("sync", xg[:], xin_hbm[:, g * 8:(g + 1) * 8, :])
        b.bar_phase()
        emit_ln_group(b, s, xg, xng)
        emit_tok2feat(b, ps, identb, xng, xn, g * 8)
    if aps.get("_debug"):
        b.dma("gpsimd", aps["xn_dbg"], xn[:])
        b.bar_phase()
    b.free("l1a_xng")
    b.free("l1a_xg")

    wstage = b.sbuf("l1a_wstage", [128, 1024], F32)
    wa = b.sbuf("l1a_wa", [128, NCH, 2048], BF16)
    wr = b.sbuf("l1a_wr", [128, NCH, 1024], BF16)
    csc = b.sbuf("l1a_csc", [128, 512], F32)
    xnf = b.sbuf("l1a_xnf", [128, 512], F32)
    rsl = b.sbuf("l1a_rsl", [128, 8, 512], F32)
    xio = b.sbuf("l1a_xio", [128, 4, D], F32)
    wp_hbm = aps["WpT"].rearrange("(c p) o -> p c o", p=128)
    for c in range(NCH):
        for (dst, lo) in ((wa[:, c, 0:1024], 0), (wa[:, c, 1024:2048], 2048),
                          (wr[:, c, :], 3072)):
            b.dma("sync", wstage[:], wp_hbm[:, c, lo:lo + 1024])
            b.bar_phase()
            b.on("vector", (lambda d: lambda e: e.tensor_copy(d, wstage[:]))(dst))
            b.bar_phase()

    def proj_mms(wsl, ntn):
        def f(e):
            r = None
            for mc in range(8):
                for dic in range(NCH):
                    r = e.matmul(ps[:, mc * 512:(mc + 1) * 512],
                                 wsl(mc, dic),
                                 xn[:, dic, ntn * 512:(ntn + 1) * 512],
                                 start=(dic == 0), stop=(dic == NCH - 1))
            return r
        return f

    for nt in range(8):
        b.on("tensor", proj_mms(
            lambda mc, dic: wa[:, dic, mc * 128:(mc + 1) * 128], nt))
        b.bar_phase()
        for mc in range(8):
            b.on("scalar", (lambda m: lambda e: e.activation(
                dstg[:, m, :], ps[:, m * 512:(m + 1) * 512], AF.Copy))(mc))
        b.bar_phase()
        b.dma("gpsimd", aps["din_hbm"][:, :, nt * 512:(nt + 1) * 512], dstg[:])
        b.bar_phase()
        b.on("tensor", proj_mms(
            lambda mc, dic: wa[:, dic, 1024 + mc * 128:1024 + (mc + 1) * 128], nt))
        b.bar_phase()
        for mc in range(8):
            b.on("vector", (lambda m: lambda e: e.tensor_copy(
                csc[:], ps[:, m * 512:(m + 1) * 512]))(mc))
            b.on("vector", (lambda m, ntn: lambda e: e.tensor_copy(
                xnf[:], xn[:, m, ntn * 512:(ntn + 1) * 512]))(mc, nt))
            b.bar_phase()
            b.on("vector", (lambda m, ntn: lambda e: e.tensor_tensor(
                s["sq"][:, 0:512], csc[:], xnf[:], op=ALU.mult))(mc, nt))
            b.on("vector", (lambda m, ntn: lambda e: e.reduce_sum(
                uc_p[:, m, ntn:ntn + 1], s["sq"][:, 0:512], axis=AX.X))(mc, nt))
            b.on("vector", (lambda m, ntn: lambda e: e.reduce_sum(
                cs_p[:, m, ntn:ntn + 1], csc[:], axis=AX.X))(mc, nt))
            b.bar_phase()
        b.on("tensor", proj_mms(
            lambda mc, dic: wr[:, dic, mc * 128:(mc + 1) * 128], nt))
        b.bar_phase()
        for mc in range(8):
            b.on("vector", (lambda m: lambda e: e.tensor_scalar(
                rsl[:, m, :], ps[:, m * 512:(m + 1) * 512], dv[:, 0:1], None,
                op0=ALU.mult))(mc))
        b.bar_phase()
        b.dma("sync", xio[:], xin_hbm[:, nt * 4:(nt + 1) * 4, :])

        def tr_all(e):
            r = None
            for st in range(4):
                for oc in range(8):
                    r = e.transpose(
                        ps[:, st * 1024 + oc * 128: st * 1024 + (oc + 1) * 128],
                        rsl[:, oc, st * 128:(st + 1) * 128], identf[:])
            return r
        b.on("tensor", tr_all)
        b.bar_phase()
        for st in range(4):
            b.on("vector", (lambda ss: lambda e: e.tensor_tensor(
                xio[:, ss, :], xio[:, ss, :], ps[:, ss * 1024:(ss + 1) * 1024],
                op=ALU.add))(st))
        b.bar_phase()
        b.dma("gpsimd", xout_hbm[:, nt * 4:(nt + 1) * 4, :], xio[:])
        b.bar_phase()

    for c in range(NCH):
        b.on("vector", (lambda cc: lambda e: e.reduce_sum(
            uc[:, cc:cc + 1], uc_p[:, cc, :], axis=AX.X))(c))
        b.on("vector", (lambda cc: lambda e: e.reduce_sum(
            csum[:, cc:cc + 1], cs_p[:, cc, :], axis=AX.X))(c))
    b.bar_phase()
    b.on("vector", lambda e: e.tensor_tensor(b1uc[:], b1[:], uc[:], op=ALU.mult))
    b.bar_phase()
    if aps.get("_debug"):
        b.dma("gpsimd", aps["ucsum_dbg"][:, 0:8], uc[:])
        b.dma("gpsimd", aps["ucsum_dbg"][:, 8:16], csum[:])
        b.bar_phase()

    for nm in ("l1a_xio", "l1a_rsl", "l1a_xnf", "l1a_csc", "l1a_wr", "l1a_wa",
               "l1a_wstage", "l1a_xn"):
        b.free(nm)

    wstage2 = b.sbuf("l1a_wstage2", [128, 1024], F32)
    wd = b.sbuf("l1a_wd", [128, NCH, 1024], BF16)
    spA = b.sbuf("l1a_spA", [128, 8, 512], F32)
    spB = b.sbuf("l1a_spB", [128, 8, 512], F32)
    dlt = b.sbuf("l1a_dlt", [128, 8, 512], F32)
    ysl = b.sbuf("l1a_ysl", [128, 8, 512], F32)
    xio2 = b.sbuf("l1a_xio2", [128, 4, D], F32)
    wd_hbm = aps["WdT"].rearrange("(c p) o -> p c o", p=128)
    for c in range(NCH):
        b.dma("sync", wstage2[:], wd_hbm[:, c, :])
        b.bar_phase()
        b.on("vector", (lambda cc: lambda e: e.tensor_copy(
            wd[:, cc, :], wstage2[:]))(c))
        b.bar_phase()

    for nt in range(8):
        b.dma("sync", dstg[:], aps["din_hbm"][:, :, nt * 512:(nt + 1) * 512])
        b.bar_phase()

        def mmd(e):
            r = None
            for oc in range(8):
                for dic in range(NCH):
                    r = e.matmul(ps[:, oc * 512:(oc + 1) * 512],
                                 wd[:, dic, oc * 128:(oc + 1) * 128],
                                 dstg[:, dic, :],
                                 start=(dic == 0), stop=(dic == NCH - 1))
            return r
        b.on("tensor", mmd)
        b.bar_phase()
        for oc in range(8):
            b.on("scalar", (lambda o: lambda e: e.activation(
                spA[:, o, :], ps[:, o * 512:(o + 1) * 512], AF.Abs,
                bias=bdel[:, o:o + 1]))(oc))
        b.bar_phase()
        b.on("scalar", lambda e: e.activation(
            spB[:].rearrange("p c t -> p (c t)"),
            spA[:].rearrange("p c t -> p (c t)"), AF.Exp, scale=-1.0))
        b.bar_phase()
        b.on("vector", lambda e: e.tensor_scalar(
            spB[:].rearrange("p c t -> p (c t)"),
            spB[:].rearrange("p c t -> p (c t)"), 1.0, 1.0,
            op0=ALU.mult, op1=ALU.add))
        b.bar_phase()
        b.on("scalar", lambda e: e.activation(
            spA[:].rearrange("p c t -> p (c t)"),
            spB[:].rearrange("p c t -> p (c t)"), AF.Ln))
        b.bar_phase()
        for oc in range(8):
            b.on("scalar", (lambda o: lambda e: e.activation(
                spB[:, o, :], ps[:, o * 512:(o + 1) * 512], AF.Relu,
                bias=bdel[:, o:o + 1]))(oc))
        b.bar_phase()
        b.on("vector", lambda e: e.tensor_tensor(
            dlt[:].rearrange("p c t -> p (c t)"),
            spA[:].rearrange("p c t -> p (c t)"),
            spB[:].rearrange("p c t -> p (c t)"), op=ALU.add))
        b.bar_phase()
        for oc in range(8):
            b.on("scalar", (lambda o: lambda e: e.activation(
                spA[:, o, :], dlt[:, o, :], AF.Exp, scale=a1[:, o:o + 1]))(oc))
        b.bar_phase()
        for oc in range(8):
            b.on("vector", (lambda o: lambda e: e.tensor_scalar(
                ysl[:, o, :], spA[:, o, :], b1uc[:, o:o + 1], None,
                op0=ALU.mult))(oc))
            b.on("vector", (lambda o: lambda e: e.tensor_scalar(
                spB[:, o, :], dlt[:, o, :], b1[:, o:o + 1], None,
                op0=ALU.mult))(oc))
        b.bar_phase()
        for oc in range(8):
            b.on("vector", (lambda o: lambda e: e.tensor_tensor_scan(
                spA[:, o, :], ones512[:], spB[:, o, :], carry[:, o:o + 1],
                op0=ALU.mult, op1=ALU.add))(oc))
        b.bar_phase()
        for oc in range(8):
            b.on("vector", (lambda o: lambda e: e.tensor_copy(
                carry[:, o:o + 1], spA[:, o, 511:512]))(oc))
        b.bar_phase()
        for oc in range(8):
            b.on("vector", (lambda o: lambda e: e.scalar_tensor_tensor(
                ysl[:, o, :], spA[:, o, :], csum[:, o:o + 1], ysl[:, o, :],
                op0=ALU.mult, op1=ALU.add))(oc))
        b.bar_phase()
        b.dma("sync", xio2[:], xout_hbm[:, nt * 4:(nt + 1) * 4, :])

        def tr_all2(e):
            r = None
            for st in range(4):
                for oc in range(8):
                    r = e.transpose(
                        ps[:, st * 1024 + oc * 128: st * 1024 + (oc + 1) * 128],
                        ysl[:, oc, st * 128:(st + 1) * 128], identf[:])
            return r
        b.on("tensor", tr_all2)
        b.bar_phase()
        for st in range(4):
            b.on("vector", (lambda ss: lambda e: e.tensor_tensor(
                xio2[:, ss, :], xio2[:, ss, :], ps[:, ss * 1024:(ss + 1) * 1024],
                op=ALU.add))(st))
        b.bar_phase()
        b.dma("gpsimd", xout_hbm[:, nt * 4:(nt + 1) * 4, :], xio2[:])
        b.bar_phase()

    for nm in ("l1a_xio2", "l1a_ysl", "l1a_dlt", "l1a_spB", "l1a_spA", "l1a_wd",
               "l1a_wstage2"):
        b.free(nm)
    for nm in ("l1a_ln_eps", "l1a_ln_sq", "l1a_ln_bias", "l1a_ln_rstd",
               "l1a_ln_sd", "l1a_ln_var", "l1a_ln_m2", "l1a_ln_mu", "l1a_ln_ss",
               "l1a_ln_sums", "l1a_ones", "l1a_carry", "l1a_b1uc", "l1a_csum",
               "l1a_uc", "l1a_csp", "l1a_ucp", "l1a_dv", "l1a_b1", "l1a_a1",
               "l1a_bdel", "l1a_dstg"):
        b.free(nm)



def emit_l1ffn(b, aps, xin_hbm, xout_hbm, ps, identb, identf, stop_after=None):
    logT = b.sbuf("l1f_logT", [8, T], F32)
    cwT = b.sbuf("l1f_cwT", [8, T], F32)
    cwTb = b.sbuf("l1f_cwTb", [8, T], BF16)
    beb = b.sbuf("l1f_beb", [8, D], BF16)
    bef = b.sbuf("l1f_bef", [8, D], F32)
    wg_hi = b.sbuf("l1f_wghi", [128, NCH, 8], BF16)
    wg_lo = b.sbuf("l1f_wglo", [128, NCH, 8], BF16)
    wg_f = b.sbuf("l1f_wgf", [128, NCH, 8], F32)
    wg_t = b.sbuf("l1f_wgt", [128, NCH, 8], F32)
    onesb = b.sbuf("l1f_onesb", [1, 128], BF16)
    s = mk_lnslab(b, "l1f_ln")
    m1 = b.sbuf("l1f_m1", [128, 8], F32)
    m2 = b.sbuf("l1f_m2", [128, 8], F32)
    sg = b.sbuf("l1f_sg", [128, 8], F32)
    w1 = b.sbuf("l1f_w1", [128, 8], F32)
    w2 = b.sbuf("l1f_w2", [128, 8], F32)
    msk1 = b.sbuf("l1f_msk1", [128, 8, 8], F32)
    msk2 = b.sbuf("l1f_msk2", [128, 8, 8], F32)
    lgt = b.sbuf("l1f_lgt", [128, 8, 8], F32)
    cwg = b.sbuf("l1f_cwg", [128, 8, 8], F32)
    xn = b.sbuf("l1f_xn", [128, NCH, T], BF16)

    b.dma("sync", wg_f[:], aps["WgT"].rearrange("(c p) e -> p c e", p=128))
    b.dma("sync", bef[:], aps["be"])
    b.on("vector", lambda e: e.memset(onesb[:], 1.0))
    b.on("vector", lambda e: e.memset(s["eps"][:], 1e-5))
    b.bar_phase()
    b.on("vector", lambda e: e.tensor_copy(
        wg_hi.ap().rearrange("p c e -> p (c e)"),
        wg_f.ap().rearrange("p c e -> p (c e)")))
    b.on("vector", lambda e: e.tensor_copy(beb[:], bef[:]))
    b.bar_phase()
    b.on("vector", lambda e: e.tensor_copy(
        wg_t.ap().rearrange("p c e -> p (c e)"),
        wg_hi.ap().rearrange("p c e -> p (c e)")))
    b.bar_phase()
    b.on("vector", lambda e: e.tensor_tensor(
        wg_t.ap().rearrange("p c e -> p (c e)"),
        wg_f.ap().rearrange("p c e -> p (c e)"),
        wg_t.ap().rearrange("p c e -> p (c e)"), op=ALU.subtract))
    b.bar_phase()
    b.on("vector", lambda e: e.tensor_copy(
        wg_lo.ap().rearrange("p c e -> p (c e)"),
        wg_t.ap().rearrange("p c e -> p (c e)")))
    b.bar_phase()
    if stop_after == "prep":
        return

    xg = b.sbuf("l1f_xg", [128, 8, D], F32)
    xng = b.sbuf("l1f_xng", [128, 8, D], BF16)
    xlg = b.sbuf("l1f_xlg", [128, 8, D], BF16)
    xlo = b.sbuf("l1f_xlo", [128, NCH, 1024], BF16)
    for g in range(4):
        b.dma("sync", xg[:], xin_hbm[:, g * 8:(g + 1) * 8, :])
        b.bar_phase()
        emit_ln_group(b, s, xg, xng)
        for j in range(8):
            b.on("vector", (lambda jj: lambda e: e.tensor_scalar(
                s["sq"][:], xg[:, jj, :], s["rstd"][:, jj:jj + 1],
                s["bias"][:, jj:jj + 1], op0=ALU.mult, op1=ALU.add))(jj if False else j))
            b.on("vector", (lambda jj: lambda e: e.tensor_tensor(
                xlg[:, jj, :], s["sq"][:], xng[:, jj, :], op=ALU.subtract))(j))
        b.bar_phase()
        emit_tok2feat(b, ps, identb, xng, xn, g * 8)
        for j in range(8):
            pbf = ps[:, j * 512:(j + 1) * 512].bitcast(BF16)
            b.on("tensor", (lambda jj, pb: lambda e: [
                e.transpose(pb[:, c * 128:(c + 1) * 128],
                            xlg[:, jj, c * 128:(c + 1) * 128], identb[:])
                for c in range(NCH)][-1])(j, pbf))
        b.bar_phase()
        for j in range(8):
            pbf = ps[:, j * 512:(j + 1) * 512].bitcast(BF16)
            b.on("scalar", (lambda jj, pb: lambda e: e.activation(
                xlo[:, :, jj * 128:(jj + 1) * 128],
                pb.rearrange("p (c t) -> p c t", c=NCH), AF.Copy))(j, pbf))
        b.bar_phase()
        def gmm(gg):
            def f(e):
                r = None
                for half in range(2):
                    pb = ps[0:8, half * 512:(half + 1) * 512]
                    for dic in range(NCH):
                        xs_hi = xn[:, dic, gg * 1024 + half * 512:
                                   gg * 1024 + (half + 1) * 512]
                        xs_lo = xlo[:, dic, half * 512:(half + 1) * 512]
                        e.matmul(pb, wg_hi[:, dic, :], xs_hi,
                                 start=(dic == 0), stop=False)
                        e.matmul(pb, wg_lo[:, dic, :], xs_hi,
                                 start=False, stop=False)
                        r = e.matmul(pb, wg_hi[:, dic, :], xs_lo,
                                     start=False, stop=(dic == NCH - 1))
                return r
            return f
        if stop_after != "nogmm":
            b.on("tensor", gmm(g))
            b.bar_phase()
            if stop_after != "noevac":
                b.on("scalar", (lambda gg: lambda e: e.activation(
                    logT[:, gg * 1024:gg * 1024 + 512], ps[0:8, 0:512],
                    AF.Copy))(g))
                b.on("scalar", (lambda gg: lambda e: e.activation(
                    logT[:, gg * 1024 + 512:(gg + 1) * 1024],
                    ps[0:8, 512:1024], AF.Copy))(g))
                b.bar_phase()
    b.free("l1f_xlo")
    b.free("l1f_xlg")
    b.free("l1f_xng")
    b.free("l1f_xg")
    if stop_after in ("p1", "nogmm", "noevac"):
        return

    for g in range(4):
        def ltr(gg):
            def f(e):
                r = None
                for j in range(8):
                    n = gg * 8 + j
                    r = e.transpose(ps[:, j * 512:j * 512 + 8],
                                    logT[:, n * 128:(n + 1) * 128],
                                    identf[0:8, 0:8])
                return r
            return f
        b.on("tensor", ltr(g))
        b.bar_phase()
        for j in range(8):
            b.on("scalar", (lambda jj: lambda e: e.activation(
                lgt[:, jj, :], ps[:, jj * 512:jj * 512 + 8], AF.Copy))(j))
        b.bar_phase()
        for j in range(8):
            b.on("vector", (lambda jj: lambda e: e.reduce_max(
                m1[:, jj:jj + 1], lgt[:, jj, :], axis=AX.X))(j))
        b.bar_phase()
        for j in range(8):
            b.on("vector", (lambda jj: lambda e: e.tensor_single_scalar(
                msk1[:, jj, :], lgt[:, jj, :], m1[:, jj:jj + 1],
                op=ALU.is_equal))(j))
        b.bar_phase()
        for j in range(8):
            b.on("vector", (lambda jj: lambda e: e.scalar_tensor_tensor(
                cwg[:, jj, :], msk1[:, jj, :], -1e30, lgt[:, jj, :],
                op0=ALU.mult, op1=ALU.add))(j))
        b.bar_phase()
        for j in range(8):
            b.on("vector", (lambda jj: lambda e: e.reduce_max(
                m2[:, jj:jj + 1], cwg[:, jj, :], axis=AX.X))(j))
        b.bar_phase()
        for j in range(8):
            b.on("vector", (lambda jj: lambda e: e.tensor_single_scalar(
                msk2[:, jj, :], cwg[:, jj, :], m2[:, jj:jj + 1],
                op=ALU.is_equal))(j))
        b.bar_phase()
        b.on("vector", lambda e: e.tensor_tensor(sg[:], m2[:], m1[:],
                                                 op=ALU.subtract))
        b.bar_phase()
        b.on("scalar", lambda e: e.activation(sg[:], sg[:], AF.Exp))
        b.bar_phase()
        b.on("vector", lambda e: e.tensor_scalar(w1[:], sg[:], 1.0, 1.0,
                                                 op0=ALU.mult, op1=ALU.add))
        b.bar_phase()
        b.on("vector", lambda e: e.reciprocal(w1[:], w1[:]))
        b.bar_phase()
        b.on("vector", lambda e: e.tensor_tensor(w2[:], sg[:], w1[:],
                                                 op=ALU.mult))
        b.bar_phase()
        for j in range(8):
            b.on("vector", (lambda jj: lambda e: e.tensor_scalar(
                cwg[:, jj, :], msk1[:, jj, :], w1[:, jj:jj + 1], 0.0,
                op0=ALU.mult, op1=ALU.add))(j))
        b.bar_phase()
        for j in range(8):
            b.on("vector", (lambda jj: lambda e: e.scalar_tensor_tensor(
                cwg[:, jj, :], msk2[:, jj, :], w2[:, jj:jj + 1], cwg[:, jj, :],
                op0=ALU.mult, op1=ALU.add))(j))
        b.bar_phase()
        def ctr(gg):
            def f(e):
                r = None
                for j in range(8):
                    r = e.transpose(ps[0:8, j * 512:j * 512 + 128],
                                    cwg[:, j, :], identf[:])
                return r
            return f
        b.on("tensor", ctr(g))
        b.bar_phase()
        for j in range(8):
            n = g * 8 + j
            b.on("scalar", (lambda nn, jj: lambda e: e.activation(
                cwT[:, nn * 128:(nn + 1) * 128],
                ps[0:8, jj * 512:jj * 512 + 128], AF.Copy))(n, j))
        b.bar_phase()
    b.on("vector", lambda e: e.tensor_copy(cwTb[:], cwT[:]))
    b.bar_phase()
    if stop_after == "top2":
        return

    westg = b.sbuf("l1f_westg", [128, 4, D], F32)
    webst = b.sbuf("l1f_webst", [128, 4, D], BF16)
    we_hbm = aps["WeT"].rearrange("e (c p) o -> p e c o", p=128)
    for e_ in range(8):
        for hh in range(2):
            b.dma("sync", westg[:], we_hbm[:, e_, hh * 4:(hh + 1) * 4, :])
            b.bar_phase()
            b.on("vector", (lambda: lambda e: e.tensor_copy(
                webst.ap().rearrange("p c o -> p (c o)"),
                westg.ap().rearrange("p c o -> p (c o)")))())
            b.bar_phase()
            b.dma("gpsimd", aps["webf_hbm"][:, e_, hh * 4:(hh + 1) * 4, :],
                  webst[:])
            b.bar_phase()
    b.free("l1f_webst")
    b.free("l1f_westg")
    if stop_after == "wecast":
        return

    wes = b.sbuf("l1f_wes", [128, NCH, D], BF16)
    xe = b.sbuf("l1f_xe", [128, NCH, 512], BF16)
    rows = b.sbuf("l1f_rows", [1, 8 * 512], BF16)
    cwb8 = b.sbuf("l1f_cwb8", [128, 8, 512], BF16)
    ysl = b.sbuf("l1f_ysl", [128, 8, 512], F32)
    xio = b.sbuf("l1f_xio", [128, 4, D], F32)
    for nt in range(8):
        for e_ in range(8):
            b.dma("gpsimd", rows[0:1, e_ * 512:(e_ + 1) * 512],
                  cwTb[e_:e_ + 1, nt * 512:(nt + 1) * 512])
        b.bar_phase()
        def bmm(e):
            r = None
            for e_ in range(8):
                r = e.matmul(ps[:, e_ * 512:(e_ + 1) * 512], onesb[:],
                             rows[0:1, e_ * 512:(e_ + 1) * 512],
                             start=True, stop=True)
            return r
        b.on("tensor", bmm)
        b.bar_phase()
        for e_ in range(8):
            b.on("scalar", (lambda ee: lambda e: e.activation(
                cwb8[:, ee, :], ps[:, ee * 512:(ee + 1) * 512], AF.Copy))(e_))
        b.bar_phase()
        for e_ in range(8):
            b.dma("sync", wes[:], aps["webf_hbm"][:, e_, :, :])
            for c in range(NCH):
                b.on("vector", (lambda ee, cc, ntn: lambda e: e.tensor_tensor(
                    xe[:, cc, :], xn[:, cc, ntn * 512:(ntn + 1) * 512],
                    cwb8[:, ee, :], op=ALU.mult))(e_, c, nt))
            b.bar_phase()
            def emm(ee):
                def f(e):
                    r = None
                    for oc in range(8):
                        for dic in range(NCH):
                            r = e.matmul(ps[:, oc * 512:(oc + 1) * 512],
                                         wes[:, dic, oc * 128:(oc + 1) * 128],
                                         xe[:, dic, :],
                                         start=(ee == 0 and dic == 0),
                                         stop=False)
                    return r
                return f
            b.on("tensor", emm(e_))
            b.bar_phase()
        def biasmm(ntn):
            def f(e):
                r = None
                for oc in range(8):
                    r = e.matmul(ps[:, oc * 512:(oc + 1) * 512],
                                 beb[:, oc * 128:(oc + 1) * 128],
                                 cwTb[:, ntn * 512:(ntn + 1) * 512],
                                 start=False, stop=True)
                return r
            return f
        b.on("tensor", biasmm(nt))
        b.bar_phase()
        for oc in range(8):
            b.on("scalar", (lambda o: lambda e: e.activation(
                ysl[:, o, :], ps[:, o * 512:(o + 1) * 512], AF.Copy))(oc))
        b.bar_phase()
        b.dma("sync", xio[:], xin_hbm[:, nt * 4:(nt + 1) * 4, :])

        def trf(e):
            r = None
            for st in range(4):
                for oc in range(8):
                    r = e.transpose(
                        ps[:, st * 1024 + oc * 128: st * 1024 + (oc + 1) * 128],
                        ysl[:, oc, st * 128:(st + 1) * 128], identf[:])
            return r
        b.on("tensor", trf)
        b.bar_phase()
        for st in range(4):
            b.on("vector", (lambda ss: lambda e: e.tensor_tensor(
                xio[:, ss, :], xio[:, ss, :], ps[:, ss * 1024:(ss + 1) * 1024],
                op=ALU.add))(st))
        b.bar_phase()
        b.dma("gpsimd", xout_hbm[:, nt * 4:(nt + 1) * 4, :], xio[:])
        b.bar_phase()

    for nm in ("l1f_xio", "l1f_ysl", "l1f_cwb8", "l1f_rows", "l1f_xe",
               "l1f_wes", "l1f_xn", "l1f_cwg", "l1f_lgt", "l1f_msk2",
               "l1f_msk1", "l1f_w2", "l1f_w1", "l1f_sg", "l1f_m2", "l1f_m1"):
        b.free(nm)
    for nm in ("l1f_ln_eps", "l1f_ln_sq", "l1f_ln_bias", "l1f_ln_rstd",
               "l1f_ln_sd", "l1f_ln_var", "l1f_ln_m2", "l1f_ln_mu",
               "l1f_ln_ss", "l1f_ln_sums", "l1f_onesb", "l1f_wgt", "l1f_wgf",
               "l1f_wglo", "l1f_wghi", "l1f_bef", "l1f_beb", "l1f_cwTb",
               "l1f_cwT", "l1f_logT"):
        b.free(nm)


def l1ffn_host_inputs(inputs):
    Wg = np.asarray(inputs["L1_ffn_Wg"], np.float32)
    We = np.asarray(inputs["L1_ffn_We"], np.float32)
    be = np.asarray(inputs["L1_ffn_be"], np.float32)
    return {
        "WgT": np.ascontiguousarray(Wg.T),
        "WeT": np.ascontiguousarray(We.transpose(0, 2, 1)),
        "be": np.ascontiguousarray(be),
    }


def build_l1ffn_prog(stop_after=None):
    nc = bass.Bass("TRN2", target_bir_lowering=False)
    xd = nc.dram_tensor("x", [T, D], F32, kind="ExternalInput").ap()
    aps = {
        "WgT": nc.dram_tensor("WgT", [D, 8], F32, kind="ExternalInput").ap(),
        "WeT": nc.dram_tensor("WeT", [8, D, D], F32, kind="ExternalInput").ap(),
        "be": nc.dram_tensor("be", [8, D], F32, kind="ExternalInput").ap(),
        "webf_hbm": nc.dram_tensor("webf_hbm", [128, 8, NCH, D], BF16,
                                   kind="Internal").ap(),
    }
    identbd = nc.dram_tensor("identb", [128, 128], BF16, kind="ExternalInput").ap()
    identfd = nc.dram_tensor("identf", [128, 128], F32, kind="ExternalInput").ap()
    xod = nc.dram_tensor("xo", [T, D], F32, kind="ExternalOutput").ap()

    b = Bld(nc)
    identb = b.sbuf("identbS", [128, 128], BF16)
    identf = b.sbuf("identfS", [128, 128], F32)
    ps = b.psum("PS", [128, 4096], F32)
    b.dma("sync", identb[:], identbd)
    b.dma("sync", identf[:], identfd)
    b.bar_phase()
    xin_hbm = xd.rearrange("(n p) d -> p n d", p=128)
    xout_hbm = xod.rearrange("(n p) d -> p n d", p=128)
    emit_l1ffn(b, aps, xin_hbm, xout_hbm, ps, identb, identf, stop_after)
    b.finalize()
    return nc



def emit_l0attn(b, aps, xin_hbm, xout_hbm, ps, identb, identf):
    s = mk_lnslab(b, "l0a_ln")
    bq = b.sbuf("l0a_bq", [128, 24], F32)
    bk8 = b.sbuf("l0a_bk8", [128, 8], F32)
    wrx = b.sbuf("l0a_wrxf", [128, 512], F32)
    wrxb = b.sbuf("l0a_wrxb", [128, 512], BF16)
    brx = b.sbuf("l0a_brx", [128, 512], F32)
    rpc = b.sbuf("l0a_rpc", [128, 2, 8, 32], F32)
    den_p = b.sbuf("l0a_denp", [128, 2, 16], F32)
    recip = b.sbuf("l0a_recip", [128, 8], F32)
    xn = b.sbuf("l0a_xn", [128, NCH, T], BF16)
    b.dma("sync", bq[:], aps["bqkv_pp"])
    b.dma("sync", wrx[:], aps["wrx2"])
    b.dma("sync", brx[:], aps["brx_pp"])
    b.on("vector", lambda e: e.memset(s["eps"][:], 1e-5))
    b.bar_phase()
    b.on("vector", lambda e: e.tensor_scalar(bk8[:], bq[:, 8:16], 0.125, 0.0,
                                             op0=ALU.mult, op1=ALU.add))
    b.on("vector", lambda e: e.tensor_copy(wrxb[:], wrx[:]))
    b.bar_phase()

    xg = b.sbuf("l0a_xg", [128, 8, D], F32)
    xng = b.sbuf("l0a_xng", [128, 8, D], BF16)
    for g in range(4):
        b.dma("sync", xg[:], xin_hbm[:, g * 8:(g + 1) * 8, :])
        b.bar_phase()
        emit_ln_group(b, s, xg, xng)
        emit_tok2feat(b, ps, identb, xng, xn, g * 8)
    b.free("l0a_xng")
    b.free("l0a_xg")

    wstg = b.sbuf("l0a_wstg", [128, 8, 128], F32)
    wbf = b.sbuf("l0a_wbf", [128, 8, 128], BF16)
    qsl = b.sbuf("l0a_qsl", [128, 4, T], BF16)
    wq_hbm = aps["WqkvT"].rearrange("(c p) o -> p c o", p=128)
    for hg in range(2):
        for kind in range(3):
            for pcl in range(4):
                mc = kind * 8 + hg * 4 + pcl
                b.dma("sync", wstg[:], wq_hbm[:, :, mc * 128:(mc + 1) * 128])
                b.bar_phase()
                b.on("vector", lambda e: e.tensor_copy(
                    wbf.ap().rearrange("p c o -> p (c o)"),
                    wstg.ap().rearrange("p c o -> p (c o)")))
                b.bar_phase()

                def qkvmm(e):
                    r = None
                    for nt in range(8):
                        for dic in range(NCH):
                            r = e.matmul(ps[:, nt * 512:(nt + 1) * 512],
                                         wbf[:, dic, :],
                                         xn[:, dic, nt * 512:(nt + 1) * 512],
                                         start=(dic == 0),
                                         stop=(dic == NCH - 1))
                    return r
                b.on("tensor", qkvmm)
                b.bar_phase()
                for nt in range(8):
                    if kind == 1:
                        b.on("scalar", (lambda ntn, m=mc, p_=pcl: lambda e:
                             e.activation(qsl[:, p_, ntn * 512:(ntn + 1) * 512],
                                          ps[:, ntn * 512:(ntn + 1) * 512],
                                          AF.Identity,
                                          bias=bk8[:, m - 8:m - 7],
                                          scale=0.125))(nt))
                    else:
                        b.on("scalar", (lambda ntn, m=mc, p_=pcl: lambda e:
                             e.activation(qsl[:, p_, ntn * 512:(ntn + 1) * 512],
                                          ps[:, ntn * 512:(ntn + 1) * 512],
                                          AF.Identity,
                                          bias=bq[:, m:m + 1]))(nt))
                b.bar_phase()
            b.dma("gpsimd", aps["qkv_hbm"][hg, kind], qsl[:])
            b.bar_phase()
    b.free("l0a_qsl")
    b.free("l0a_wbf")
    b.free("l0a_wstg")
    b.free("l0a_xn")

    for hg in range(2):
        qs = b.sbuf(f"l0a_qs{hg}", [128, 4, T], BF16)
        ks = b.sbuf(f"l0a_ks{hg}", [128, 4, T], BF16)
        vt = b.sbuf(f"l0a_vt{hg}", [128, 32, 512], BF16)
        vl = b.sbuf(f"l0a_vl{hg}", [128, 4, T], BF16)
        b.dma("sync", qs[:], aps["qkv_hbm"][hg, 0])
        b.dma("sync", ks[:], aps["qkv_hbm"][hg, 1])
        b.dma("sync", vl[:], aps["qkv_hbm"][hg, 2])
        b.bar_phase()
        for pcl in range(4):
            def vtr(p_):
                def f(e):
                    r = None
                    for jsub in range(32):
                        bank = jsub // 4
                        r = e.transpose(
                            ps[:, bank * 512:(bank + 1) * 512].bitcast(BF16)
                            [:, (jsub % 4) * 128:(jsub % 4) * 128 + 128],
                            vl[:, p_, jsub * 128:(jsub + 1) * 128], identb[:])
                    return r
                return f
            b.on("tensor", vtr(pcl))
            b.bar_phase()
            for bank in range(8):
                b.on("scalar", (lambda p_, bk: lambda e: e.activation(
                    vt[:, bk * 4:(bk + 1) * 4, p_ * 128:(p_ + 1) * 128],
                    ps[:, bk * 512:(bk + 1) * 512].bitcast(BF16)[:, 0:512]
                    .rearrange("p (a t) -> p a t", a=4),
                    AF.Copy))(pcl, bank))
            b.bar_phase()
        b.free(f"l0a_vl{hg}")
        rp_sb = b.sbuf(f"l0a_rpsb{hg}", [128, 8, 512], F32)

        for qt in range(32):
            def rpmm(qtn):
                def f(e):
                    rv = None
                    for pcl in range(4):
                        for hh in range(2):
                            rv = e.matmul(
                                ps[:, (pcl * 2 + hh) * 512:
                                   (pcl * 2 + hh + 1) * 512],
                                qs[hh * 64:(hh + 1) * 64, pcl,
                                   qtn * 128:(qtn + 1) * 128],
                                wrxb[hh * 64:(hh + 1) * 64, :],
                                start=True, stop=True,
                                tile_position=(hh * 64, 0))
                    return rv
                return f
            b.on("tensor", rpmm(qt))
            b.bar_phase()
            for hl in range(8):
                b.on("vector", (lambda h: lambda e: e.tensor_tensor(
                    rp_sb[:, h, :], ps[:, h * 512:(h + 1) * 512], brx[:],
                    op=ALU.add))(hl))
            for hl in range(8):
                b.on("vector", (lambda h, qtn: lambda e: e.tensor_copy(
                    rpc[:, 0, h, qtn:qtn + 1], rp_sb[:, h, 0:1]))(hl, qt))
                b.on("vector", (lambda h, qtn: lambda e: e.tensor_copy(
                    rpc[:, 1, h, qtn:qtn + 1], rp_sb[:, h, 511:512]))(hl, qt))
            b.bar_phase()
            b.dma("gpsimd",
                  aps["rp_hbm"][hg * 8:(hg + 1) * 8, qt].rearrange(
                      "h p u -> p h u"),
                  rp_sb[:])
            b.bar_phase()
        b.free(f"l0a_rpsb{hg}")

        Pp = b.sbuf(f"l0a_Pp{hg}", [128, 2, 2, T], BF16)
        PT2 = b.sbuf(f"l0a_PT2{hg}", [128, 2, 32, 256], BF16)
        Eb2 = b.sbuf(f"l0a_Eb2{hg}", [128, 2, 384], F32)
        osl2 = b.sbuf(f"l0a_osl2{hg}", [128, 4, 256], F32)
        xio = b.sbuf(f"l0a_xio{hg}", [128, 2, D], F32)
        den_g = b.sbuf(f"l0a_deng{hg}", [128, 16], F32)

        for qg in range(16):
            for pc in range(4):
                pend = []
                for qtl in range(2):
                    qt = qg * 2 + qtl
                    q0 = qt * 128
                    jb0 = max(0, q0 - 128)
                    jb1 = min(T, q0 + 256)
                    par = qtl
                    b.on("vector", (lambda p_=par: lambda e: e.memset(
                        den_p.ap().rearrange("p h k -> p (h k)"), 0.0))())
                    for hh in range(2):
                        h_abs = hg * 8 + pc * 2 + hh
                        off = ((h_abs * 32 + qt) * 128 * 512) + (jb0 - q0 + 256)
                        b.dma("sync", Eb2[:, hh, 0:jb1 - jb0],
                              AP(aps["rp_hbm_h"], off, [[511, 128],
                                                        [1, jb1 - jb0]]))
                    b.bar_phase()

                    def smm(jcn, qtn=qt, pcn=pc):
                        st = (jcn % 2) * 2

                        def f(e):
                            rv = None
                            for hh in range(2):
                                rv = e.matmul(
                                    ps[:, (st + hh) * 512:(st + hh + 1) * 512],
                                    qs[hh * 64:(hh + 1) * 64, pcn,
                                       qtn * 128:(qtn + 1) * 128],
                                    ks[hh * 64:(hh + 1) * 64, pcn,
                                       jcn * 512:(jcn + 1) * 512],
                                    start=True, stop=True,
                                    tile_position=(hh * 64, 0))
                            return rv
                        return f

                    def banddve(jcn, jb0_=None, jb1_=None):
                        c0 = jcn * 512
                        bl0, bl1 = max(c0, jb0_), min(c0 + 512, jb1_)
                        st = (jcn % 2) * 2
                        for hh in range(2):
                            b.on("vector", (lambda h, a0=bl0, a1=bl1, cc=c0,
                                            jb=jb0_, s_=st: lambda e:
                                 e.tensor_tensor(
                                     ps[:, (s_ + h) * 512 + (a0 - cc):
                                        (s_ + h) * 512 + (a1 - cc)],
                                     ps[:, (s_ + h) * 512 + (a0 - cc):
                                        (s_ + h) * 512 + (a1 - cc)],
                                     Eb2[:, h, (a0 - jb):(a1 - jb)],
                                     op=ALU.add))(hh))

                    def expph(jcn, qtn=qt, jb0_=None, jb1_=None, par_=None):
                        c0, c1 = jcn * 512, (jcn + 1) * 512
                        bl0, bl1 = max(c0, jb0_), min(c1, jb1_)
                        st = (jcn % 2) * 2
                        for hh in range(2):
                            hl = pc * 2 + hh
                            pieces = []
                            if c0 < jb0_:
                                pieces.append((c0, min(c1, jb0_), 0))
                            if bl0 < bl1:
                                pieces.append((bl0, bl1, None))
                            if c1 > jb1_:
                                pieces.append((max(c0, jb1_), c1, 1))
                            for k_, (a0, a1, bi) in enumerate(pieces):
                                slot = jcn if len(pieces) == 1 else                                     8 + (jcn % 2) * 4 + k_
                                kw = {}
                                if bi is not None:
                                    kw["bias"] = rpc[:, bi, hl, qtn:qtn + 1]
                                b.on("scalar", (lambda h, x0=a0, x1=a1, cc=c0,
                                                sl=slot, s_=st, pp=par_,
                                                kww=dict(kw): lambda e:
                                     e.activation(
                                         Pp[:, pp, h, x0:x1],
                                         ps[:, (s_ + h) * 512 + (x0 - cc):
                                            (s_ + h) * 512 + (x1 - cc)],
                                         AF.Exp,
                                         accum_out=den_p[:, h, sl:sl + 1],
                                         **kww))(hh))

                    def inject():
                        if pend:
                            pend.pop(0)()

                    pendexp = None
                    for jc in range(8):
                        c0 = jc * 512
                        isband = max(c0, jb0) < min(c0 + 512, jb1)
                        if isband:
                            if pendexp is not None:
                                expph(pendexp, jb0_=jb0, jb1_=jb1, par_=par)
                                inject()
                                b.bar_phase()
                            b.on("tensor", smm(jc))
                            inject()
                            b.bar_phase()
                            banddve(jc, jb0_=jb0, jb1_=jb1)
                            b.bar_phase()
                            expph(jc, jb0_=jb0, jb1_=jb1, par_=par)
                            inject()
                            b.bar_phase()
                            pendexp = None
                        else:
                            b.on("tensor", smm(jc))
                            if pendexp is not None:
                                expph(pendexp, jb0_=jb0, jb1_=jb1, par_=par)
                            inject()
                            b.bar_phase()
                            pendexp = jc
                    if pendexp is not None:
                        expph(pendexp, jb0_=jb0, jb1_=jb1, par_=par)
                        inject()
                        b.bar_phase()
                    for hh in range(2):
                        idx = pc * 4 + qtl * 2 + hh
                        b.on("vector", (lambda h, ix=idx: lambda e:
                             e.reduce_sum(den_g[:, ix:ix + 1],
                                          den_p[:, h, :], axis=AX.X))(hh))
                    b.bar_phase()

                    def mkdmat(h_, pp_, qq_):
                        def f():
                            b.dmat("sync",
                                   PT2[:, h_, :, qq_ * 128:(qq_ + 1) * 128],
                                   Pp[:, pp_, h_, :])
                        return f
                    pend.append(mkdmat(0, par, qtl))
                    pend.append(mkdmat(1, par, qtl))
                while pend:
                    pend.pop(0)()
                    b.bar_phase()

                def avmm(pcn=pc):
                    def f(e):
                        rv = None
                        for jsub in range(32):
                            st = (jsub == 0)
                            sp = (jsub == 31)
                            rv = e.matmul(ps[0:64, 0:256],
                                          vt[:, jsub, pcn * 128:pcn * 128 + 64],
                                          PT2[:, 0, jsub, :],
                                          start=st, stop=sp,
                                          tile_position=(0, 0))
                            rv = e.matmul(ps[64:128, 0:256],
                                          vt[:, jsub,
                                             pcn * 128 + 64:pcn * 128 + 128],
                                          PT2[:, 1, jsub, :],
                                          start=st, stop=sp,
                                          tile_position=(0, 64))
                        return rv
                    return f
                b.on("tensor", avmm())
                b.bar_phase()
                b.on("scalar", (lambda p_: lambda e: e.activation(
                    osl2[:, p_, :], ps[:, 0:256], AF.Identity))(pc))
                b.bar_phase()

            src_x = xin_hbm if hg == 0 else xout_hbm
            b.dma("sync", xio[:], src_x[:, qg * 2:(qg + 1) * 2, :])
            b.on("vector", lambda e: e.reciprocal(den_g[:], den_g[:]))
            b.bar_phase()

            def otr(e):
                rv = None
                for ql in range(2):
                    for p_ in range(4):
                        rv = e.transpose(
                            ps[:, ql * 1024 + p_ * 128:
                               ql * 1024 + (p_ + 1) * 128],
                            osl2[:, p_, ql * 128:(ql + 1) * 128], identf[:])
                return rv
            b.on("tensor", otr)
            b.bar_phase()
            for ql in range(2):
                for p_ in range(4):
                    for hh in range(2):
                        idx = p_ * 4 + ql * 2 + hh
                        dcol = hg * 512 + p_ * 128 + hh * 64
                        b.on("vector", (lambda ix=idx, dc=dcol, q_=ql,
                                        pp_=p_, hx=hh: lambda e:
                             e.scalar_tensor_tensor(
                                 xio[:, q_, dc:dc + 64],
                                 ps[:, q_ * 1024 + pp_ * 128 + hx * 64:
                                    q_ * 1024 + pp_ * 128 + hx * 64 + 64],
                                 den_g[:, ix:ix + 1],
                                 xio[:, q_, dc:dc + 64],
                                 op0=ALU.mult, op1=ALU.add))())
            b.bar_phase()
            b.dma("gpsimd", xout_hbm[:, qg * 2:(qg + 1) * 2, :], xio[:])
            b.bar_phase()

        for nm in (f"l0a_deng{hg}", f"l0a_xio{hg}", f"l0a_osl2{hg}",
                   f"l0a_Eb2{hg}", f"l0a_PT2{hg}", f"l0a_Pp{hg}",
                   f"l0a_vt{hg}", f"l0a_ks{hg}", f"l0a_qs{hg}"):
            b.free(nm)

    for nm in ("l0a_recip", "l0a_denp", "l0a_rpc", "l0a_brx", "l0a_wrxb",
               "l0a_wrxf", "l0a_bk8", "l0a_bq", "l0a_ln_eps", "l0a_ln_sq",
               "l0a_ln_bias", "l0a_ln_rstd", "l0a_ln_sd", "l0a_ln_var",
               "l0a_ln_m2", "l0a_ln_mu", "l0a_ln_ss", "l0a_ln_sums"):
        b.free(nm)


def l0attn_host_inputs(inputs):
    Wqkv = np.asarray(inputs["L0_attn_Wqkv"], np.float32)
    bqkv = np.asarray(inputs["L0_attn_bqkv"], np.float32)
    Wr = np.asarray(inputs["L0_attn_Wr"], np.float32)
    br = np.asarray(inputs["L0_attn_br"], np.float32)
    perm = np.array([h * 192 + kind * 64 + c
                     for kind in range(3) for h in range(16) for c in range(64)])
    WqkvT = np.ascontiguousarray(Wqkv[perm, :].T)
    bperm = bqkv[perm]
    idx = np.clip(np.arange(512) - 128, 0, 256)
    Wr_ext = Wr[idx, :].T
    return {
        "WqkvT": WqkvT,
        "bqkv_pp": np.ascontiguousarray(bperm.reshape(24, 128).T),
        "wrx2": np.ascontiguousarray(np.vstack([Wr_ext, Wr_ext])),
        "brx_pp": np.ascontiguousarray(np.tile(br[idx][None, :], (128, 1))),
    }


def build_l0attn_prog():
    nc = bass.Bass("TRN2", target_bir_lowering=False)
    xd = nc.dram_tensor("x", [T, D], F32, kind="ExternalInput").ap()
    rp_h = nc.dram_tensor("rp_hbm", [16, 32, 128, 512], F32, kind="Internal")
    aps = {
        "WqkvT": nc.dram_tensor("WqkvT", [D, 3072], F32, kind="ExternalInput").ap(),
        "bqkv_pp": nc.dram_tensor("bqkv_pp", [128, 24], F32, kind="ExternalInput").ap(),
        "wrx2": nc.dram_tensor("wrx2", [128, 512], F32, kind="ExternalInput").ap(),
        "brx_pp": nc.dram_tensor("brx_pp", [128, 512], F32, kind="ExternalInput").ap(),
        "rp_hbm": rp_h.ap(),
        "rp_hbm_h": rp_h,
        "qkv_hbm": nc.dram_tensor("qkv_hbm", [2, 3, 128, 4, T], BF16,
                                  kind="Internal").ap(),
    }
    identbd = nc.dram_tensor("identb", [128, 128], BF16, kind="ExternalInput").ap()
    identfd = nc.dram_tensor("identf", [128, 128], F32, kind="ExternalInput").ap()
    xod = nc.dram_tensor("xo", [T, D], F32, kind="ExternalOutput").ap()

    b = Bld(nc)
    identb = b.sbuf("identbS", [128, 128], BF16)
    identf = b.sbuf("identfS", [128, 128], F32)
    ps = b.psum("PS", [128, 4096], F32)
    b.dma("sync", identb[:], identbd)
    b.dma("sync", identf[:], identfd)
    b.bar_phase()
    xin_hbm = xd.rearrange("(n p) d -> p n d", p=128)
    xout_hbm = xod.rearrange("(n p) d -> p n d", p=128)
    emit_l0attn(b, aps, xin_hbm, xout_hbm, ps, identb, identf)
    b.finalize()
    return nc



def _shift_load(b, eng, dst, z_flat, n):
    t0 = n * 128
    b.dma(eng, dst[:, 0:512], z_flat[t0:t0 + 128, 0:512])
    if n == 0:
        b.on("vector", lambda e: e.memset(dst[0:1, 512:1024], 0.0))
        b.dma(eng, dst[1:128, 512:1024], z_flat[0:127, 512:1024])
    else:
        b.dma(eng, dst[:, 512:1024], z_flat[t0 - 1:t0 + 127, 512:1024])


def emit_l0ffn_attnhalf(b, aps, li, ps, identb, identf, z_in, z2, dbg=None):
    qs_pp = b.sbuf("lfa_qspp", [128, 1], F32)
    ks_pp = b.sbuf("lfa_kspp", [128, 1], F32)
    perm = b.sbuf("lfa_perm", [128, 128], BF16)
    ind2 = b.sbuf("lfa_ind2", [128, 2], BF16)
    bc2 = b.sbuf("lfa_bc2", [2, 128], BF16)
    mski = b.sbuf("lfa_mski", [128, 384], F32)
    msk0a = b.sbuf("lfa_msk0a", [128, 384], F32)
    msk0b = b.sbuf("lfa_msk0b", [128, 384], F32)
    wq = b.sbuf("lfa_wq", [128, NCH, 3072], BF16)
    wo = b.sbuf("lfa_wo", [128, NCH, 1024], BF16)
    kc = b.sbuf("lfa_kc", [128, NCH, 512], BF16)
    vc = b.sbuf("lfa_vc", [128, 4, 1024], BF16)
    tabs = b.sbuf("lfa_tabs", [128, 4, 256], F32)
    den = b.sbuf("lfa_den", [128, 16], F32)
    nmx = b.sbuf("lfa_nmx", [128, 16], F32)
    rec = b.sbuf("lfa_rec", [128, 16], F32)
    wstg = b.sbuf("lfa_wstg", [128, 3072], F32)

    b.dma("sync", qs_pp[:], aps["qs_pp"][li])
    b.dma("sync", ks_pp[:], aps["ks_pp"][li])
    b.dma("sync", perm[:], aps["perm"])
    b.dma("sync", ind2[:], aps["ind2"])
    b.dma("sync", bc2[:], aps["bc2"])
    b.dma("sync", mski[:], aps["mask_i"])
    b.dma("sync", msk0a[:], aps["mask_0a"])
    b.dma("sync", msk0b[:], aps["mask_0b"])
    b.on("vector", lambda e: e.memset(kc.ap().rearrange("p c v -> p (c v)"), 0.0))
    b.on("vector", lambda e: e.memset(vc.ap().rearrange("p s v -> p (s v)"), 0.0))
    b.bar_phase()
    for dic in range(NCH):
        b.dma("sync", wstg[:], aps["Wqkv"][li].rearrange(
            "(c p) o -> p c o", p=128)[:, dic, :])
        b.bar_phase()
        b.on("vector", (lambda d: lambda e: e.tensor_copy(
            wq[:, d, :], wstg[:]))(dic))
        b.bar_phase()
        b.dma("sync", wstg[:, 0:1024], aps["Wo"][li].rearrange(
            "(c p) o -> p c o", p=128)[:, dic, :])
        b.bar_phase()
        b.on("vector", (lambda d: lambda e: e.tensor_copy(
            wo[:, d, :], wstg[:, 0:1024]))(dic))
        b.bar_phase()
    b.free("lfa_wstg")

    ush = b.sbuf("lfa_ush", [128, 2, D], F32)
    qf = b.sbuf("lfa_qf", [128, NCH, 256], F32)
    kf = b.sbuf("lfa_kf", [128, NCH, 256], F32)
    rq = b.sbuf("lfa_rq", [2, 2048], F32)
    rqb = b.sbuf("lfa_rqb", [2, 2048], BF16)
    rql = b.sbuf("lfa_rql", [2, 2048], BF16)
    rqt = b.sbuf("lfa_rqt", [2, 2048], F32)
    t1 = b.sbuf("lfa_t1", [128, NCH, 256], F32)
    tb = b.sbuf("lfa_tb", [128, NCH, 256], BF16)
    qpb = b.sbuf("lfa_qpb", [128, NCH, 256], BF16)
    sq2 = tb
    vev = qpb
    uf = tb
    P16 = b.sbuf("lfa_P16", [128, 16, 384], BF16)
    PT16 = b.sbuf("lfa_PT16", [128, 48, 128], BF16)
    ao = b.sbuf("lfa_ao", [128, NCH, 256], BF16)
    wsl = b.sbuf("lfa_wsl", [128, NCH, 256], F32)
    ub = wsl.ap()[:, 0:4, :].bitcast(BF16).rearrange(
        "p a t -> p (a t)").rearrange("p (a d) -> p a d", a=2)
    zres = b.sbuf("lfa_zres", [128, 2, D], F32)

    z_flat = z_in
    zo_flat = z2
    ush2 = b.sbuf("lfa_ush2", [128, 2, D], F32)

    def issue_win_loads(wn, ubuf):
        _shift_load(b, "sync", ubuf[:, 0, :], z_flat, wn * 2)
        _shift_load(b, "sync", ubuf[:, 1, :], z_flat, wn * 2 + 1)

    issue_win_loads(0, ush)
    b.bar_phase()
    for w in range(16):
        w0 = w * 256
        ubuf = ush if w % 2 == 0 else ush2
        nbuf = ush2 if w % 2 == 0 else ush
        b.dma("gpsimd", tabs[:],
              aps["rotab"][:, :, w0:w0 + 256].rearrange("k p t -> p k t"))
        b.on("scalar", (lambda ub_=ubuf: lambda e: e.activation(
            ub.rearrange("p a d -> p (a d)"),
            ub_.ap().rearrange("p a d -> p (a d)"), AF.Copy))())
        b.bar_phase()
        for j in range(2):
            pbf = ps[:, j * 512:(j + 1) * 512].bitcast(BF16)
            b.on("tensor", (lambda jj, pb: lambda e: [
                e.transpose(pb[:, c * 128:(c + 1) * 128],
                            ub[:, jj, c * 128:(c + 1) * 128], identb[:])
                for c in range(NCH)][-1])(j, pbf))
        b.bar_phase()
        for j in range(2):
            pbf = ps[:, j * 512:(j + 1) * 512].bitcast(BF16)
            b.on("scalar", (lambda jj, pb: lambda e: e.activation(
                uf[:, :, jj * 128:(jj + 1) * 128],
                pb.rearrange("p (c t) -> p c t", c=NCH), AF.Copy))(j, pbf))
        b.bar_phase()
        def qkmm(r_):
            def f(e):
                rv = None
                base = (r_ % 2) * 2048
                for m8 in range(8):
                    mc = r_ * 8 + m8
                    for dic in range(NCH):
                        rv = e.matmul(ps[:, base + m8 * 256:base + m8 * 256 + 256],
                                      wq[:, dic, mc * 128:(mc + 1) * 128],
                                      uf[:, dic, :],
                                      start=(dic == 0),
                                      stop=(dic == NCH - 1))
                return rv
            return f

        def qkevac(r_):
            base = (r_ % 2) * 2048
            if r_ < 2:
                dst = qf if r_ == 0 else kf
                for m8 in range(8):
                    b.on("scalar", (lambda m, d=dst, ba=base: lambda e:
                         e.activation(d[:, m, :],
                                      ps[:, ba + m * 256:ba + m * 256 + 256],
                                      AF.Copy))(m8))
            else:
                b.on("vector", lambda e: e.tensor_copy(
                    vc.ap().rearrange("p s v -> p (s v)")[:, 0:2048],
                    vc.ap().rearrange("p s v -> p (s v)")[:, 2048:4096]))
                b.on("vector", lambda e: e.tensor_copy(
                    kc[:, :, 0:256], kc[:, :, 256:512]))
                for m8 in range(8):
                    b.on("scalar", (lambda m, ba=base: lambda e: e.activation(
                        vev[:, m, :], ps[:, ba + m * 256:ba + m * 256 + 256],
                        AF.Copy))(m8))

        b.on("tensor", qkmm(0))
        if w < 15:
            issue_win_loads(w + 1, nbuf)
        b.bar_phase()
        b.on("tensor", qkmm(1))
        qkevac(0)
        b.bar_phase()
        b.on("tensor", qkmm(2))
        qkevac(1)
        b.bar_phase()
        qkevac(2)
        b.bar_phase()

        def vtr(e):
            rv = None
            for m8 in range(8):
                for jl in range(2):
                    rv = e.transpose(
                        ps[:, 2048 + jl * 512:2048 + (jl + 1) * 512]
                        .bitcast(BF16)[:, m8 * 128:(m8 + 1) * 128],
                        vev[:, m8, jl * 128:(jl + 1) * 128], identb[:])
            return rv
        b.on("tensor", vtr)
        b.on("scalar", lambda e: e.activation(
            sq2.ap().rearrange("p c t -> p (c t)"),
            qf.ap().rearrange("p c t -> p (c t)"), AF.Square))
        b.bar_phase()
        for jl in range(2):
            b.on("scalar", (lambda j: lambda e: e.activation(
                vc[:, 2 + j, :],
                ps[:, 2048 + j * 512:2048 + (j + 1) * 512].bitcast(BF16),
                AF.Copy))(jl))

        def ssq_q(e):
            rv = None
            for pc in range(8):
                rv = e.matmul(ps[0:2, pc * 256:(pc + 1) * 256],
                              ind2[:], sq2[:, pc, :], start=True, stop=True)
            return rv
        b.on("tensor", ssq_q)
        b.bar_phase()
        b.on("scalar", lambda e: e.activation(rqt[0:2, 0:2048],
                                              ps[0:2, 0:2048], AF.Ln))
        b.on("scalar", lambda e: e.activation(
            sq2.ap().rearrange("p c t -> p (c t)"),
            kf.ap().rearrange("p c t -> p (c t)"), AF.Square))
        b.bar_phase()

        def ssq_k(e):
            rv = None
            for pc in range(8):
                rv = e.matmul(ps[0:2, 2048 + pc * 256:2048 + (pc + 1) * 256],
                              ind2[:], sq2[:, pc, :], start=True, stop=True)
            return rv
        b.on("tensor", ssq_k)
        b.on("scalar", lambda e: e.activation(rqt[0:2, 0:2048],
                                              rqt[0:2, 0:2048], AF.Exp,
                                              scale=-0.5))
        b.bar_phase()
        b.on("scalar", lambda e: e.activation(rq[0:2, 0:2048],
                                              ps[0:2, 2048:4096], AF.Ln))
        b.on("vector", lambda e: e.tensor_copy(rqb[0:2, 0:2048], rqt[0:2, 0:2048]))
        b.bar_phase()
        b.on("scalar", lambda e: e.activation(rq[0:2, 0:2048],
                                              rq[0:2, 0:2048], AF.Exp,
                                              scale=-0.5))
        b.on("vector", lambda e: e.tensor_tensor(rqt[0:2, 0:2048],
                                                 rqt[0:2, 0:2048],
                                                 rqb[0:2, 0:2048],
                                                 op=ALU.subtract))
        b.bar_phase()
        b.on("vector", lambda e: e.tensor_copy(rql[0:2, 0:2048], rqt[0:2, 0:2048]))
        b.bar_phase()

        for side in range(2):
            tabA = tabs[:, 0 if side == 0 else 2, :]
            tabB = tabs[:, 1 if side == 0 else 3, :]
            spp = qs_pp if side == 0 else ks_pp
            srcf = qf if side == 0 else kf

            def fbcmm(e):
                rv = None
                for pc in range(8):
                    rv = e.matmul(ps[:, pc * 256:(pc + 1) * 256], bc2[:],
                                  rqb[0:2, pc * 256:(pc + 1) * 256],
                                  start=True, stop=False)
                    rv = e.matmul(ps[:, pc * 256:(pc + 1) * 256], bc2[:],
                                  rql[0:2, pc * 256:(pc + 1) * 256],
                                  start=False, stop=True)
                return rv
            b.on("tensor", fbcmm)
            b.bar_phase()
            if dbg is not None and w == 8 and side == 0:
                b.on("scalar", lambda e: e.activation(
                    dbg["_fbcs"][:], ps[:, 0:256], AF.Copy))
                b.bar_phase()
                b.dma("gpsimd", dbg["fbc"], dbg["_fbcs"][:])
                b.dma("gpsimd", dbg["qspp_d"], qs_pp[:])
                b.bar_phase()
            for pc in range(8):
                b.on("vector", (lambda p_, sf=srcf, sp=spp: lambda e:
                     e.scalar_tensor_tensor(
                         t1[:, p_, :], sf[:, p_, :], sp[:, 0:1],
                         ps[:, p_ * 256:(p_ + 1) * 256],
                         op0=ALU.mult, op1=ALU.mult))(pc))
            if side == 0:
                b.on("vector", lambda e: e.tensor_copy(rqb[0:2, 0:2048],
                                                       rq[0:2, 0:2048]))
            b.bar_phase()
            for pc in range(8):
                b.on("vector", (lambda p_, tB=tabB: lambda e: e.tensor_tensor(
                    tb[:, p_, :], t1[:, p_, :], tB, op=ALU.mult))(pc))
            if side == 0:
                b.on("vector", lambda e: e.tensor_tensor(rq[0:2, 0:2048],
                                                         rq[0:2, 0:2048],
                                                         rqb[0:2, 0:2048],
                                                         op=ALU.subtract))
            b.bar_phase()
            for pc in range(8):
                b.on("vector", (lambda p_, tA=tabA: lambda e: e.tensor_tensor(
                    t1[:, p_, :], t1[:, p_, :], tA, op=ALU.mult))(pc))
            if side == 0:
                b.on("vector", lambda e: e.tensor_copy(rql[0:2, 0:2048],
                                                       rq[0:2, 0:2048]))

            def permm(e):
                rv = None
                for pc in range(8):
                    rv = e.matmul(ps[:, pc * 256:(pc + 1) * 256], perm[:],
                                  tb[:, pc, :], start=True, stop=True)
                return rv
            b.on("tensor", permm)
            b.bar_phase()

            if dbg is not None and w == 8 and side == 0:
                b.dma("gpsimd", dbg["tb"], tb[:])
                b.dma("gpsimd", dbg["t1"], t1[:])
                b.dma("gpsimd", dbg["qf"], qf[:])
                b.dma("gpsimd", dbg["rq2"], rqb[:])
                b.bar_phase()

            for pc in range(8):
                if side == 0:
                    dstap = qpb[:, pc, :]
                else:
                    dstap = kc[:, pc, 256:512]
                b.on("vector", (lambda p_, da=dstap: lambda e: e.tensor_tensor(
                    da, t1[:, p_, :], ps[:, p_ * 256:(p_ + 1) * 256],
                    op=ALU.add))(pc))
            b.bar_phase()

        for qtl in range(2):
            msk = mski if w > 0 else (msk0a if qtl == 0 else msk0b)
            jlo = qtl * 128
            for rnd in range(2):
                def smm(r_, qq=qtl, jl=jlo):
                    def f(e):
                        rv = None
                        for p4 in range(4):
                            pc = r_ * 4 + p4
                            for hh in range(2):
                                rv = e.matmul(
                                    ps[:, (p4 * 2 + hh) * 512:
                                       (p4 * 2 + hh) * 512 + 384],
                                    qpb[hh * 64:(hh + 1) * 64, pc,
                                        qq * 128:(qq + 1) * 128],
                                    kc[hh * 64:(hh + 1) * 64, pc,
                                       jl:jl + 384],
                                    start=True, stop=True,
                                    tile_position=(hh * 64, 0))
                        return rv
                    return f
                b.on("tensor", smm(rnd))
                b.bar_phase()
                for p4 in range(4):
                    for hh in range(2):
                        hl = (rnd * 4 + p4) * 2 + hh
                        bk = (p4 * 2 + hh)
                        b.on("vector", (lambda h, k: lambda e: e.reduce_max(
                            nmx[:, h:h + 1],
                            ps[:, k * 512:k * 512 + 384], axis=AX.X,
                            negate=True))(hl, bk))
                for p4 in range(4):
                    for hh in range(2):
                        hl = (rnd * 4 + p4) * 2 + hh
                        bk = (p4 * 2 + hh)
                        b.on("vector", (lambda h, k, m=msk: lambda e:
                             e.scalar_tensor_tensor(
                                 ps[:, k * 512:k * 512 + 384],
                                 ps[:, k * 512:k * 512 + 384],
                                 nmx[:, h:h + 1], m[:],
                                 op0=ALU.add, op1=ALU.add))(hl, bk))
                b.bar_phase()
                for p4 in range(4):
                    for hh in range(2):
                        hl = (rnd * 4 + p4) * 2 + hh
                        bk = (p4 * 2 + hh)
                        b.on("scalar", (lambda h, k: lambda e: e.activation(
                            P16[:, h, :], ps[:, k * 512:k * 512 + 384],
                            AF.Exp, accum_out=den[:, h:h + 1]))(hl, bk))
                b.bar_phase()
            b.on("vector", lambda e: e.reciprocal(rec[:], den[:]))
            b.bar_phase()
            for hl in range(16):
                b.on("vector", (lambda h: lambda e: e.tensor_scalar(
                    P16[:, h, :], P16[:, h, :], rec[:, h:h + 1], None,
                    op0=ALU.mult))(hl))
            b.bar_phase()

            b.dmat("sync", PT16.ap(),
                   P16.ap().rearrange("p h t -> p (h t)"))
            b.bar_phase()

            def avmm(qq=qtl):
                def f(e):
                    rv = None
                    for pc in range(8):
                        for js in range(3):
                            slot = qq + js
                            st = (js == 0)
                            sp = (js == 2)
                            h0 = pc * 2
                            rv = e.matmul(
                                ps[0:64, pc * 512:pc * 512 + 128],
                                vc[:, slot, h0 * 64:h0 * 64 + 64],
                                PT16[:, h0 * 3 + js, :],
                                start=st, stop=sp, tile_position=(0, 0))
                            rv = e.matmul(
                                ps[64:128, pc * 512:pc * 512 + 128],
                                vc[:, slot, h0 * 64 + 64:h0 * 64 + 128],
                                PT16[:, (h0 + 1) * 3 + js, :],
                                start=st, stop=sp, tile_position=(0, 64))
                    return rv
                return f
            b.on("tensor", avmm())
            b.bar_phase()
            for pc in range(8):
                b.on("scalar", (lambda p_, qq=qtl: lambda e: e.activation(
                    ao[:, p_, qq * 128:(qq + 1) * 128],
                    ps[:, p_ * 512:p_ * 512 + 128], AF.Identity))(pc))
            b.bar_phase()

        if dbg is not None and w == 8:
            b.dma("gpsimd", dbg["tabs"], tabs[:])
            b.dma("gpsimd", dbg["qpb"], qpb[:])
            b.dma("gpsimd", dbg["kc"], kc[:])
            b.dma("gpsimd", dbg["vc"], vc[:])
            b.dma("gpsimd", dbg["P16"], P16[:])
            b.dma("gpsimd", dbg["ao"], ao[:])
            b.dma("gpsimd", dbg["uf"], uf[:])
            b.bar_phase()
        def womm(e):
            rv = None
            for oc in range(8):
                for dic in range(NCH):
                    rv = e.matmul(ps[:, oc * 512:oc * 512 + 256],
                                  wo[:, dic, oc * 128:(oc + 1) * 128],
                                  ao[:, dic, :],
                                  start=(dic == 0), stop=(dic == NCH - 1))
            return rv
        b.on("tensor", womm)
        b.bar_phase()
        for oc in range(8):
            b.on("scalar", (lambda o: lambda e: e.activation(
                wsl[:, o, :], ps[:, o * 512:o * 512 + 256], AF.Copy))(oc))
        b.bar_phase()
        b.dma("sync", zres[:, 0, :], z_flat[w0:w0 + 128, :])
        b.dma("sync", zres[:, 1, :], z_flat[w0 + 128:w0 + 256, :])

        def wotr(e):
            rv = None
            for oc in range(8):
                for jl in range(2):
                    rv = e.transpose(
                        ps[:, jl * 2048 + oc * 128:jl * 2048 + (oc + 1) * 128],
                        wsl[:, oc, jl * 128:(jl + 1) * 128], identf[:])
            return rv
        b.on("tensor", wotr)
        b.bar_phase()
        for jl in range(2):
            b.on("vector", (lambda j: lambda e: e.tensor_tensor(
                zres[:, j, :], zres[:, j, :],
                ps[:, j * 2048:j * 2048 + 1024], op=ALU.add))(jl))
        b.bar_phase()
        b.dma("gpsimd", zo_flat[w0:w0 + 128, :], zres[:, 0, :])
        b.dma("gpsimd", zo_flat[w0 + 128:w0 + 256, :], zres[:, 1, :])
        b.bar_phase()

    for nm in ("lfa_ush2", "lfa_zres", "lfa_wsl", "lfa_ao", "lfa_PT16", "lfa_P16",
               "lfa_qpb", "lfa_tb", "lfa_t1", "lfa_rqt", "lfa_rql", "lfa_rqb",
               "lfa_rq", "lfa_kf", "lfa_qf",
               "lfa_ush", "lfa_rec", "lfa_nmx",
               "lfa_den", "lfa_tabs", "lfa_vc", "lfa_kc", "lfa_wo", "lfa_wq",
               "lfa_msk0b", "lfa_msk0a", "lfa_mski", "lfa_bc2", "lfa_ind2",
               "lfa_perm", "lfa_kspp", "lfa_qspp"):
        b.free(nm)


def emit_l0ffn_ffnhalf(b, aps, li, ps, identb, identf, z2, z_out):
    s = mk_lnslab(b, "lff_ln")
    gcol = b.sbuf("lff_gcol", [128, 8], F32)
    bcol = b.sbuf("lff_bcol", [128, 8], BF16)
    b1e = b.sbuf("lff_b1e", [128, 44], F32)
    b1p = b.sbuf("lff_b1p", [128, 44], F32)
    b2c = b.sbuf("lff_b2c", [128, 8], F32)
    w2bf = b.sbuf("lff_w2bf", [128, 22, 1024], BF16)

    b.dma("sync", gcol[:], aps["g_pp"][li])
    b.dma("sync", b1p[:], aps["b1_pp"][li])
    b.dma("sync", b2c[:], aps["b2_pp"][li])
    b.on("vector", lambda e: e.memset(s["eps"][:], 1e-5))
    b.bar_phase()
    bstg = b.sbuf("lff_bstg", [128, 8], F32)
    b.dma("sync", bstg[:], aps["beta_pp"][li])
    b.bar_phase()
    b.on("vector", lambda e: e.tensor_copy(bcol[:], bstg[:]))
    b.bar_phase()

    stg = b.sbuf("lff_stg", [128, 5632], F32)
    tmpb = b.sbuf("lff_tmpb", [128, 5632], BF16)
    w1g = b.sbuf("lff_w1g", [128, 5632], BF16)
    for dic in range(NCH):
        b.dma("sync", stg[:], aps["W1p"][li].rearrange(
            "(c p) o -> p c o", p=128)[:, dic, :])
        b.bar_phase()
        b.on("vector", (lambda d: lambda e: e.tensor_scalar(
            w1g[:], stg[:], gcol[:, d:d + 1], None, op0=ALU.mult))(dic))
        b.on("vector", lambda e: e.tensor_copy(tmpb[:], stg[:]))
        b.bar_phase()

        def b1emm(d):
            def f(e):
                rv = None
                for mc in range(44):
                    rv = e.matmul(ps[:, mc:mc + 1],
                                  tmpb[:, mc * 128:(mc + 1) * 128],
                                  bcol[:, d:d + 1],
                                  start=(d == 0), stop=(d == NCH - 1))
                return rv
            return f
        b.on("tensor", b1emm(dic))
        b.bar_phase()
        b.dma("gpsimd", aps["w1bf_hbm"][:, dic, :], w1g[:])
        b.bar_phase()
    b.on("vector", lambda e: e.tensor_tensor(b1e[:], ps[:, 0:44], b1p[:],
                                             op=ALU.add))
    b.bar_phase()
    b.free("lff_w1g")
    b.free("lff_tmpb")
    b.free("lff_stg")

    stg2 = b.sbuf("lff_stg2", [128, 4, 1024], F32)
    w2_hbm = aps["W2p"][li].rearrange("(c p) o -> p c o", p=128)
    for g in range(6):
        c0 = g * 4
        nc_ = min(4, 22 - c0)
        b.dma("sync", stg2[:, 0:nc_, :], w2_hbm[:, c0:c0 + nc_, :])
        b.bar_phase()
        b.on("vector", (lambda cc0=c0, nn=nc_: lambda e: e.tensor_copy(
            w2bf[:, cc0:cc0 + nn, :].rearrange("p c o -> p (c o)"),
            stg2[:, 0:nn, :].rearrange("p c o -> p (c o)")))())
        b.bar_phase()
    b.free("lff_stg2")

    zsh = b.sbuf("lff_zsh", [128, 4, D], F32)
    xng = b.sbuf("lff_xng", [128, 4, D], BF16)
    xf = b.sbuf("lff_xf", [128, NCH, 512], BF16)
    w1r = b.sbuf("lff_w1r", [128, NCH, 512], BF16)
    w1r2 = b.sbuf("lff_w1r2", [128, NCH, 512], BF16)
    sa = b.sbuf("lff_sa", [128, 4, 512], F32)
    sg = b.sbuf("lff_sg", [128, 4, 512], F32)
    hb = b.sbuf("lff_hb", [128, 22, 512], BF16)
    wsl2 = b.sbuf("lff_wsl2", [128, NCH, 512], F32)
    zres2 = b.sbuf("lff_zres2", [128, 4, D], F32)

    z2_flat = z2
    zo_flat = z_out
    for nt in range(8):
        for j in range(4):
            _shift_load(b, "sync", zsh[:, j, :], z2_flat, nt * 4 + j)
        b.bar_phase()
        emit_ln_group(b, s, zsh, xng, nj=4)
        for j in range(4):
            pbf = ps[:, j * 512:(j + 1) * 512].bitcast(BF16)
            b.on("tensor", (lambda jj, pb: lambda e: [
                e.transpose(pb[:, c * 128:(c + 1) * 128],
                            xng[:, jj, c * 128:(c + 1) * 128], identb[:])
                for c in range(NCH)][-1])(j, pbf))
        b.bar_phase()
        for j in range(4):
            pbf = ps[:, j * 512:(j + 1) * 512].bitcast(BF16)
            b.on("scalar", (lambda jj, pb: lambda e: e.activation(
                xf[:, :, jj * 128:(jj + 1) * 128],
                pb.rearrange("p (c t) -> p c t", c=NCH), AF.Copy))(j, pbf))
        b.bar_phase()
        w1v = aps["w1bf_hbm"].rearrange("p c (m x) -> p c m x", x=128)

        def w1dma(r_):
            buf = w1r if r_ % 2 == 0 else w1r2
            b.dma("sync", buf[:, :, 0:256].rearrange(
                "p c (a x) -> p c a x", a=2), w1v[:, :, r_ * 2:r_ * 2 + 2, :])
            b.dma("sync", buf[:, :, 256:512].rearrange(
                "p c (a x) -> p c a x", a=2),
                w1v[:, :, 22 + r_ * 2:22 + r_ * 2 + 2, :])

        def w1mm(r_):
            buf = w1r if r_ % 2 == 0 else w1r2
            base = (r_ % 2) * 4

            def f(e):
                rv = None
                for k in range(2):
                    for half in range(2):
                        bk = base + k * 2 + half
                        for dic in range(NCH):
                            rv = e.matmul(
                                ps[:, bk * 512:(bk + 1) * 512],
                                buf[:, dic, half * 256 + k * 128:
                                    half * 256 + (k + 1) * 128],
                                xf[:, dic, :],
                                start=(dic == 0), stop=(dic == NCH - 1))
                return rv
            return f

        def w1evac(r_):
            base = (r_ % 2) * 4
            for k in range(2):
                mc = r_ * 2 + k
                sl = (r_ % 2) * 2 + k
                b.on("scalar", (lambda s_, m, bk: lambda e: e.activation(
                    sa[:, s_, :], ps[:, bk * 512:(bk + 1) * 512],
                    AF.Identity, bias=b1e[:, m:m + 1]))(sl, mc, base + k * 2))
                b.on("scalar", (lambda s_, m, bk: lambda e: e.activation(
                    sg[:, s_, :], ps[:, bk * 512:(bk + 1) * 512],
                    AF.Gelu, bias=b1e[:, m + 22:m + 23]))(sl, mc,
                                                          base + k * 2 + 1))

        def w1hmult(r_):
            for k in range(2):
                mc = r_ * 2 + k
                sl = (r_ % 2) * 2 + k
                b.on("vector", (lambda s_, m: lambda e: e.tensor_tensor(
                    hb[:, m, :], sa[:, s_, :], sg[:, s_, :],
                    op=ALU.mult))(sl, mc))

        w1dma(0)
        b.bar_phase()
        for r in range(11):
            b.on("tensor", w1mm(r))
            if r < 10:
                w1dma(r + 1)
            if r >= 1:
                w1evac(r - 1)
            if r >= 2:
                w1hmult(r - 2)
            b.bar_phase()
        w1evac(10)
        w1hmult(9)
        b.bar_phase()
        w1hmult(10)
        b.bar_phase()

        def w2mm(e):
            rv = None
            for oc in range(8):
                for hc in range(22):
                    rv = e.matmul(ps[:, oc * 512:(oc + 1) * 512],
                                  w2bf[:, hc, oc * 128:(oc + 1) * 128],
                                  hb[:, hc, :],
                                  start=(hc == 0), stop=(hc == 21))
            return rv
        b.on("tensor", w2mm)
        b.bar_phase()
        for oc in range(8):
            b.on("scalar", (lambda o: lambda e: e.activation(
                wsl2[:, o, :], ps[:, o * 512:(o + 1) * 512], AF.Identity,
                bias=b2c[:, o:o + 1]))(oc))
        b.bar_phase()
        for j in range(4):
            t0 = (nt * 4 + j) * 128
            b.dma("sync", zres2[:, j, :], z2_flat[t0:t0 + 128, :])

        def w2tr(e):
            rv = None
            for j in range(4):
                for oc in range(8):
                    rv = e.transpose(
                        ps[:, j * 1024 + oc * 128:j * 1024 + (oc + 1) * 128],
                        wsl2[:, oc, j * 128:(j + 1) * 128], identf[:])
            return rv
        b.on("tensor", w2tr)
        b.bar_phase()
        for j in range(4):
            b.on("vector", (lambda jj: lambda e: e.tensor_tensor(
                zres2[:, jj, :], zres2[:, jj, :],
                ps[:, jj * 1024:(jj + 1) * 1024], op=ALU.add))(j))
        b.bar_phase()
        for j in range(4):
            t0 = (nt * 4 + j) * 128
            b.dma("gpsimd", zo_flat[t0:t0 + 128, :], zres2[:, j, :])
        b.bar_phase()

    for nm in ("lff_zres2", "lff_wsl2", "lff_hb", "lff_sg", "lff_sa",
               "lff_w1r2", "lff_w1r", "lff_xf", "lff_xng", "lff_zsh", "lff_bstg",
               "lff_w2bf", "lff_b2c", "lff_b1p", "lff_b1e", "lff_bcol",
               "lff_gcol", "lff_ln_eps", "lff_ln_sq", "lff_ln_bias",
               "lff_ln_rstd", "lff_ln_sd", "lff_ln_var", "lff_ln_m2",
               "lff_ln_mu", "lff_ln_ss", "lff_ln_sums"):
        b.free(nm)


def emit_ln_to_z(b, aps, xin_hbm, z_out, ps, identb):
    s = mk_lnslab(b, "lz_ln")
    xg = b.sbuf("lz_xg", [128, 8, D], F32)
    zng = b.sbuf("lz_zng", [128, 8, D], F32)
    b.on("vector", lambda e: e.memset(s["eps"][:], 1e-5))
    b.bar_phase()
    z_flat = z_out
    for g in range(4):
        b.dma("sync", xg[:], xin_hbm[:, g * 8:(g + 1) * 8, :])
        b.bar_phase()
        emit_ln_group(b, s, xg, zng)
        for j in range(8):
            n = g * 8 + j
            b.dma("gpsimd", z_flat[n * 128:(n + 1) * 128, :], zng[:, j, :])
        b.bar_phase()
    for nm in ("lz_zng", "lz_xg", "lz_ln_eps", "lz_ln_sq", "lz_ln_bias",
               "lz_ln_rstd", "lz_ln_sd", "lz_ln_var", "lz_ln_m2", "lz_ln_mu",
               "lz_ln_ss", "lz_ln_sums"):
        b.free(nm)


def emit_final_add(b, aps, xin_hbm, xout_hbm, z_in):
    xg = b.sbuf("lfz_xg", [128, 8, D], F32)
    zg = b.sbuf("lfz_zg", [128, 8, D], F32)
    z_flat = z_in
    for g in range(4):
        b.dma("sync", xg[:], xin_hbm[:, g * 8:(g + 1) * 8, :])
        b.dma("sync", zg[:],
              z_flat[g * 1024:(g + 1) * 1024, :].rearrange(
                  "(a p) d -> p a d", p=128))
        b.bar_phase()
        b.on("vector", lambda e: e.tensor_tensor(
            xg.ap().rearrange("p a d -> p (a d)"),
            xg.ap().rearrange("p a d -> p (a d)"),
            zg.ap().rearrange("p a d -> p (a d)"), op=ALU.add))
        b.bar_phase()
        b.dma("gpsimd", xout_hbm[:, g * 8:(g + 1) * 8, :], xg[:])
        b.bar_phase()
    b.free("lfz_zg")
    b.free("lfz_xg")


def _l0ffn_aps(nc):
    return {
        "Wqkv": nc.dram_tensor("f_Wqkv", [6, D, 3072], F32, kind="ExternalInput").ap(),
        "Wo": nc.dram_tensor("f_Wo", [6, D, D], F32, kind="ExternalInput").ap(),
        "W1p": nc.dram_tensor("f_W1p", [6, D, 5632], F32, kind="ExternalInput").ap(),
        "W2p": nc.dram_tensor("f_W2p", [6, 2816, D], F32, kind="ExternalInput").ap(),
        "b1_pp": nc.dram_tensor("f_b1pp", [6, 128, 44], F32, kind="ExternalInput").ap(),
        "b2_pp": nc.dram_tensor("f_b2pp", [6, 128, 8], F32, kind="ExternalInput").ap(),
        "g_pp": nc.dram_tensor("f_gpp", [6, 128, 8], F32, kind="ExternalInput").ap(),
        "beta_pp": nc.dram_tensor("f_betapp", [6, 128, 8], F32, kind="ExternalInput").ap(),
        "qs_pp": nc.dram_tensor("f_qspp", [6, 128, 1], F32, kind="ExternalInput").ap(),
        "ks_pp": nc.dram_tensor("f_kspp", [6, 128, 1], F32, kind="ExternalInput").ap(),
        "rotab": nc.dram_tensor("f_rotab", [4, 128, T], F32, kind="ExternalInput").ap(),
        "perm": nc.dram_tensor("f_perm", [128, 128], BF16, kind="ExternalInput").ap(),
        "ind2": nc.dram_tensor("f_ind2", [128, 2], BF16, kind="ExternalInput").ap(),
        "bc2": nc.dram_tensor("f_bc2", [2, 128], BF16, kind="ExternalInput").ap(),
        "mask_i": nc.dram_tensor("f_maski", [128, 384], F32, kind="ExternalInput").ap(),
        "mask_0a": nc.dram_tensor("f_mask0a", [128, 384], F32, kind="ExternalInput").ap(),
        "mask_0b": nc.dram_tensor("f_mask0b", [128, 384], F32, kind="ExternalInput").ap(),
        "w1bf_hbm": nc.dram_tensor("f_w1bf", [128, NCH, 5632], BF16, kind="Internal").ap(),
    }


def l0ffn_host_inputs(inputs):
    Wqkv = np.asarray(inputs["L0_ffn_Wqkv"], np.float32)
    Wo = np.asarray(inputs["L0_ffn_Wo"], np.float32)
    W1 = np.asarray(inputs["L0_ffn_W1"], np.float32)
    b1 = np.asarray(inputs["L0_ffn_b1"], np.float32)
    W2 = np.asarray(inputs["L0_ffn_W2"], np.float32)
    b2 = np.asarray(inputs["L0_ffn_b2"], np.float32)
    g = np.asarray(inputs["L0_ffn_ln_g"], np.float32)
    beta = np.asarray(inputs["L0_ffn_ln_b"], np.float32)
    qsc = np.asarray(inputs["L0_ffn_q_scale"], np.float32)
    ksc = np.asarray(inputs["L0_ffn_k_scale"], np.float32)

    W1p = np.zeros((6, D, 5632), np.float32)
    W1p[:, :, 0:2730] = W1[:, :, 0:2730]
    W1p[:, :, 2816:2816 + 2730] = W1[:, :, 2730:5460]
    b1p = np.zeros((6, 5632), np.float32)
    b1p[:, 0:2730] = b1[:, 0:2730]
    b1p[:, 2816:2816 + 2730] = b1[:, 2730:5460]
    W2p = np.zeros((6, 2816, D), np.float32)
    W2p[:, 0:2730, :] = W2

    DIM_HEAD, WINDOW = 64, 256
    half = np.arange(0, DIM_HEAD, 2, dtype=np.float64)
    inv_freq = 1.0 / 10000.0 ** (half / DIM_HEAD)
    t = np.arange(T, dtype=np.float64)
    freqs = t[:, None] * inv_freq[None, :]
    freqs = np.concatenate([freqs, freqs], -1)
    base_scale = (half + 0.4 * DIM_HEAD) / (1.4 * DIM_HEAD)
    power = (t - T // 2) / (WINDOW // 2)
    scale = base_scale[None, :] ** power[:, None]
    scale = np.concatenate([scale, scale], -1)
    cos, sin = np.cos(freqs), np.sin(freqs)
    Aq = (cos * scale).T
    Bq = (sin * scale).T
    Ak = (cos / scale).T
    Bk = (sin / scale).T
    sig = np.where(np.arange(64) < 32, -1.0, 1.0)
    sgm = (np.arange(64) + 32) % 64
    Bqs = sig[sgm][:, None] * Bq[sgm, :]
    Bks = sig[sgm][:, None] * Bk[sgm, :]
    def dup(a):
        return np.vstack([a, a]).astype(np.float32)
    rotab = np.stack([dup(Aq) * 8.0, dup(Bqs) * 8.0, dup(Ak), dup(Bks)])

    permM = np.zeros((128, 128), np.float32)
    for m in range(128):
        sm = (m // 64) * 64 + ((m % 64) + 32) % 64
        permM[sm, m] = 1.0
    ind2 = np.zeros((128, 2), np.float32)
    ind2[0:64, 0] = 1.0
    ind2[64:128, 1] = 1.0
    bc2 = np.zeros((2, 128), np.float32)
    bc2[0, 0:64] = 1.0
    bc2[1, 64:128] = 1.0

    NEG = np.float32(-3.38e38)
    p = np.arange(128)[:, None]
    jl = np.arange(384)[None, :]
    mi = np.where((jl >= p) & (jl <= p + 256), 0.0, NEG).astype(np.float32)
    m0a = np.where((jl >= 256) & (jl <= 256 + p), 0.0, NEG).astype(np.float32)
    m0b = np.where((jl >= 128) & (jl <= 256 + p), 0.0, NEG).astype(np.float32)

    def pp(a, w):
        return np.ascontiguousarray(a.reshape(6, w, 128).transpose(0, 2, 1))

    return {
        "f_Wqkv": Wqkv, "f_Wo": Wo, "f_W1p": W1p, "f_W2p": W2p,
        "f_b1pp": pp(b1p, 44), "f_b2pp": pp(b2, 8), "f_gpp": pp(g, 8),
        "f_betapp": pp(beta, 8),
        "f_qspp": np.ascontiguousarray(
            np.concatenate([qsc, qsc], 1)[:, :, None]),
        "f_kspp": np.ascontiguousarray(
            np.concatenate([ksc, ksc], 1)[:, :, None]),
        "f_rotab": rotab, "f_perm": bfc(permM), "f_ind2": bfc(ind2),
        "f_bc2": bfc(bc2), "f_maski": mi, "f_mask0a": m0a, "f_mask0b": m0b,
    }


def build_l0ffn_layer_prog(li, first=False, last=False, debug_z2=False, debug_w=False):
    nc = bass.Bass("TRN2", target_bir_lowering=False)
    aps = _l0ffn_aps(nc)
    if first:
        xd = nc.dram_tensor("x", [T, D], F32, kind="ExternalInput").ap()
        zin = nc.dram_tensor("zmid", [T, D], F32, kind="Internal").ap()
    else:
        xd = nc.dram_tensor("x", [T, D], F32, kind="ExternalInput").ap()             if last else None
        zin = nc.dram_tensor("zin", [T, D], F32, kind="ExternalInput").ap()
    z2 = nc.dram_tensor("z2", [T, D], F32,
                        kind="ExternalOutput" if debug_z2 else "Internal").ap()
    if last:
        zout = nc.dram_tensor("zfin", [T, D], F32, kind="Internal").ap()
        xod = nc.dram_tensor("xo", [T, D], F32, kind="ExternalOutput").ap()
    else:
        zout = nc.dram_tensor("zout", [T, D], F32, kind="ExternalOutput").ap()
        xod = None
    identbd = nc.dram_tensor("identb", [128, 128], BF16, kind="ExternalInput").ap()
    identfd = nc.dram_tensor("identf", [128, 128], F32, kind="ExternalInput").ap()

    b = Bld(nc)
    identb = b.sbuf("identbS", [128, 128], BF16)
    identf = b.sbuf("identfS", [128, 128], F32)
    ps = b.psum("PS", [128, 4096], F32)
    b.dma("sync", identb[:], identbd)
    b.dma("sync", identf[:], identfd)
    b.bar_phase()
    if first:
        xin_hbm = xd.rearrange("(n p) d -> p n d", p=128)
        emit_ln_to_z(b, aps, xin_hbm, zin, ps, identb)
    dbg = None
    if debug_w:
        dbg = {
            "qpb": nc.dram_tensor("dbg_qpb", [128, NCH, 256], BF16, kind="ExternalOutput").ap(),
            "kc": nc.dram_tensor("dbg_kc", [128, NCH, 512], BF16, kind="ExternalOutput").ap(),
            "vc": nc.dram_tensor("dbg_vc", [128, 4, 1024], BF16, kind="ExternalOutput").ap(),
            "P16": nc.dram_tensor("dbg_P16", [128, 16, 384], BF16, kind="ExternalOutput").ap(),
            "ao": nc.dram_tensor("dbg_ao", [128, NCH, 256], BF16, kind="ExternalOutput").ap(),
            "uf": nc.dram_tensor("dbg_uf", [128, NCH, 256], BF16, kind="ExternalOutput").ap(),
            "tabs": nc.dram_tensor("dbg_tabs", [128, 4, 256], F32, kind="ExternalOutput").ap(),
            "tb": nc.dram_tensor("dbg_tb", [128, NCH, 256], BF16, kind="ExternalOutput").ap(),
            "t1": nc.dram_tensor("dbg_t1", [128, NCH, 256], F32, kind="ExternalOutput").ap(),
            "qf": nc.dram_tensor("dbg_qf", [128, NCH, 256], F32, kind="ExternalOutput").ap(),
            "rq2": nc.dram_tensor("dbg_rq2", [2, 2048], BF16, kind="ExternalOutput").ap(),
            "fbc": nc.dram_tensor("dbg_fbc", [128, 256], F32, kind="ExternalOutput").ap(),
            "qspp_d": nc.dram_tensor("dbg_qspp", [128, 1], F32, kind="ExternalOutput").ap(),
        }
        dbg["_fbcs"] = b.sbuf("dbg_fbcs", [128, 256], F32)
    emit_l0ffn_attnhalf(b, aps, li, ps, identb, identf, zin, z2, dbg)
    emit_l0ffn_ffnhalf(b, aps, li, ps, identb, identf, z2, zout)
    if last:
        xin_hbm = xd.rearrange("(n p) d -> p n d", p=128)
        xout_hbm = xod.rearrange("(n p) d -> p n d", p=128)
        emit_final_add(b, aps, xin_hbm, xout_hbm, zout)
    b.finalize()
    return nc


def l1attn_host_inputs(inputs):
    Wp = np.asarray(inputs["L1_attn_W_proj"], np.float32)
    Wd = np.asarray(inputs["L1_attn_W_delta"], np.float32)
    bd = np.asarray(inputs["L1_attn_b_delta"], np.float32)
    A1 = np.asarray(inputs["L1_attn_A"], np.float32)[0, 0]
    B1 = np.asarray(inputs["L1_attn_Bp"], np.float32)[0, :, 0]
    Dv = np.asarray(inputs["L1_attn_D"], np.float32)
    return {
        "WpT": np.ascontiguousarray(Wp.T),
        "WdT": np.ascontiguousarray(Wd.T),
        "bdel_pp": np.ascontiguousarray(bd.reshape(8, 128).T),
        "A1_pp": np.ascontiguousarray(A1.reshape(8, 128).T),
        "B1_pp": np.ascontiguousarray(B1.reshape(8, 128).T),
        "Dv": np.broadcast_to(Dv.reshape(1, 1), (128, 1)).copy(),
    }


def common_host_inputs():
    return {
        "identb": bfc(np.eye(128, dtype=np.float32)),
        "identf": np.eye(128, dtype=np.float32),
    }



def build_l1attn_prog(debug=False):
    nc = bass.Bass("TRN2", target_bir_lowering=False)
    xd = nc.dram_tensor("x", [T, D], F32, kind="ExternalInput").ap()
    aps = {
        "WpT": nc.dram_tensor("WpT", [D, 4096], F32, kind="ExternalInput").ap(),
        "WdT": nc.dram_tensor("WdT", [D, D], F32, kind="ExternalInput").ap(),
        "bdel_pp": nc.dram_tensor("bdel_pp", [128, 8], F32, kind="ExternalInput").ap(),
        "A1_pp": nc.dram_tensor("A1_pp", [128, 8], F32, kind="ExternalInput").ap(),
        "B1_pp": nc.dram_tensor("B1_pp", [128, 8], F32, kind="ExternalInput").ap(),
        "Dv": nc.dram_tensor("Dv", [128, 1], F32, kind="ExternalInput").ap(),
    }
    aps["din_hbm"] = nc.dram_tensor("din_hbm", [128, NCH, T], BF16,
                                    kind="ExternalOutput" if debug else "Internal").ap()
    aps["_debug"] = debug
    identbd = nc.dram_tensor("identb", [128, 128], BF16, kind="ExternalInput").ap()
    identfd = nc.dram_tensor("identf", [128, 128], F32, kind="ExternalInput").ap()
    xod = nc.dram_tensor("xo", [T, D], F32, kind="ExternalOutput").ap()

    b = Bld(nc)
    identb = b.sbuf("identbS", [128, 128], BF16)
    identf = b.sbuf("identfS", [128, 128], F32)
    ps = b.psum("PS", [128, 4096], F32)
    b.dma("sync", identb[:], identbd)
    b.dma("sync", identf[:], identfd)
    b.bar_phase()
    if debug:
        aps["xn_dbg"] = nc.dram_tensor("xn_dbg", [128, NCH, T], BF16,
                                       kind="ExternalOutput").ap()
        aps["ucsum_dbg"] = nc.dram_tensor("ucsum_dbg", [128, 16], F32,
                                          kind="ExternalOutput").ap()
    xin_hbm = xd.rearrange("(n p) d -> p n d", p=128)
    xout_hbm = xod.rearrange("(n p) d -> p n d", p=128)
    emit_l1attn(b, aps, xin_hbm, xout_hbm, ps, identb, identf)
    b.finalize()
    return nc


def build_fused_prog():
    nc = bass.Bass("TRN2", target_bir_lowering=False)
    xd = nc.dram_tensor("x", [T, D], F32, kind="ExternalInput").ap()
    xod = nc.dram_tensor("xo", [T, D], F32, kind="ExternalOutput").ap()
    identbd = nc.dram_tensor("identb", [128, 128], BF16, kind="ExternalInput").ap()
    identfd = nc.dram_tensor("identf", [128, 128], F32, kind="ExternalInput").ap()

    x1 = nc.dram_tensor("x1_h", [T, D], F32, kind="Internal").ap()
    x2 = nc.dram_tensor("x2_h", [T, D], F32, kind="Internal").ap()
    x3 = nc.dram_tensor("x3_h", [T, D], F32, kind="Internal").ap()
    zA = nc.dram_tensor("zA_h", [T, D], F32, kind="Internal").ap()
    z2 = nc.dram_tensor("z2_h", [T, D], F32, kind="Internal").ap()

    rp_h = nc.dram_tensor("rp_hbm", [16, 32, 128, 512], F32, kind="Internal")
    aps0 = {
        "WqkvT": nc.dram_tensor("WqkvT", [D, 3072], F32, kind="ExternalInput").ap(),
        "bqkv_pp": nc.dram_tensor("bqkv_pp", [128, 24], F32, kind="ExternalInput").ap(),
        "wrx2": nc.dram_tensor("wrx2", [128, 512], F32, kind="ExternalInput").ap(),
        "brx_pp": nc.dram_tensor("brx_pp", [128, 512], F32, kind="ExternalInput").ap(),
        "rp_hbm": rp_h.ap(),
        "rp_hbm_h": rp_h,
        "qkv_hbm": nc.dram_tensor("qkv_hbm", [2, 3, 128, 4, T], BF16,
                                  kind="Internal").ap(),
    }
    apsf = _l0ffn_aps(nc)
    aps1a = {
        "WpT": nc.dram_tensor("WpT", [D, 4096], F32, kind="ExternalInput").ap(),
        "WdT": nc.dram_tensor("WdT", [D, D], F32, kind="ExternalInput").ap(),
        "bdel_pp": nc.dram_tensor("bdel_pp", [128, 8], F32, kind="ExternalInput").ap(),
        "A1_pp": nc.dram_tensor("A1_pp", [128, 8], F32, kind="ExternalInput").ap(),
        "B1_pp": nc.dram_tensor("B1_pp", [128, 8], F32, kind="ExternalInput").ap(),
        "Dv": nc.dram_tensor("Dv", [128, 1], F32, kind="ExternalInput").ap(),
        "din_hbm": nc.dram_tensor("din_hbm", [128, NCH, T], BF16,
                                  kind="Internal").ap(),
    }
    aps1f = {
        "WgT": nc.dram_tensor("WgT", [D, 8], F32, kind="ExternalInput").ap(),
        "WeT": nc.dram_tensor("WeT", [8, D, D], F32, kind="ExternalInput").ap(),
        "be": nc.dram_tensor("be", [8, D], F32, kind="ExternalInput").ap(),
        "webf_hbm": nc.dram_tensor("webf_hbm", [128, 8, NCH, D], BF16,
                                   kind="Internal").ap(),
    }

    b = Bld(nc)
    identb = b.sbuf("identbS", [128, 128], BF16)
    identf = b.sbuf("identfS", [128, 128], F32)
    ps = b.psum("PS", [128, 4096], F32)
    b.dma("sync", identb[:], identbd)
    b.dma("sync", identf[:], identfd)
    b.bar_phase()

    xin = xd.rearrange("(n p) d -> p n d", p=128)
    x1v = x1.rearrange("(n p) d -> p n d", p=128)
    x2v = x2.rearrange("(n p) d -> p n d", p=128)
    x3v = x3.rearrange("(n p) d -> p n d", p=128)
    xov = xod.rearrange("(n p) d -> p n d", p=128)

    emit_l0attn(b, aps0, xin, x1v, ps, identb, identf)
    emit_ln_to_z(b, apsf, x1v, zA, ps, identb)
    for li in range(6):
        emit_l0ffn_attnhalf(b, apsf, li, ps, identb, identf, zA, z2)
        emit_l0ffn_ffnhalf(b, apsf, li, ps, identb, identf, z2, zA)
    emit_final_add(b, apsf, x1v, x2v, zA)
    emit_l1attn(b, aps1a, x2v, x3v, ps, identb, identf)
    emit_l1ffn(b, aps1f, x3v, xov, ps, identb, identf)
    b.finalize()
    return nc


_PROGS = {}


def _prog(key, fn):
    if key not in _PROGS:
        _PROGS[key] = fn()
    return _PROGS[key]


FUSED = True


def kernel(**inputs):
    if FUSED:
        return kernel_fused(**inputs)
    return kernel_unfused(**inputs)


def kernel_fused(**inputs):
    x = np.ascontiguousarray(np.asarray(inputs["L0_attn_x"], np.float32))
    ncore = 8
    shared = {}
    shared.update(common_host_inputs())
    shared.update(l0attn_host_inputs(inputs))
    shared.update(l0ffn_host_inputs(inputs))
    shared.update(l1attn_host_inputs(inputs))
    shared.update(l1ffn_host_inputs(inputs))
    nc = _prog("fused", build_fused_prog)
    in_maps = []
    for c in range(ncore):
        m = dict(shared)
        m["x"] = x[c]
        in_maps.append(m)
    res = run_bass_kernel_spmd(nc, in_maps, core_ids=list(range(ncore)))
    return np.stack([np.asarray(res.results[c]["xo"]) for c in range(ncore)])


def kernel_unfused(**inputs):
    x = np.ascontiguousarray(np.asarray(inputs["L0_attn_x"], np.float32))
    ncore = 8
    com = common_host_inputs()
    h0a = l0attn_host_inputs(inputs)
    h0f = l0ffn_host_inputs(inputs)
    h1a = l1attn_host_inputs(inputs)
    h1f = l1ffn_host_inputs(inputs)

    def launch(nc, per_core_extra, shared):
        in_maps = []
        for c in range(ncore):
            m = dict(shared)
            m.update(com)
            m.update(per_core_extra[c])
            in_maps.append(m)
        res = run_bass_kernel_spmd(nc, in_maps, core_ids=list(range(ncore)))
        return res.results

    nc = _prog("l0a", build_l0attn_prog)
    r = launch(nc, [{"x": x[c]} for c in range(ncore)], h0a)
    x1 = [np.asarray(r[c]["xo"]) for c in range(ncore)]

    z = None
    for li in range(6):
        first, last = li == 0, li == 5
        nc = _prog(f"l0f{li}", lambda li=li, f=first, l=last:
                   build_l0ffn_layer_prog(li, first=f, last=l))
        if first:
            per = [{"x": x1[c]} for c in range(ncore)]
        elif last:
            per = [{"x": x1[c], "zin": z[c]} for c in range(ncore)]
        else:
            per = [{"zin": z[c]} for c in range(ncore)]
        r = launch(nc, per, h0f)
        if last:
            x2 = [np.asarray(r[c]["xo"]) for c in range(ncore)]
        else:
            z = [np.asarray(r[c]["zout"]) for c in range(ncore)]

    nc = _prog("l1a", build_l1attn_prog)
    r = launch(nc, [{"x": x2[c]} for c in range(ncore)], h1a)
    x3 = [np.asarray(r[c]["xo"]) for c in range(ncore)]

    nc = _prog("l1f", build_l1ffn_prog)
    r = launch(nc, [{"x": x3[c]} for c in range(ncore)], h1f)
    out = np.stack([np.asarray(r[c]["xo"]) for c in range(ncore)])
    return out
